# version 56
# baseline (speedup 1.0000x reference)
import contextlib
import numpy as np
import concourse.bass as bass
import concourse.mybir as mybir
from concourse.bass_utils import run_bass_kernel_spmd

F32 = mybir.dt.float32
BF16 = mybir.dt.bfloat16
I32 = mybir.dt.int32
AF = mybir.ActivationFunctionType
ALU = mybir.AluOpType
AX = mybir.AxisListType

NCORES = 8
SEQ = 2
T = 2048
TT = 256
TPS = T // TT
NTILES = SEQ * TPS
D = 1024
DI = 2048
NST = 16
DTR = 64
I0 = 2816
ZC = 2080
CH = 128
ND = D // CH
NDI = DI // CH
NIC = I0 // CH
NZC = 17
PI = float(np.pi)

ENG = {"T": "tensor", "V": "vector", "S": "scalar", "G": "gpsimd", "D": "sync"}
CONST_VALS = [0.0, 1.0, -1.0, 1e-5, 2.0, PI / 2, 1.0 / 64, 0.5, 127.0, PI, -PI,
              256.5, 1e-4, 1e-12]


def cdiv(a, b):
    return (a + b - 1) // b


class Prog:
    def __init__(self, nc, name, step=1):
        self.sem = nc.semaphore(name).__enter__()
        self.n = 0
        self.step = step

    def tick(self, instr, k=None):
        k = self.step if k is None else k
        instr.then_inc(self.sem, k)
        self.n += k
        return instr


class Rec:

    def __init__(self):
        self.calls = []

    def __getattr__(self, name):
        calls = self.calls

        def call(*a, **k):
            entry = [name, a, k, []]
            calls.append(entry)

            class H:
                def then_inc(self, sem, n):
                    entry[3].append((sem, n))
                    return self
            return H()
        return call

    def replay(self, e):
        for name, a, k, chain in self.calls:
            h = getattr(e, name)(*a, **k)
            for sem, n in chain:
                h = h.then_inc(sem, n)


class EM:

    MAXSTEPS = 60
    BATCH_LO = 1371
    BATCH_HI = 10**9

    def __init__(self, nc):
        self.nc = nc
        self.prog = {e: Prog(nc, "prog_" + e) for e in "TVSG"}
        self.progD = Prog(nc, "prog_D", 16)
        self.progW = Prog(nc, "prog_W", 16)
        self.progC = Prog(nc, "prog_C", 1)
        self.last_wait = {e: {} for e in "TVSGD"}
        self.pend = {e: [] for e in "TVSGD"}
        self.npend = 0
        self.afset = set()
        self.ndma = 0
        self.wmap = {}
        self.rmap = {}

    def flush(self):
        if self.npend == 0:
            return
        with self.nc.Block() as b:
            for eng in "DTVSG":
                recs = self.pend[eng]
                if not recs:
                    continue
                dec = getattr(b, ENG[eng])

                @dec
                def _(e, recs=recs):
                    for r in recs:
                        r.replay(e)
        self.pend = {e: [] for e in "TVSGD"}
        self.npend = 0
        self.afset = set()
        self.ndma = 0

    new_block = flush
    close = flush

    @staticmethod
    def _afs(rec):
        s = set()
        for name, a, k, chain in rec.calls:
            if name == "activation":
                s.add((str(a[2]), str(a[0].dtype), str(a[1].dtype)))
            elif name == "copy":
                s.add(("COPY", str(a[0].dtype), str(a[1].dtype)))
        return s

    def marks(self):
        m = {e: self.prog[e].n for e in "TVSG"}
        m["D"] = self.progD.n
        m["C"] = self.progC.n
        return m

    @staticmethod
    def _rw(rec):
        rd, wr = set(), set()

        def isap(x):
            return hasattr(x, "space") and hasattr(x, "dtype")

        for name, a, k, chain in rec.calls:
            if name in ("wait_ge", "sem_inc"):
                continue
            if name == "dma_start":
                out = k.get("out", a[0] if a else None)
                in_ = k.get("in_", a[1] if len(a) > 1 else None)
                if isap(out):
                    wr.add(out.name)
                if isap(in_):
                    rd.add(in_.name)
                continue
            if a and isap(a[0]):
                wr.add(a[0].name)
            for x in a[1:]:
                if isap(x):
                    rd.add(x.name)
            for x in k.values():
                if isap(x):
                    rd.add(x.name)
        return rd, wr

    def _autodeps(self, rd, wr, eng):
        deps = {}

        def need(e2, cnt):
            if cnt <= 0 or (e2 == eng and e2 != "D"):
                return
            if deps.get(e2, 0) < cnt:
                deps[e2] = cnt

        for nm in rd | wr:
            w = self.wmap.get(nm)
            if w:
                need(*w)
        for nm in wr:
            for e2, cnt in self.rmap.get(nm, {}).items():
                need(e2, cnt)
        return deps

    def _track(self, rd, wr, eng):
        cnt = self.progD.n if eng == "D" else self.prog[eng].n
        for nm in wr:
            self.wmap[nm] = (eng, cnt)
            self.rmap[nm] = {}
        for nm in rd:
            self.rmap.setdefault(nm, {})[eng] = cnt

    def step(self, eng, fn, deps=None):
        idx = getattr(self, "stepidx", 0) + 1
        inwin = self.BATCH_LO <= idx < self.BATCH_HI
        marks0 = self.marks()
        r = Rec()
        fn(r)
        rd, wr = self._rw(r)
        selfdep = False
        if deps is None:
            if inwin and eng != "G":
                deps = self._autodeps(rd, wr, eng)
                selfdep = True
            else:
                deps = marks0
        marks = dict(deps)
        w = Rec()
        dwait = False
        for d, cnt in marks.items():
            if cnt <= 0 or (d == eng and not (d == "D" and selfdep)):
                continue
            if self.last_wait[eng].get(d, -1) >= cnt:
                continue
            sem = (self.progD.sem if d == "D"
                   else self.progC.sem if d == "C" else self.prog[d].sem)
            w.wait_ge(sem, cnt)
            self.last_wait[eng][d] = cnt
            if d == "D":
                dwait = True
        if dwait and eng in ("T", "V", "S") and hasattr(self, "flushers"):
            fl = self.flushers.get(eng)
            if fl is not None:
                fl(w)
        r.calls = w.calls + r.calls
        if eng != "D":
            p = self.prog[eng]
            fl = self.flushers.get(eng) if hasattr(self, "flushers") else None
            if fl is not None:
                fl(r)
            tgt = None
            for entry in reversed(r.calls):
                if (entry[0] not in ("wait_ge", "sem_inc", "dma_start")
                        and not entry[3]):
                    tgt = entry
                    break
            if tgt is not None:
                tgt[3].append((p.sem, 1))
            else:
                r.sem_inc(p.sem, 1)
            p.n += 1
        self._track(rd, wr, eng)
        if eng == "S":
            af = self._afs(r)
            if len(self.afset | af) > 5:
                self.flush()
            self.afset |= af
        self.stepidx = getattr(self, "stepidx", 0) + 1
        if eng == "D":
            def big(c):
                if c[0] != "dma_start":
                    return False
                out = c[2].get("out")
                if str(out.space) != "MemorySpace.SBUF":
                    return False
                n = 1
                for s in out.shape[1:]:
                    n *= s
                return n * (2 if "bfloat16" in str(out.dtype) else 4) > 1024
            if any(big(c) for c in r.calls):
                self.flush()
                self.pend[eng].append(r)
                self.npend += 1
                self.flush()
                return
        self.pend[eng].append(r)
        self.npend += 1
        inwin = self.BATCH_LO <= self.stepidx < self.BATCH_HI
        if self.npend >= (self.MAXSTEPS if inwin else 1):
            self.flush()

    def dma(self, e, out, in_):
        self.progD.tick(e.dma_start(out=out, in_=in_))


def build_program(taps=()):
    nc = bass.Bass("TRN2", target_bir_lowering=False, debug=False)
    taps = set(taps)

    def din(name, shape, dt=F32):
        return nc.dram_tensor(name, shape, dt, kind="ExternalInput").ap()

    x_fm = din("x_fm", [D, SEQ * T])
    w0_inT = din("w0_inT", [D, 2 * DI])
    conv_w0 = din("conv_w0", [128, NDI * 4])
    conv_b0 = din("conv_b0", [128, NDI])
    w0_xprojT = din("w0_xprojT", [DI, 96])
    w0_dtT = din("w0_dtT", [DTR, DI])
    b_dt0 = din("b_dt0", [128, NDI])
    A_log0 = din("A_log0", [128, NDI * NST])
    D_skip0 = din("D_skip0", [128, NDI])
    w0_outT = din("w0_outT", [DI, D])
    clip_x = din("clip_x", [1, 1])
    clip_down = din("clip_down", [1, 1])
    wg0 = din("wg0", [I0, D])
    wu0 = din("wu0", [I0, D])
    wd0 = din("wd0", [D, I0])
    w1_in = din("w1_in", [D, ZC])
    conv_w1 = din("conv_w1", [128, NZC * 4])
    theta1 = din("theta1", [128, ND])
    slopes1 = din("slopes1", [32, 1])
    amask_i = din("amask", [32, 1])
    dmask_i = din("dmask", [32, 1])
    score1 = din("score1", [32, 1])
    w_re = din("w_re", [32, 32])
    w_im = din("w_im", [32, 32])
    ns_re = din("ns_re", [128, ND])
    ns_im = din("ns_im", [128, ND])
    w1_out = din("w1_out", [D, D])
    wg1 = din("wg1", [I0, D])
    wu1 = din("wu1", [I0, D])
    wd1 = din("wd1", [D, I0])
    ident_i = din("ident", [128, 128])
    selrow_b_i = din("selrow_b", [32, 32 * 128], BF16)
    selinv_i = din("selinv", [32, 8 * 128])
    selsq_i = din("selsq", [128, 8 * 32])
    ones1f_i = din("ones1f", [1, 128])
    consts_i = din("consts", [128, 16])

    out_fm = nc.dram_tensor("out_fm", [D, SEQ * T], F32, kind="ExternalOutput").ap()
    dbg = {}
    for tname, shape, dt in [("xn", [D, SEQ * T], F32), ("s0", [D, SEQ * T], BF16),
                             ("gmax", [1, 2], F32), ("hmid", [I0, SEQ * T], F32),
                             ("x1", [D, SEQ * T], F32), ("g1", [D, SEQ * T], BF16),
                             ("ym", [DI, SEQ * T], BF16),
                             ("kx", [D, SEQ * T], BF16), ("wqg", [D, I0], BF16),
                             ("swg", [I0, 1], F32), ("gt", [I0, SEQ * T], F32),
                             ("up", [I0, SEQ * T], F32),
                             ("xn2", [D, SEQ * T], F32), ("iq", [128, 2], F32),
                             ("wab", [32, 8], F32), ("pw", [32, SEQ * T], F32),
                             ("xv", [D, SEQ * T], F32), ("yg", [D, SEQ * T], BF16),
                             ("idn", [32, SEQ * T], F32)]:
        if tname in taps:
            dbg[tname] = nc.dram_tensor("dbg_" + tname, shape, dt,
                                        kind="ExternalOutput").ap()

    xn_d = nc.dram_tensor("xn_d", [D, SEQ * T], F32).ap()
    ym_d = nc.dram_tensor("ym_d", [DI, SEQ * T], BF16).ap()
    s0_d = nc.dram_tensor("s0_d", [D, SEQ * T], BF16).ap()
    hmid_d = nc.dram_tensor("hmid_d", [I0, SEQ * T], F32).ap()
    x1_d = nc.dram_tensor("x1_d", [D, SEQ * T], F32).ap()
    xn1_d = nc.dram_tensor("xn1_d", [D, SEQ * T], F32).ap()
    g1_d = nc.dram_tensor("g1_d", [D, SEQ * T], BF16).ap()
    kh1_d = nc.dram_tensor("kh1_d", [SEQ * T, I0], BF16).ap()
    wqgT_d = nc.dram_tensor("wqgT_d", [D, I0], BF16).ap()
    wquT_d = nc.dram_tensor("wquT_d", [D, I0], BF16).ap()
    wqdT_d = nc.dram_tensor("wqdT_d", [I0, D], BF16).ap()
    swg_d = nc.dram_tensor("swg_d", [I0, 1], F32).ap()
    swu_d = nc.dram_tensor("swu_d", [I0, 1], F32).ap()
    swd_d = nc.dram_tensor("swd_d", [D, 1], F32).ap()
    wq4gT_d = nc.dram_tensor("wq4gT_d", [D, I0], BF16).ap()
    wq4uT_d = nc.dram_tensor("wq4uT_d", [D, I0], BF16).ap()
    wq4dT_d = nc.dram_tensor("wq4dT_d", [I0, D], BF16).ap()
    cc_in = nc.dram_tensor("cc_in", [1, 1], F32).ap()
    cc_out = nc.dram_tensor("cc_out", [1, 1], F32).ap()
    cc_in2 = nc.dram_tensor("cc_in2", [1, 1], F32).ap()
    cc_out2 = nc.dram_tensor("cc_out2", [1, 1], F32).ap()

    gs = contextlib.ExitStack()

    def g_sb(name, shape, dt=F32):
        return gs.enter_context(nc.sbuf_tensor(name, shape, dt))

    sident = g_sb("sident", [128, 128])
    sidentb = g_sb("sidentb", [128, 128], BF16)
    s1f = g_sb("s1f", [1, 128])
    selrow_b = g_sb("sb_selrow_b", [32, 32 * 128], BF16)
    selinv = g_sb("sb_selinv", [32, 8 * 128])
    selsq = g_sb("sb_selsq", [128, 8 * 32])
    convw0 = g_sb("convw0", [128, NDI * 4])
    convb0 = g_sb("convb0", [128, NDI])
    bdt0 = g_sb("bdt0", [128, NDI])
    negA = g_sb("negA", [128, NDI * NST])
    dskip = g_sb("dskip", [128, NDI])
    sclipx = g_sb("sclipx", [1, 1])
    sclipd = g_sb("sclipd", [1, 1])
    convw1 = g_sb("convw1", [128, NZC * 4])
    thcol = g_sb("thcol", [128, ND])
    th2pi = g_sb("th2pi", [128, ND])
    nsre_c = g_sb("nsre_c", [128, ND])
    nsim_c = g_sb("nsim_c", [128, ND])
    spd = g_sb("spd", [32, 1])
    nspd = g_sb("nspd", [32, 1])
    amask = g_sb("amask_s", [32, 1])
    dmask = g_sb("dmask_s", [32, 1])
    walpha = g_sb("walpha", [32, 1])
    wbeta = g_sb("wbeta", [32, 1])
    sco1 = g_sb("sco1", [32, 1])
    sl_a = g_sb("sl_a", [32, 1])
    sl_b = g_sb("sl_b", [32, 1])
    bdiag_re = g_sb("bdiag_re", [128, 128], BF16)
    bdiag_im = g_sb("bdiag_im", [128, 128], BF16)
    wre_st = g_sb("wre_st", [128, 32])
    wim_st = g_sb("wim_st", [128, 32])
    mx1buf = g_sb("mx1buf", [128, NTILES * ND])
    mx2buf = g_sb("mx2buf", [128, NIC * NTILES])
    vpad = g_sb("vpad", [128, 512])
    spad = g_sb("spad", [128, 512])
    iqst = g_sb("iqst", [128, 2])
    wabst = g_sb("wabst", [32, 8])
    hcarry = g_sb("hcarry", [128, NDI * NST])
    gmax1 = g_sb("gmax1", [1, 1])
    gmax2 = g_sb("gmax2", [1, 1])
    invq1 = g_sb("invq1", [128, 1])
    invq2 = g_sb("invq2", [128, 1])
    sxcol = g_sb("sxcol", [128, 1])
    shcol = g_sb("shcol", [128, 1])
    sq1 = g_sb("sq1", [1, 2])
    sq2 = g_sb("sq2", [1, 2])
    onescol = g_sb("onescol", [128, 1])
    mrow = g_sb("mrow", [1, TT])
    rrow = g_sb("rrow", [1, TT])
    qrow = g_sb("qrow", [1, TT])
    wsc_mx = g_sb("wsc_mx", [128, 1])

    sconsts = g_sb("sconsts", [128, 16])

    pb = [gs.enter_context(nc.psum_tensor(f"pb{i}", [128, 512], F32))
          for i in range(8)]

    em = EM(nc)
    em.flushers = {
        "V": lambda e: e.tensor_scalar_mul(vpad[:], vpad[:], 1.0),
        "S": lambda e: e.copy(spad[:], spad[:]),
        "T": lambda e: e.matmul(pb[2][0:1, 448:449], sident[0:1, 0:1],
                                sident[0:1, 0:1], start=True, stop=True),
    }
    for i, v in enumerate(CONST_VALS):
        nc.const_aps.register(F32, float(v), sconsts[:, i:i + 1])

    def seg(buf, c, W):
        return buf[:, c * W:(c + 1) * W]

    def col(buf, c):
        return buf[:, c:c + 1]

    def sep(e):
        e.tensor_scalar_mul(vpad[:], vpad[:], 1.0)

    def d_setup(e):
        for dst, src in [(sident, ident_i), (s1f, ones1f_i), (sconsts, consts_i),
                         (selrow_b, selrow_b_i), (selinv, selinv_i),
                         (selsq, selsq_i), (convw0, conv_w0), (convb0, conv_b0),
                         (bdt0, b_dt0), (negA, A_log0), (dskip, D_skip0),
                         (sclipx, clip_x), (sclipd, clip_down),
                         (convw1, conv_w1), (thcol, theta1), (spd, slopes1),
                         (amask, amask_i), (dmask, dmask_i),
                         (sco1, score1), (nsre_c, ns_re), (nsim_c, ns_im)]:
            em.dma(e, dst[:], src)
        for j in range(4):
            em.dma(e, wre_st[32 * j:32 * j + 32, :], w_re)
            em.dma(e, wim_st[32 * j:32 * j + 32, :], w_im)
    em.step("D", d_setup)

    def s_setup(e):
        if "wab" in dbg:
            e.copy(wabst[:, 6:7], spd[:])
        e.copy(sidentb[:], sident[:])
        e.activation(negA[:], negA[:], AF.Exp)
        e.activation(sl_a[:], spd[:], AF.Abs)
        e.activation(sl_a[:], sl_a[:], AF.Exp, bias=0.0, scale=-1.0)
        e.activation(sl_a[:], sl_a[:], AF.Ln, bias=1.0, scale=1.0)
        e.activation(sl_b[:], spd[:], AF.Relu)
    em.step("S", s_setup)

    def v_slopes(e):
        e.tensor_tensor(nspd[:], sl_a[:], sl_b[:], ALU.add)
        sep(e)
        e.tensor_tensor(walpha[:], nspd[:], amask[:], ALU.mult)
        e.tensor_tensor(wbeta[:], nspd[:], dmask[:], ALU.mult)
        sep(e)
    em.step("V", v_slopes)
    if "wab" in dbg:
        def s_wab(e):
            e.copy(wabst[:, 0:1], spd[:])
            e.copy(wabst[:, 1:2], nspd[:])
            e.copy(wabst[:, 2:3], walpha[:])
            e.copy(wabst[:, 3:4], wbeta[:])
            e.copy(wabst[:, 4:5], sl_a[:])
            e.copy(wabst[:, 5:6], sl_b[:])
            e.copy(wabst[:, 7:8], amask[:])
        em.step("S", s_wab)
        em.step("D", lambda e: em.dma(e, dbg["wab"], wabst[:]))

    def v_setup(e):
        e.tensor_scalar_mul(negA[:], negA[:], -1.0)
        e.tensor_scalar_mul(th2pi[:], thcol[:], 1.0 / (2.0 * PI))
        e.memset(vpad[:], 0.0)
        e.memset(hcarry[:], 0.0)
        e.memset(bdiag_re[:], 0.0)
        e.memset(bdiag_im[:], 0.0)
        e.memset(onescol[:], 1.0)
    em.step("V", v_setup)

    def s_bdiag(e):
        for j in range(4):
            e.copy(bdiag_re[32 * j:32 * j + 32, 32 * j:32 * j + 32],
                   wre_st[32 * j:32 * j + 32, :])
            e.copy(bdiag_im[32 * j:32 * j + 32, 32 * j:32 * j + 32],
                   wim_st[32 * j:32 * j + 32, :])
    em.step("S", s_bdiag)

    em.new_block()
    wb = contextlib.ExitStack()

    def wb_sb(name, shape, dt=BF16):
        return wb.enter_context(nc.sbuf_tensor(name, shape, dt))

    bw_in = wb_sb("bw_in", [128, ND * (2 * DI)])
    bw_xp = wb_sb("bw_xp", [128, NDI * 96])
    bw_dt = wb_sb("bw_dt", [DTR, DI])
    ws = contextlib.ExitStack()

    def ws_sb(name, shape, dt=F32):
        return ws.enter_context(nc.sbuf_tensor(name, shape, dt))

    wstage = ws_sb("wstage", [128, 4096])
    wrow = ws_sb("wrow", [128, I0], BF16)
    wsc = ws_sb("wsc", [128, 1])
    wrec = ws_sb("wrec", [128, 1])
    wint = ws_sb("wint", [128, I0], I32)
    wtmp = ws_sb("wtmp", [128, I0])
    wtm2 = ws_sb("wtm2", [128, I0])
    wtm3 = ws_sb("wtm3", [128, I0])
    wq4sc = ws_sb("wq4sc", [128, I0 // 16])
    wq4rc = ws_sb("wq4rc", [128, I0 // 16])

    def cast_weight(dst, src, R, C):
        for c in range(cdiv(R, CH)):
            n = min(CH, R - c * CH)
            em.step("D", lambda e, c=c, n=n: em.dma(
                e, wstage[0:n, 0:C], src[c * CH:c * CH + n, :]))
            em.step("S", lambda e, c=c, n=n: e.copy(
                seg(dst, c, C)[0:n, :], wstage[0:n, 0:C]))

    cast_weight(bw_in, w0_inT, D, 2 * DI)
    cast_weight(bw_xp, w0_xprojT, DI, 96)

    def d_bwdt(e):
        em.dma(e, wstage[0:DTR, 0:DI], w0_dtT)
    em.step("D", d_bwdt)
    em.step("S", lambda e: e.copy(bw_dt[:], wstage[0:DTR, 0:DI]))

    def transpose_store(srcf, n, C, dst_d, c):
        for j0 in range(0, cdiv(C, CH), 2):
            def tstep(e, j0=j0):
                for j in range(j0, min(j0 + 2, cdiv(C, CH))):
                    m = min(CH, C - j * CH)
                    e.transpose(pb[j % 2][0:m, 0:n],
                                srcf[0:n, j * CH:j * CH + m], sident[0:n, 0:n])
                e.matmul(pb[j0 % 2][0:1, 400:401], sident[0:1, 0:1],
                         sident[0:1, 0:1], start=True, stop=True)
            em.step("T", tstep)

            def sstep(e, j0=j0):
                for j in range(j0, min(j0 + 2, cdiv(C, CH))):
                    m = min(CH, C - j * CH)
                    e.copy(wrow[0:m, 0:n], pb[j % 2][0:m, 0:n])
                    em.progW.tick(e.dma_start(
                        out=dst_d[j * CH:j * CH + m, c * CH:c * CH + n],
                        in_=wrow[0:m, 0:n]))
                    e.wait_ge(em.progW.sem, em.progW.n)
            em.step("S", sstep)

    def quant_int8_weight(src, R, C, dstT_d, scale_d):
        for c in range(cdiv(R, CH)):
            n = min(CH, R - c * CH)
            em.step("D", lambda e, c=c, n=n: em.dma(
                e, wstage[0:n, 0:C], src[c * CH:c * CH + n, :]))

            def v1(e, c=c, n=n):
                e.tensor_reduce(wsc[0:n, :], wstage[0:n, 0:C], AX.X, ALU.max,
                                apply_absolute_value=True)
                sep(e)
                e.reciprocal(wrec[0:n, :], wsc[0:n, :])
                e.tensor_scalar_mul(wsc[0:n, :], wsc[0:n, :], 1.0 / 127.0)
                sep(e)
                e.tensor_scalar(wtmp[0:n, 0:C], wstage[0:n, 0:C],
                                wrec[0:n, 0:1], 127.0, ALU.mult, ALU.mult)
                e.tensor_scalar(wtmp[0:n, 0:C], wtmp[0:n, 0:C], 1.0, 0.5,
                                ALU.mult, ALU.add)
                e.tensor_scalar(wint[0:n, 0:C], wtmp[0:n, 0:C], 1.0, 0.0,
                                ALU.mult, ALU.add)
                e.tensor_scalar(wtm3[0:n, 0:C], wint[0:n, 0:C], 1.0, 0.0,
                                ALU.mult, ALU.add)
                e.tensor_tensor(wtm2[0:n, 0:C], wtm3[0:n, 0:C], wtmp[0:n, 0:C],
                                ALU.is_gt)
                e.tensor_tensor(wtm3[0:n, 0:C], wtm3[0:n, 0:C], wtm2[0:n, 0:C],
                                ALU.subtract)
                e.tensor_scalar(wtm2[0:n, 0:C], wtm3[0:n, 0:C], 127.0, -128.0,
                                ALU.min, ALU.max)
            em.step("V", v1)
            em.step("D", lambda e, c=c, n=n: em.dma(
                e, scale_d[c * CH:c * CH + n, :], wsc[0:n, :]))
            transpose_store(wtm2, n, C, dstT_d, c)

    quant_int8_weight(wg0, I0, D, wqgT_d, swg_d)
    if "wqg" in dbg:
        em.step("D", lambda e: em.dma(e, dbg["wqg"], wqgT_d))
        em.step("D", lambda e: em.dma(e, dbg["swg"], swg_d))
    quant_int8_weight(wu0, I0, D, wquT_d, swu_d)
    quant_int8_weight(wd0, D, I0, wqdT_d, swd_d)

    def fp4_core(e, zb, n, C, scb, rcb, ib, sgn, ab, fb, outb):
        nb = C // 16
        z = zb[0:n, 0:C]
        sg = sgn[0:n, 0:C]
        a = ab[0:n, 0:C]
        f = fb[0:n, 0:C]
        ii = ib[0:n, 0:C]
        e.tensor_reduce(scb[0:n, 0:nb].rearrange("p (g o) -> p g o", o=1),
                        z.rearrange("p (g k) -> p g k", k=16),
                        AX.X, ALU.max, apply_absolute_value=True)
        e.tensor_single_scalar(sg, z, 0.0, ALU.is_gt)
        e.tensor_scalar(sg, sg, 2.0, -1.0, ALU.mult, ALU.add)
        e.tensor_scalar(scb[0:n, 0:nb], scb[0:n, 0:nb], 1.0 / 6.0, 1e-12,
                        ALU.mult, ALU.max)
        e.tensor_scalar_mul(a, z, -1.0)
        e.tensor_tensor(a, a, z, ALU.max)
        e.reciprocal(rcb[0:n, 0:nb], scb[0:n, 0:nb])
        e.tensor_scalar_mul(a, a, 2.0)
        rb = rcb[0:n, 0:nb].unsqueeze(2).broadcast_to([n, nb, 16])
        e.tensor_tensor(a.rearrange("p (g k) -> p g k", k=16),
                        a.rearrange("p (g k) -> p g k", k=16), rb, ALU.mult)
        e.tensor_scalar(ii, a, 1.0, 0.0, ALU.mult, ALU.add)
        e.tensor_scalar(f, ii, 1.0, 0.0, ALU.mult, ALU.add)
        e.tensor_tensor(z, a, f, ALU.subtract)
        e.tensor_single_scalar(z, z, 0.5, ALU.is_gt)
        e.tensor_tensor(f, f, z, ALU.add)
        e.tensor_scalar(f, f, 0.5, 2.0, ALU.mult, ALU.min)
        e.tensor_single_scalar(z, a, 10.0, ALU.is_gt)
        e.scalar_tensor_tensor(f, z, 2.0, f, ALU.mult, ALU.add)
        e.tensor_scalar_mul(a, a, 0.5)
        e.tensor_scalar(ii, a, 1.0, 0.0, ALU.mult, ALU.add)
        e.tensor_scalar(z, ii, 1.0, 0.0, ALU.mult, ALU.add)
        e.tensor_tensor(a, a, z, ALU.subtract)
        e.tensor_single_scalar(a, a, 0.5, ALU.is_gt)
        e.tensor_scalar(z, z, 1.0, -2.0, ALU.mult, ALU.add)
        e.tensor_tensor(z, z, a, ALU.add)
        e.tensor_scalar(z, z, 0.0, 2.0, ALU.max, ALU.min)
        e.tensor_tensor(f, f, z, ALU.add)
        e.tensor_tensor(f, f, sg, ALU.mult)
        sb_ = scb[0:n, 0:nb].unsqueeze(2).broadcast_to([n, nb, 16])
        e.tensor_tensor(outb[0:n, 0:C].rearrange("p (g k) -> p g k", k=16),
                        f.rearrange("p (g k) -> p g k", k=16), sb_, ALU.mult)

    def quant_fp4_weight(src, R, C, dstT_d):
        for c in range(cdiv(R, CH)):
            n = min(CH, R - c * CH)
            em.step("D", lambda e, c=c, n=n: em.dma(
                e, wstage[0:n, 0:C], src[c * CH:c * CH + n, :]))
            em.step("V", lambda e, c=c, n=n: fp4_core(
                e, wstage, n, C, wq4sc, wq4rc, wint, wtmp, wtm2, wtm3, wtm3))
            transpose_store(wtm3, n, C, dstT_d, c)

    quant_fp4_weight(wg1, I0, D, wq4gT_d)
    quant_fp4_weight(wu1, I0, D, wq4uT_d)
    quant_fp4_weight(wd1, D, I0, wq4dT_d)

    ws.close()

    def emit_ln(srcbuf, dstbuf, sqbuf, it_for_max=None, mxbuf=None, dst_bf=None):
        def s_sq(e):
            for c in range(ND):
                e.activation(seg(sqbuf, c, TT), seg(srcbuf, c, TT), AF.Square)
        em.step("S", s_sq)

        def t_sums(e):
            for c in range(ND):
                e.matmul(pb[0][0:1, 0:TT], onescol[:], seg(srcbuf, c, TT),
                         start=(c == 0), stop=(c == ND - 1))
            for c in range(ND):
                e.matmul(pb[1][0:1, 0:TT], onescol[:], seg(sqbuf, c, TT),
                         start=(c == 0), stop=(c == ND - 1))
        em.step("T", t_sums)

        def v_stats(e):
            e.tensor_scalar_mul(mrow[:], pb[0][0:1, 0:TT], 1.0 / D)
            e.tensor_scalar_mul(rrow[:], pb[1][0:1, 0:TT], 1.0 / D)
            e.tensor_tensor(qrow[:], mrow[:], mrow[:], ALU.mult)
            e.tensor_tensor(rrow[:], rrow[:], qrow[:], ALU.subtract)
        em.step("V", v_stats)
        em.step("S", lambda e: e.activation(rrow[:], rrow[:], AF.Sqrt,
                                            bias=1e-5, scale=1.0))

        def v_recip(e):
            e.reciprocal(qrow[:], rrow[:])
            e.tensor_tensor(mrow[:], mrow[:], qrow[:], ALU.mult)
        em.step("V", v_recip)

        def t_bcast(e):
            e.matmul(pb[2][:, 0:TT], s1f[:], qrow[:], start=True, stop=True)
            e.matmul(pb[3][:, 0:TT], s1f[:], mrow[:], start=True, stop=True)
        em.step("T", t_bcast)

        def v_xn(e):
            W = ND * TT
            e.tensor_tensor(
                dstbuf[:, 0:W].rearrange("p (c t) -> p c t", t=TT),
                srcbuf[:, 0:W].rearrange("p (c t) -> p c t", t=TT),
                pb[2][:, 0:TT].unsqueeze(1).broadcast_to([128, ND, TT]),
                ALU.mult)
            e.tensor_tensor(
                dstbuf[:, 0:W].rearrange("p (c t) -> p c t", t=TT),
                dstbuf[:, 0:W].rearrange("p (c t) -> p c t", t=TT),
                pb[3][:, 0:TT].unsqueeze(1).broadcast_to([128, ND, TT]),
                ALU.subtract)
            if it_for_max is not None:
                e.tensor_reduce(
                    mxbuf[:, it_for_max * ND:(it_for_max + 1) * ND]
                    .rearrange("p (c o) -> p c o", o=1),
                    dstbuf[:, 0:W].rearrange("p (c t) -> p c t", t=TT),
                    AX.X, ALU.max, apply_absolute_value=True)
        em.step("V", v_xn)
        if dst_bf is not None:
            def s_cast(e):
                for c in range(ND):
                    e.copy(seg(dst_bf, c, TT), seg(dstbuf, c, TT))
            em.step("S", s_cast)

    em.new_block()
    p1 = contextlib.ExitStack()

    def p1sb(name, shape, dt=F32):
        return p1.enter_context(nc.sbuf_tensor(name, shape, dt))

    xpool = p1sb("xpool", [128, ND * TT])
    xt = xpool
    xinbuf = xpool.bitcast(BF16)
    XW = TT + 3
    xpad = p1sb("xpad", [128, 24])
    xnt = p1sb("xnt", [128, ND * TT])
    spa = xnt
    xnb = p1sb("xnb", [128, ND * TT], BF16)
    cacc = p1sb("cacc", [128, NDI * TT])
    ubuf = p1sb("ubuf", [128, NDI * TT], BF16)
    zraw = p1sb("zraw", [128, NDI * TT], BF16)
    dtlo_b = p1sb("dtlo_b", [DTR, TT], BF16)
    bcrow = p1sb("bcrow", [32, TT], BF16)
    Bbc = p1sb("Bbc", [128, NST * TT], BF16)
    Cbc = p1sb("Cbc", [128, NST * TT], BF16)
    dAb = p1sb("dAb", [128, NST * TT], BF16)
    dBb = p1sb("dBb", [128, NST * TT], BF16)
    hbb = p1sb("hbb", [128, NST * TT], BF16)
    dub = p1sb("dub", [128, TT], BF16)
    szk = p1sb("szk", [128, TT], BF16)
    ybuf = p1sb("ybuf", [128, TT])
    ymb = p1sb("ymb", [128, NDI * TT], BF16)
    dtf = cacc
    xtail = p1sb("xtail", [128, NDI * 3], BF16)

    def xin(c):
        return xinbuf[:, c * TT:(c + 1) * TT]

    for it in range(NTILES):
        t0 = it * TT
        seq_start = (it % TPS == 0)

        def d_x(e, t0=t0):
            for c in range(ND):
                em.dma(e, seg(xt, c, TT), x_fm[c * CH:(c + 1) * CH, t0:t0 + TT])
        em.step("D", d_x)

        emit_ln(xt, xnt, cacc, it_for_max=it, mxbuf=mx1buf, dst_bf=xnb)

        def d_xn(e, t0=t0):
            for c in range(ND):
                em.dma(e, xn_d[c * CH:(c + 1) * CH, t0:t0 + TT], seg(xnt, c, TT))
                if "xn" in dbg:
                    em.dma(e, dbg["xn"][c * CH:(c + 1) * CH, t0:t0 + TT],
                           seg(xnt, c, TT))
        em.step("D", d_xn)

        for g in range(8):
            def t_xz(e, g=g):
                for mm in range(4):
                    m = g * 4 + mm
                    for k in range(ND):
                        e.matmul(pb[4 + mm][:, 0:TT],
                                 seg(bw_in, k, 2 * DI)[:, m * CH:(m + 1) * CH],
                                 seg(xnb, k, TT), start=(k == 0), stop=(k == ND - 1))
            em.step("T", t_xz)

            def s_xz(e, g=g):
                for mm in range(4):
                    m = g * 4 + mm
                    if m < NDI:
                        e.copy(xin(m), pb[4 + mm][:, 0:TT])
                    else:
                        e.copy(seg(zraw, m - NDI, TT), pb[4 + mm][:, 0:TT])
            em.step("S", s_xz)

        def v_conv(e, seq_start=seq_start):
            for c in range(NDI):
                tl = seg(xtail, c, 3)
                xb_ = xin(c)
                if seq_start:
                    e.tensor_scalar(seg(cacc, c, TT)[:, 0:3], xb_[:, 0:3],
                                    0.0, col(convb0, c), ALU.mult, ALU.add)
                else:
                    e.tensor_scalar(seg(cacc, c, TT)[:, 0:3], tl,
                                    col(convw0, c * 4 + 0), col(convb0, c),
                                    ALU.mult, ALU.add)
                e.tensor_scalar(seg(cacc, c, TT)[:, 3:TT], xb_[:, 0:TT - 3],
                                col(convw0, c * 4 + 0), col(convb0, c),
                                ALU.mult, ALU.add)
                for j in (1, 2, 3):
                    sh = 3 - j
                    if sh > 0 and not seq_start:
                        e.scalar_tensor_tensor(seg(cacc, c, TT)[:, 0:sh],
                                               tl[:, j:3], col(convw0, c * 4 + j),
                                               seg(cacc, c, TT)[:, 0:sh],
                                               ALU.mult, ALU.add)
                    e.scalar_tensor_tensor(seg(cacc, c, TT)[:, sh:TT],
                                           xb_[:, 0:TT - sh], col(convw0, c * 4 + j),
                                           seg(cacc, c, TT)[:, sh:TT],
                                           ALU.mult, ALU.add)
            for c in range(NDI):
                e.tensor_copy(seg(xtail, c, 3), xin(c)[:, TT - 3:TT])
        em.step("V", v_conv)

        def s_silu_u(e):
            for c in range(NDI):
                e.activation(seg(ubuf, c, TT), seg(cacc, c, TT), AF.Silu)
        em.step("S", s_silu_u)

        def t_dbc(e):
            for k in range(NDI):
                e.matmul(pb[0][0:64, 0:TT], seg(bw_xp, k, 96)[:, 0:64],
                         seg(ubuf, k, TT), start=(k == 0), stop=(k == NDI - 1))
            for k in range(NDI):
                e.matmul(pb[1][0:32, 0:TT], seg(bw_xp, k, 96)[:, 64:96],
                         seg(ubuf, k, TT), start=(k == 0), stop=(k == NDI - 1))
        em.step("T", t_dbc)

        def s_dbc(e):
            e.copy(dtlo_b[:], pb[0][0:64, 0:TT])
            e.copy(bcrow[:], pb[1][0:32, 0:TT])
        em.step("S", s_dbc)

        for g in range(4):
            def t_bc(e, g=g):
                for q in range(8):
                    n8 = g * 8 + q
                    bank = pb[4 + q // 2]
                    off = (q % 2) * TT
                    e.matmul(bank[:, off:off + TT],
                             selrow_b[:, n8 * 128:(n8 + 1) * 128],
                             bcrow[:], start=True, stop=True)
            em.step("T", t_bc)

            def s_bc(e, g=g):
                for q in range(8):
                    n8 = g * 8 + q
                    bank = pb[4 + q // 2]
                    off = (q % 2) * TT
                    dst = Bbc if n8 < 16 else Cbc
                    e.copy(seg(dst, n8 % 16, TT), bank[:, off:off + TT])
            em.step("S", s_bc)

        for g in range(4):
            def t_dt(e, g=g):
                for q in range(4):
                    c = g * 4 + q
                    e.matmul(pb[4 + q][:, 0:TT], bw_dt[:, c * CH:(c + 1) * CH],
                             dtlo_b[:], start=True, stop=True)
            em.step("T", t_dt)

            def s_dt(e, g=g):
                for q in range(4):
                    c = g * 4 + q
                    e.copy(seg(cacc, g * 4 + q, TT), pb[4 + q][:, 0:TT])
            em.step("S", s_dt)

        for g in range(4):
            def s_sp(e, g=g):
                for q in range(4):
                    c = g * 4 + q
                    e.activation(seg(spa, q, TT), seg(cacc, c, TT), AF.Abs,
                                 bias=col(bdt0, c), scale=1.0)
                    e.activation(seg(spa, q, TT), seg(spa, q, TT), AF.Exp,
                                 bias=0.0, scale=-1.0)
                    e.activation(seg(spa, q, TT), seg(spa, q, TT), AF.Ln,
                                 bias=1.0, scale=1.0)
                    e.activation(seg(spa, 4 + q, TT), seg(cacc, c, TT), AF.Relu,
                                 bias=col(bdt0, c), scale=1.0)
            em.step("S", s_sp)

            def v_sp(e, g=g):
                for q in range(4):
                    c = g * 4 + q
                    e.tensor_tensor(seg(dtf, c, TT), seg(spa, q, TT),
                                    seg(spa, 4 + q, TT), ALU.add)
            em.step("V", v_sp)

        if seq_start:
            em.step("V", lambda e: e.memset(hcarry[:], 0.0))

        for c in range(NDI):
            NH = NST // 2
            for h in range(2):
                def v_dAr(e, c=c, h=h):
                    e.tensor_tensor(
                        spa[:, 0:NH * TT].rearrange("p (n t) -> p n t", t=TT),
                        seg(dtf, c, TT).unsqueeze(1).broadcast_to([128, NH, TT]),
                        negA[:, c * NST + h * NH:c * NST + (h + 1) * NH]
                        .unsqueeze(2).broadcast_to([128, NH, TT]),
                        ALU.mult)
                em.step("V", v_dAr)
                em.step("S", lambda e, h=h: e.activation(
                    dAb[:, h * NH * TT:(h + 1) * NH * TT],
                    spa[:, 0:NH * TT], AF.Exp))
            em.step("S", lambda e, c=c: e.activation(
                szk[:], seg(zraw, c, TT), AF.Silu))

            def v_scan(e, c=c):
                e.tensor_tensor(dub[:], seg(dtf, c, TT), seg(ubuf, c, TT), ALU.mult)
                e.tensor_tensor(
                    dBb[:].rearrange("p (n t) -> p n t", t=TT),
                    dub[:].unsqueeze(1).broadcast_to([128, NST, TT]),
                    Bbc[:].rearrange("p (n t) -> p n t", t=TT),
                    ALU.mult)
                for n in range(NST):
                    e.tensor_tensor_scan(seg(hbb, n, TT), seg(dAb, n, TT),
                                         seg(dBb, n, TT),
                                         col(hcarry, c * NST + n),
                                         ALU.mult, ALU.add)
                e.tensor_tensor(dBb[:], hbb[:], Cbc[:], ALU.mult)
                e.tensor_copy(hcarry[:, c * NST:(c + 1) * NST],
                              hbb[:].rearrange("p (n t) -> p n t", t=TT)[:, :, TT - 1])
                e.tensor_reduce(ybuf[:].rearrange("p (t o) -> p t o", o=1),
                                dBb[:].rearrange("p (n t) -> p t n", t=TT),
                                AX.X, ALU.add)
                e.scalar_tensor_tensor(ybuf[:], seg(ubuf, c, TT), col(dskip, c),
                                       ybuf[:], ALU.mult, ALU.add)
                e.tensor_tensor(seg(ymb, c, TT), ybuf[:], szk[:], ALU.mult)
            em.step("V", v_scan)

        def d_ym(e, t0=t0):
            for c in range(NDI):
                em.dma(e, ym_d[c * CH:(c + 1) * CH, t0:t0 + TT], seg(ymb, c, TT))
                if "ym" in dbg:
                    em.dma(e, dbg["ym"][c * CH:(c + 1) * CH, t0:t0 + TT],
                           seg(ymb, c, TT))
        em.step("D", d_ym)

    em.new_block()
    def v_gm1(e):
        sep(e)
        e.tensor_reduce(wsc_mx[:], mx1buf[:], AX.X, ALU.max)
    em.step("V", v_gm1)
    em.step("T", lambda e: e.matmul(pb[0][0:1, 0:128], wsc_mx[:], sident[:],
                                    start=True, stop=True))
    em.step("V", lambda e: e.tensor_reduce(gmax1[:], pb[0][0:1, 0:128],
                                           AX.X, ALU.max))
    em.step("D", lambda e: em.dma(e, cc_in, gmax1[:]))

    def g_cc1(e):
        em.progC.tick(e.collective_compute(
            "AllReduce", ALU.max, replica_groups=[list(range(NCORES))],
            ins=[cc_in.opt()], outs=[cc_out.opt()]))
    em.new_block()
    em.step("G", g_cc1)
    em.new_block()
    em.step("D", lambda e: em.dma(e, gmax1[:], cc_out), deps=em.marks())
    em.step("S", lambda e: e.activation(sq1[:, 0:1], sclipx[:], AF.Sigmoid))

    def v_sq1(e):
        e.tensor_tensor(sq1[:, 0:1], sq1[:, 0:1], gmax1[:], ALU.mult)
        sep(e)
        e.reciprocal(sq1[:, 1:2], sq1[:, 0:1])
        sep(e)
        e.tensor_scalar_mul(sq1[:, 1:2], sq1[:, 1:2], 127.0)
        e.tensor_scalar_mul(sq1[:, 0:1], sq1[:, 0:1], 1.0 / 127.0)
        sep(e)
    em.step("V", v_sq1)
    em.step("T", lambda e: e.matmul(pb[0][:, 0:2], s1f[:], sq1[:, 0:2],
                                    start=True, stop=True))

    def v_iq1(e):
        e.tensor_copy(sxcol[:], pb[0][:, 0:1])
        e.tensor_copy(invq1[:], pb[0][:, 1:2])
        sep(e)
    em.step("V", v_iq1)
    if "gmax" in dbg:
        em.step("D", lambda e: em.dma(e, dbg["gmax"][:, 0:1], gmax1[:]))
    if "iq" in dbg:
        def v_iqt(e):
            e.tensor_copy(iqst[:, 0:1], sxcol[:])
            e.tensor_copy(iqst[:, 1:2], invq1[:])
            sep(e)
        em.step("V", v_iqt)
        em.step("D", lambda e: em.dma(e, dbg["iq"], iqst[:]))

    p1.close()
    wb.close()

    em.new_block()
    p2 = contextlib.ExitStack()

    def p2sb(name, shape, dt=F32):
        return p2.enter_context(nc.sbuf_tensor(name, shape, dt))

    wqgT = p2sb("wqgT", [128, ND * I0], BF16)
    wquT = p2sb("wquT", [128, ND * I0], BF16)
    bw_out2 = p2sb("bw_out2", [128, NDI * D], BF16)
    swgc = p2sb("swgc", [128, NIC])
    swuc = p2sb("swuc", [128, NIC])
    scolg = p2sb("scolg", [128, NIC])
    scolu = p2sb("scolu", [128, NIC])
    xnt2 = p2sb("xnt2", [128, ND * TT])
    kxb = p2sb("kxb", [128, ND * TT], BF16)
    qi2 = p2sb("qi2", [128, TT], I32)
    qt2 = p2sb("qt2", [128, TT])
    qv2 = p2sb("qv2", [128, TT])
    qn2 = p2sb("qn2", [128, TT])
    hsil = p2sb("hsil", [128, TT])
    hm2 = p2sb("hm2", [128, 2 * TT])
    ym2 = p2sb("ym2", [128, NDI * TT], BF16)
    sev = p2sb("sev", [128, ND * TT], BF16)
    wst2 = p2sb("wst2", [128, 4096])
    if "gt" in dbg:
        gsc = p2sb("gsc", [128, 2 * TT])
        usc = p2sb("usc", [128, 2 * TT])

    em.step("D", lambda e: [
        [em.dma(e, seg(wqgT, k, I0), wqgT_d[k * CH:(k + 1) * CH, :]),
         em.dma(e, seg(wquT, k, I0), wquT_d[k * CH:(k + 1) * CH, :])]
        for k in range(ND)] and None)
    for k in range(NDI):
        em.step("D", lambda e, k=k: em.dma(e, wst2[0:CH, 0:D],
                                           w0_outT[k * CH:(k + 1) * CH, :]))
        em.step("S", lambda e, k=k: e.copy(seg(bw_out2, k, D), wst2[0:CH, 0:D]))

    def d_sw(e):
        for c in range(NIC):
            em.dma(e, col(swgc, c), swg_d[c * CH:(c + 1) * CH, :])
            em.dma(e, col(swuc, c), swu_d[c * CH:(c + 1) * CH, :])
    em.step("D", d_sw)

    def v_scol(e):
        for c in range(NIC):
            e.tensor_tensor(col(scolg, c), col(swgc, c), sxcol[:], ALU.mult)
            e.tensor_tensor(col(scolu, c), col(swuc, c), sxcol[:], ALU.mult)
    em.step("V", v_scol)

    for it in range(NTILES):
        t0 = it * TT

        def d_xn2(e, t0=t0):
            for c in range(ND):
                em.dma(e, seg(xnt2, c, TT), xn_d[c * CH:(c + 1) * CH, t0:t0 + TT])
            for c in range(NDI):
                em.dma(e, seg(ym2, c, TT), ym_d[c * CH:(c + 1) * CH, t0:t0 + TT])
        em.step("D", d_xn2)
        if "xn2" in dbg:
            def d_xn2t(e, t0=t0):
                e.wait_ge(em.progD.sem, em.progD.n)
                for c in range(ND):
                    em.dma(e, dbg["xn2"][c * CH:(c + 1) * CH, t0:t0 + TT],
                           seg(xnt2, c, TT))
            em.step("D", d_xn2t)

        def v_q2(e):
            W = 4 * TT
            qv = wst2[:, 0:W]
            qi = wst2.bitcast(I32)[:, W:2 * W]
            qf = wst2[:, 2 * W:3 * W]
            qn = wst2[:, 3 * W:4 * W]
            for g in range(2):
                xs = xnt2[:, g * W:(g + 1) * W]
                e.tensor_scalar(qv, xs, invq1[:], 0.5, ALU.mult, ALU.add)
                e.tensor_scalar(qi, qv, 1.0, 0.0, ALU.mult, ALU.add)
                e.tensor_scalar(qf, qi, 1.0, 0.0, ALU.mult, ALU.add)
                e.tensor_tensor(qn, qf, qv, ALU.is_gt)
                e.tensor_tensor(qf, qf, qn, ALU.subtract)
                e.tensor_scalar(kxb[:, g * W:(g + 1) * W], qf, 127.0, -128.0,
                                ALU.min, ALU.max)
        em.step("V", v_q2)
        if "kx" in dbg:
            def d_kx(e, t0=t0):
                for c in range(ND):
                    em.dma(e, dbg["kx"][c * CH:(c + 1) * CH, t0:t0 + TT],
                           seg(kxb, c, TT))
            em.step("D", d_kx)

        for m in range(NIC):
            def t_gu(e, m=m):
                for k in range(ND):
                    e.matmul(pb[4][:, 0:TT], seg(wqgT, k, I0)[:, m * CH:(m + 1) * CH],
                             seg(kxb, k, TT), start=(k == 0), stop=(k == ND - 1))
                for k in range(ND):
                    e.matmul(pb[5][:, 0:TT], seg(wquT, k, I0)[:, m * CH:(m + 1) * CH],
                             seg(kxb, k, TT), start=(k == 0), stop=(k == ND - 1))
            em.step("T", t_gu)
            em.step("S", lambda e, m=m: e.activation(
                hsil[:], pb[4][:, 0:TT], AF.Silu, bias=0.0, scale=col(scolg, m)))
            if "gt" in dbg:
                def v_gt(e, m=m):
                    e.tensor_scalar(seg(gsc, m % 2, TT), pb[4][:, 0:TT],
                                    col(scolg, m), 0.0, ALU.mult, ALU.add)
                    e.tensor_scalar(seg(usc, m % 2, TT), pb[5][:, 0:TT],
                                    col(scolu, m), 0.0, ALU.mult, ALU.add)
                em.step("V", v_gt)

                def d_gt(e, m=m, t0=t0):
                    em.dma(e, dbg["gt"][m * CH:(m + 1) * CH, t0:t0 + TT],
                           seg(gsc, m % 2, TT))
                    em.dma(e, dbg["up"][m * CH:(m + 1) * CH, t0:t0 + TT],
                           seg(usc, m % 2, TT))
                em.step("D", d_gt)

            def v_hm(e, m=m, it=it):
                e.scalar_tensor_tensor(seg(hm2, m % 2, TT), pb[5][:, 0:TT],
                                       col(scolu, m), hsil[:], ALU.mult, ALU.mult)
                e.tensor_reduce(col(mx2buf, it * NIC + m), seg(hm2, m % 2, TT),
                                AX.X, ALU.max, apply_absolute_value=True)
            em.step("V", v_hm)

            def d_hm(e, m=m, t0=t0):
                em.dma(e, hmid_d[m * CH:(m + 1) * CH, t0:t0 + TT],
                       seg(hm2, m % 2, TT))
                if "hmid" in dbg:
                    em.dma(e, dbg["hmid"][m * CH:(m + 1) * CH, t0:t0 + TT],
                           seg(hm2, m % 2, TT))
            em.step("D", d_hm)

        for g in range(2):
            def t_wo(e, g=g):
                for mm in range(4):
                    m = g * 4 + mm
                    for k in range(NDI):
                        e.matmul(pb[4 + mm][:, 0:TT],
                                 seg(bw_out2, k, D)[:, m * CH:(m + 1) * CH],
                                 seg(ym2, k, TT), start=(k == 0), stop=(k == NDI - 1))
            em.step("T", t_wo)

            def s_wo(e, g=g):
                for mm in range(4):
                    m = g * 4 + mm
                    e.copy(seg(sev, m, TT), pb[4 + mm][:, 0:TT])
            em.step("S", s_wo)

        def d_s0(e, t0=t0):
            for c in range(ND):
                em.dma(e, s0_d[c * CH:(c + 1) * CH, t0:t0 + TT], seg(sev, c, TT))
                if "s0" in dbg:
                    em.dma(e, dbg["s0"][c * CH:(c + 1) * CH, t0:t0 + TT],
                           seg(sev, c, TT))
        em.step("D", d_s0)

    em.new_block()
    def v_gm2(e):
        sep(e)
        e.tensor_reduce(wsc_mx[:], mx2buf[:], AX.X, ALU.max)
    em.step("V", v_gm2)
    em.step("T", lambda e: e.matmul(pb[0][0:1, 0:128], wsc_mx[:], sident[:],
                                    start=True, stop=True))
    em.step("V", lambda e: e.tensor_reduce(gmax2[:], pb[0][0:1, 0:128],
                                           AX.X, ALU.max))
    em.step("D", lambda e: em.dma(e, cc_in2, gmax2[:]))

    def g_cc2(e):
        em.progC.tick(e.collective_compute(
            "AllReduce", ALU.max, replica_groups=[list(range(NCORES))],
            ins=[cc_in2.opt()], outs=[cc_out2.opt()]))
    em.new_block()
    em.step("G", g_cc2)
    em.new_block()
    em.step("D", lambda e: em.dma(e, gmax2[:], cc_out2), deps=em.marks())
    em.step("S", lambda e: e.activation(sq2[:, 0:1], sclipd[:], AF.Sigmoid))

    def v_sq2(e):
        e.tensor_tensor(sq2[:, 0:1], sq2[:, 0:1], gmax2[:], ALU.mult)
        sep(e)
        e.reciprocal(sq2[:, 1:2], sq2[:, 0:1])
        sep(e)
        e.tensor_scalar_mul(sq2[:, 1:2], sq2[:, 1:2], 127.0)
        e.tensor_scalar_mul(sq2[:, 0:1], sq2[:, 0:1], 1.0 / 127.0)
        sep(e)
    em.step("V", v_sq2)
    em.step("T", lambda e: e.matmul(pb[0][:, 0:2], s1f[:], sq2[:, 0:2],
                                    start=True, stop=True))

    def v_iq2(e):
        e.tensor_copy(shcol[:], pb[0][:, 0:1])
        e.tensor_copy(invq2[:], pb[0][:, 1:2])
        sep(e)
    em.step("V", v_iq2)
    if "gmax" in dbg:
        em.step("D", lambda e: em.dma(e, dbg["gmax"][:, 1:2], gmax2[:]))

    p2.close()

    em.new_block()
    p3 = contextlib.ExitStack()

    def p3sb(name, shape, dt=F32):
        return p3.enter_context(nc.sbuf_tensor(name, shape, dt))

    wqdT = p3sb("wqdT", [128, NIC * D], BF16)
    swdc = p3sb("swdc", [128, ND])
    scold = p3sb("scold", [128, ND])
    hmt3 = p3sb("hmt3", [128, NIC * TT])
    khb = p3sb("khb", [128, NIC * TT], BF16)
    q3i = p3sb("q3i", [128, 4 * TT], I32)
    q3t = p3sb("q3t", [128, 4 * TT])
    q3v = p3sb("q3v", [128, 4 * TT])
    q3n = p3sb("q3n", [128, 4 * TT])
    xt3 = p3sb("xt3", [128, ND * TT])
    st3 = p3sb("st3", [128, ND * TT], BF16)
    gev = p3sb("gev", [128, ND * TT], BF16)
    m3 = p3sb("m3", [128, TT])
    x1t = p3sb("x1t", [128, ND * TT])
    xn1t = p3sb("xn1t", [128, ND * TT])
    sq3 = p3sb("sq3", [128, ND * TT])

    def d_p3w(e):
        for k in range(NIC):
            em.dma(e, seg(wqdT, k, D), wqdT_d[k * CH:(k + 1) * CH, :])
        for c in range(ND):
            em.dma(e, col(swdc, c), swd_d[c * CH:(c + 1) * CH, :])
    em.step("D", d_p3w)

    def v_scold(e):
        for c in range(ND):
            e.tensor_tensor(col(scold, c), col(swdc, c), shcol[:], ALU.mult)
    em.step("V", v_scold)

    for it in range(NTILES):
        t0 = it * TT

        def d_p3l(e, t0=t0):
            for c in range(NIC):
                em.dma(e, seg(hmt3, c, TT), hmid_d[c * CH:(c + 1) * CH, t0:t0 + TT])
            for c in range(ND):
                em.dma(e, seg(xt3, c, TT), x_fm[c * CH:(c + 1) * CH, t0:t0 + TT])
                em.dma(e, seg(st3, c, TT), s0_d[c * CH:(c + 1) * CH, t0:t0 + TT])
        em.step("D", d_p3l)

        def v_q3(e):
            for g in range(6):
                c0, c1 = 4 * g, min(4 * g + 4, NIC)
                W = (c1 - c0) * TT
                qv = q3v[:, 0:W]
                qi = q3i[:, 0:W]
                qf = q3t[:, 0:W]
                qn = q3n[:, 0:W]
                hs = hmt3[:, c0 * TT:c1 * TT]
                e.tensor_scalar(qv, hs, invq2[:], 0.5, ALU.mult, ALU.add)
                e.tensor_scalar(qi, qv, 1.0, 0.0, ALU.mult, ALU.add)
                e.tensor_scalar(qf, qi, 1.0, 0.0, ALU.mult, ALU.add)
                e.tensor_tensor(qn, qf, qv, ALU.is_gt)
                e.tensor_tensor(qf, qf, qn, ALU.subtract)
                e.tensor_scalar(khb[:, c0 * TT:c1 * TT], qf, 127.0, -128.0,
                                ALU.min, ALU.max)
        em.step("V", v_q3)

        for g in range(2):
            def t_dn(e, g=g):
                for mm in range(4):
                    m = g * 4 + mm
                    for k in range(NIC):
                        e.matmul(pb[4 + mm][:, 0:TT],
                                 seg(wqdT, k, D)[:, m * CH:(m + 1) * CH],
                                 seg(khb, k, TT), start=(k == 0), stop=(k == NIC - 1))
            em.step("T", t_dn)

            def s_dn(e, g=g):
                for mm in range(4):
                    m = g * 4 + mm
                    e.activation(seg(gev, m, TT), pb[4 + mm][:, 0:TT], AF.Silu,
                                 bias=0.0, scale=col(scold, m))
            em.step("S", s_dn)

        def v_x1(e):
            for c in range(ND):
                e.tensor_tensor(m3[:], seg(st3, c, TT), seg(gev, c, TT), ALU.mult)
                e.tensor_tensor(seg(x1t, c, TT), seg(xt3, c, TT), m3[:], ALU.add)
        em.step("V", v_x1)

        emit_ln(x1t, xn1t, sq3)

        def d_p3s(e, t0=t0):
            for c in range(ND):
                em.dma(e, x1_d[c * CH:(c + 1) * CH, t0:t0 + TT], seg(x1t, c, TT))
                em.dma(e, xn1_d[c * CH:(c + 1) * CH, t0:t0 + TT], seg(xn1t, c, TT))
                if "x1" in dbg:
                    em.dma(e, dbg["x1"][c * CH:(c + 1) * CH, t0:t0 + TT],
                           seg(x1t, c, TT))
        em.step("D", d_p3s)

    p3.close()

    em.new_block()
    NTCH = TT // CH
    p4 = contextlib.ExitStack()

    def p4sb(name, shape, dt=F32):
        return p4.enter_context(nc.sbuf_tensor(name, shape, dt))

    wq4gT = p4sb("wq4gT", [128, ND * I0], BF16)
    wq4uT = p4sb("wq4uT", [128, ND * I0], BF16)
    xn4 = p4sb("xn4", [128, ND * TT])
    xn4tok = p4sb("xn4tok", [128, NTCH * D])
    xq4fm = p4sb("xq4fm", [128, ND * TT], BF16)
    f4sc = p4sb("f4sc", [128, I0 // 16])
    f4rc = p4sb("f4rc", [128, I0 // 16])
    f4i = p4sb("f4i", [128, I0], I32)
    f4t1 = p4sb("f4t1", [128, I0])
    f4t2 = p4sb("f4t2", [128, I0])
    f4z = p4sb("f4z", [128, I0])
    h1tok = p4sb("h1tok", [128, I0])
    h1sil = p4sb("h1sil", [128, 512])
    kh1tok = p4sb("kh1tok", [128, I0], BF16)

    def d_p4w(e):
        for k in range(ND):
            em.dma(e, seg(wq4gT, k, I0), wq4gT_d[k * CH:(k + 1) * CH, :])
            em.dma(e, seg(wq4uT, k, I0), wq4uT_d[k * CH:(k + 1) * CH, :])
    em.step("D", d_p4w)

    for it in range(NTILES):
        t0 = it * TT

        def d_xn4(e, t0=t0):
            for c in range(ND):
                em.dma(e, seg(xn4, c, TT), xn1_d[c * CH:(c + 1) * CH, t0:t0 + TT])
        em.step("D", d_xn4)

        for tc in range(NTCH):
            for c0 in range(0, ND, 4):
                def t_tr(e, tc=tc, c0=c0):
                    for c in range(c0, min(c0 + 4, ND)):
                        e.transpose(pb[c % 4][0:CH, 0:CH],
                                    seg(xn4, c, TT)[:, tc * CH:(tc + 1) * CH],
                                    sident[:])
                    e.matmul(pb[c0 % 4][0:1, 400:401], sident[0:1, 0:1],
                             sident[0:1, 0:1], start=True, stop=True)
                em.step("T", t_tr)

                def s_tr(e, tc=tc, c0=c0):
                    for c in range(c0, min(c0 + 4, ND)):
                        e.copy(xn4tok[:, tc * D + c * CH: tc * D + (c + 1) * CH],
                               pb[c % 4][0:CH, 0:CH])
                em.step("S", s_tr)

        def v_q4x(e):
            fp4_core(e, xn4tok, 128, NTCH * D, f4sc, f4rc, f4i, f4t1, f4t2,
                     f4z, f4z)
        em.step("V", v_q4x)

        for tc in range(NTCH):
            for c0 in range(0, ND, 4):
                def t_tb(e, tc=tc, c0=c0):
                    for c in range(c0, min(c0 + 4, ND)):
                        e.transpose(pb[c % 4][0:CH, 0:CH],
                                    f4z[:, tc * D + c * CH: tc * D + (c + 1) * CH],
                                    sident[:])
                    e.matmul(pb[c0 % 4][0:1, 400:401], sident[0:1, 0:1],
                             sident[0:1, 0:1], start=True, stop=True)
                em.step("T", t_tb)

                def s_tb(e, tc=tc, c0=c0):
                    for c in range(c0, min(c0 + 4, ND)):
                        e.copy(seg(xq4fm, c, TT)[:, tc * CH:(tc + 1) * CH],
                               pb[c % 4][0:CH, 0:CH])
                em.step("S", s_tb)

        for tc in range(NTCH):
            for i0 in range(0, I0, 512):
                iw = min(512, I0 - i0)

                def t_g4(e, tc=tc, i0=i0, iw=iw):
                    for k in range(ND):
                        e.matmul(pb[4][:, 0:iw],
                                 seg(xq4fm, k, TT)[:, tc * CH:(tc + 1) * CH],
                                 seg(wq4gT, k, I0)[:, i0:i0 + iw],
                                 start=(k == 0), stop=(k == ND - 1))
                    for k in range(ND):
                        e.matmul(pb[5][:, 0:iw],
                                 seg(xq4fm, k, TT)[:, tc * CH:(tc + 1) * CH],
                                 seg(wq4uT, k, I0)[:, i0:i0 + iw],
                                 start=(k == 0), stop=(k == ND - 1))
                em.step("T", t_g4)
                em.step("S", lambda e, i0=i0, iw=iw: e.activation(
                    h1sil[:, 0:iw], pb[4][:, 0:iw], AF.Silu))
                em.step("V", lambda e, i0=i0, iw=iw: e.tensor_tensor(
                    h1tok[:, i0:i0 + iw], h1sil[:, 0:iw], pb[5][:, 0:iw],
                    ALU.mult))

            def v_q4h(e, tc=tc):
                fp4_core(e, h1tok, 128, I0, f4sc, f4rc, f4i, f4t1, f4t2, f4z,
                         kh1tok)
            em.step("V", v_q4h)

            def d_kh(e, tc=tc, t0=t0):
                em.dma(e, kh1_d[t0 + tc * CH:t0 + (tc + 1) * CH, :], kh1tok[:])
            em.step("D", d_kh)

    p4.close()

    em.new_block()
    p42 = contextlib.ExitStack()

    def p42sb(name, shape, dt=F32):
        return p42.enter_context(nc.sbuf_tensor(name, shape, dt))

    wq4dT = p42sb("wq4dT", [128, NIC * D], BF16)
    kh1ld = p42sb("kh1ld", [128, NTCH * I0], BF16)
    kh1f32 = p42sb("kh1f32", [128, I0])
    kh1fm = p42sb("kh1fm", [128, NIC * TT], BF16)
    g1t = p42sb("g1t", [128, ND * TT], BF16)

    def d_p42w(e):
        for k in range(NIC):
            em.dma(e, seg(wq4dT, k, D), wq4dT_d[k * CH:(k + 1) * CH, :])
    em.step("D", d_p42w)

    for it in range(NTILES):
        t0 = it * TT

        def d_kh1(e, t0=t0):
            for tc in range(NTCH):
                em.dma(e, seg(kh1ld, tc, I0), kh1_d[t0 + tc * CH:t0 + (tc + 1) * CH, :])
        em.step("D", d_kh1)

        for tc in range(NTCH):
            em.step("S", lambda e, tc=tc: e.copy(kh1f32[:], seg(kh1ld, tc, I0)))
            for c0 in range(0, NIC, 4):
                def t_tk(e, tc=tc, c0=c0):
                    for c in range(c0, min(c0 + 4, NIC)):
                        e.transpose(pb[c % 4][0:CH, 0:CH],
                                    kh1f32[:, c * CH:(c + 1) * CH],
                                    sident[:])
                    e.matmul(pb[c0 % 4][0:1, 400:401], sident[0:1, 0:1],
                             sident[0:1, 0:1], start=True, stop=True)
                em.step("T", t_tk)

                def s_tk(e, tc=tc, c0=c0):
                    for c in range(c0, min(c0 + 4, NIC)):
                        e.copy(seg(kh1fm, c, TT)[:, tc * CH:(tc + 1) * CH],
                               pb[c % 4][0:CH, 0:CH])
                em.step("S", s_tk)

        for g in range(2):
            def t_dn1(e, g=g):
                for mm in range(4):
                    m = g * 4 + mm
                    for k in range(NIC):
                        e.matmul(pb[4 + mm][:, 0:TT],
                                 seg(wq4dT, k, D)[:, m * CH:(m + 1) * CH],
                                 seg(kh1fm, k, TT), start=(k == 0),
                                 stop=(k == NIC - 1))
            em.step("T", t_dn1)

            def s_dn1(e, g=g):
                for mm in range(4):
                    m = g * 4 + mm
                    e.activation(seg(g1t, m, TT), pb[4 + mm][:, 0:TT], AF.Silu)
            em.step("S", s_dn1)

        def d_g1(e, t0=t0):
            for c in range(ND):
                em.dma(e, g1_d[c * CH:(c + 1) * CH, t0:t0 + TT], seg(g1t, c, TT))
                if "g1" in dbg:
                    em.dma(e, dbg["g1"][c * CH:(c + 1) * CH, t0:t0 + TT],
                           seg(g1t, c, TT))
        em.step("D", d_g1)

    p42.close()

    em.new_block()
    p5 = contextlib.ExitStack()

    def p5sb(name, shape, dt=F32):
        return p5.enter_context(nc.sbuf_tensor(name, shape, dt))

    bw1_in = p5sb("bw1_in", [128, ND * ZC], BF16)
    bw1_out = p5sb("bw1_out", [128, ND * D], BF16)
    x1l = p5sb("x1l", [128, ND * TT])
    xn1l = p5sb("xn1l", [128, ND * TT])
    xn1b = p5sb("xn1b", [128, ND * TT], BF16)
    zrawb = p5sb("zrawb", [128, NZC * TT], BF16)
    ztail = p5sb("ztail", [128, NZC * 3], BF16)
    cz = p5sb("cz", [128, TT])
    xval = p5sb("xval", [128, ND * TT])
    xg1 = p5sb("xg1", [128, ND * TT], BF16)
    czs = p5sb("czs", [32, TT])
    pwt = p5sb("pwt", [32, TT])
    post = p5sb("post", [32, TT])
    logw = p5sb("logw", [32, TT])
    phit = p5sb("phit", [128, TT])
    aqt = p5sb("aqt", [128, TT])
    kqt = p5sb("kqt", [128, TT], I32)
    sgt = p5sb("sgt", [128, TT])
    prt = p5sb("prt", [128, TT])
    sint = p5sb("sint", [128, TT])
    cost = p5sb("cost", [128, TT])
    nret = p5sb("nret", [128, ND * TT])
    nimt = p5sb("nimt", [128, ND * TT])
    csre = p5sb("csre", [128, ND * TT])
    csim = p5sb("csim", [128, ND * TT])
    csden = p5sb("csden", [32, TT])
    cden = p5sb("cden", [32, 1])
    cre = p5sb("cre", [128, ND])
    cim = p5sb("cim", [128, ND])
    invden = p5sb("invden", [32, TT])
    rst = p5sb("rst", [32, TT])
    renb = p5sb("renb", [128, ND * TT], BF16)
    imnb = p5sb("imnb", [128, ND * TT], BF16)
    ygb = p5sb("ygb", [128, ND * TT], BF16)
    g1l = p5sb("g1l", [128, ND * TT], BF16)
    outt = p5sb("outt", [128, ND * TT])
    onesf = p5sb("onesf", [128, TT])
    wst5 = p5sb("wst5", [128, ZC])

    for k in range(ND):
        em.step("D", lambda e, k=k: em.dma(e, wst5[0:CH, 0:ZC],
                                           w1_in[k * CH:(k + 1) * CH, :]))
        em.step("S", lambda e, k=k: e.copy(seg(bw1_in, k, ZC), wst5[0:CH, 0:ZC]))
    for k in range(ND):
        em.step("D", lambda e, k=k: em.dma(e, wst5[0:CH, 0:D],
                                           w1_out[k * CH:(k + 1) * CH, :]))
        em.step("S", lambda e, k=k: e.copy(seg(bw1_out, k, D), wst5[0:CH, 0:D]))
    def v_p5init(e):
        e.memset(onesf[:], 1.0)
        e.memset(cden[:], 0.0)
        e.memset(cre[:], 0.0)
        e.memset(cim[:], 0.0)
    em.step("V", v_p5init)

    for it in range(NTILES):
        t0 = it * TT
        seq_start = (it % TPS == 0)
        tloc = (it % TPS) * TT

        def d_p5l(e, t0=t0):
            for c in range(ND):
                em.dma(e, seg(x1l, c, TT), x1_d[c * CH:(c + 1) * CH, t0:t0 + TT])
                em.dma(e, seg(xn1l, c, TT), xn1_d[c * CH:(c + 1) * CH, t0:t0 + TT])
                em.dma(e, seg(g1l, c, TT), g1_d[c * CH:(c + 1) * CH, t0:t0 + TT])
        em.step("D", d_p5l)

        def s_cast5(e):
            for c in range(ND):
                e.copy(seg(xn1b, c, TT), seg(xn1l, c, TT))
        em.step("S", s_cast5)

        for g in range(5):
            def t_z(e, g=g):
                for mm in range(4):
                    m = g * 4 + mm
                    if m >= NZC:
                        continue
                    mw = 32 if m == NZC - 1 else CH
                    for k in range(ND):
                        e.matmul(pb[4 + mm][0:mw, 0:TT],
                                 seg(bw1_in, k, ZC)[:, m * CH:m * CH + mw],
                                 seg(xn1b, k, TT), start=(k == 0), stop=(k == ND - 1))
            em.step("T", t_z)

            def s_z(e, g=g):
                for mm in range(4):
                    m = g * 4 + mm
                    if m >= NZC:
                        continue
                    mw = 32 if m == NZC - 1 else CH
                    e.copy(seg(zrawb, m, TT)[0:mw, :], pb[4 + mm][0:mw, 0:TT])
            em.step("S", s_z)

        def v_conv1(e, seq_start=seq_start):
            for c in range(NZC):
                mw = 32 if c == NZC - 1 else CH
                tl = seg(ztail, c, 3)[0:mw, :]
                xb_ = seg(zrawb, c, TT)[0:mw, :]
                acc = cz[0:mw, :]
                if seq_start:
                    e.tensor_scalar(acc[:, 0:3], xb_[:, 0:3], 0.0, 0.0,
                                    ALU.mult, ALU.add)
                else:
                    e.tensor_scalar(acc[:, 0:3], tl,
                                    col(convw1, c * 4 + 0)[0:mw, :],
                                    0.0, ALU.mult, ALU.add)
                e.tensor_scalar(acc[:, 3:TT], xb_[:, 0:TT - 3],
                                col(convw1, c * 4 + 0)[0:mw, :], 0.0,
                                ALU.mult, ALU.add)
                for j in (1, 2, 3):
                    sh = 3 - j
                    if sh > 0 and not seq_start:
                        e.scalar_tensor_tensor(acc[:, 0:sh], tl[:, j:3],
                                               col(convw1, c * 4 + j)[0:mw, :],
                                               acc[:, 0:sh], ALU.mult, ALU.add)
                    e.scalar_tensor_tensor(acc[:, sh:TT], xb_[:, 0:TT - sh],
                                           col(convw1, c * 4 + j)[0:mw, :],
                                           acc[:, sh:TT], ALU.mult, ALU.add)
                e.tensor_copy(seg(ztail, c, 3)[0:mw, :], xb_[:, TT - 3:TT])
                if c < ND:
                    e.tensor_copy(seg(xval, c, TT), acc)
                elif c < 2 * ND:
                    e.tensor_copy(seg(csre, c - ND, TT), acc)
                else:
                    e.tensor_copy(czs[0:32, :], acc[0:32, :])
        em.step("V", v_conv1)

        def s_xg(e):
            for c in range(ND):
                e.activation(seg(xg1, c, TT), seg(csre, c, TT), AF.Silu)
        em.step("S", s_xg)

        em.step("G", lambda e, tloc=tloc: e.iota(
            post[:], [[1, TT]], base=tloc, channel_multiplier=0,
            allow_small_or_imprecise_dtypes=True))

        def v_pw(e):
            e.tensor_scalar(logw[:], post[:], walpha[:], wbeta[:],
                            ALU.mult, ALU.add)
            e.scalar_tensor_tensor(logw[:], czs[0:32, :], sco1[:], logw[:],
                                   ALU.mult, ALU.add)
        em.step("V", v_pw)
        em.step("S", lambda e: e.activation(pwt[:], logw[:], AF.Exp))
        if "pw" in dbg:
            em.step("D", lambda e, t0=t0: em.dma(
                e, dbg["pw"][:, t0:t0 + TT], pwt[:]))
        if "xv" in dbg:
            def d_xv(e, t0=t0):
                for c in range(ND):
                    em.dma(e, dbg["xv"][c * CH:(c + 1) * CH, t0:t0 + TT],
                           seg(xval, c, TT))
            em.step("D", d_xv)

        for c in range(ND):
            def s_tg1(e, c=c):
                e.activation(aqt[:], seg(xval, c, TT), AF.Abs, bias=0.0,
                             scale=col(th2pi, c))
                e.activation(sgt[:], seg(xval, c, TT), AF.Sign,
                             bias=0.0, scale=col(thcol, c))
                e.activation(phit[:], seg(xval, c, TT), AF.Identity, bias=0.0,
                             scale=col(thcol, c))
            em.step("S", s_tg1)

            def v_tg(e, c=c):
                e.tensor_scalar(aqt[:], aqt[:], 1.0, 0.5, ALU.mult, ALU.add)
                e.tensor_scalar(kqt[:], aqt[:], 1.0, 0.0, ALU.mult, ALU.add)
                e.tensor_scalar(sint[:], kqt[:], 1.0, 0.0, ALU.mult, ALU.add)
                e.tensor_tensor(cost[:], sint[:], aqt[:], ALU.is_gt)
                e.tensor_tensor(sint[:], sint[:], cost[:], ALU.subtract)
                e.tensor_tensor(prt[:], sint[:], sgt[:], ALU.mult)
                e.scalar_tensor_tensor(prt[:], prt[:], -2.0 * PI, phit[:],
                                       ALU.mult, ALU.add)
                e.tensor_scalar(prt[:], prt[:], PI, -PI, ALU.min, ALU.max)
            em.step("V", v_tg)

            def s_tg2(e, c=c):
                e.activation(sint[:], prt[:], AF.Sin)
                e.activation(aqt[:], prt[:], AF.Abs)
            em.step("S", s_tg2)
            em.step("S", lambda e, c=c: e.activation(
                cost[:], aqt[:], AF.Sin, bias=PI / 2, scale=-1.0))

            em.step("T", lambda e, c=c: e.matmul(
                pb[0][:, 0:TT], selinv[:, c * CH:(c + 1) * CH], pwt[:],
                start=True, stop=True))

            def v_num(e, c=c):
                e.tensor_tensor(seg(nret, c, TT), cost[:], pb[0][:, 0:TT], ALU.mult)
                e.tensor_tensor(seg(nimt, c, TT), sint[:], pb[0][:, 0:TT], ALU.mult)
            em.step("V", v_num)

        def v_cs(e, it=it):
            e.tensor_tensor_scan(csden[:], onesf[0:32, :], pwt[:], cden[:],
                                 ALU.mult, ALU.add)
            for c in range(ND):
                e.tensor_tensor_scan(seg(csre, c, TT), onesf[:], seg(nret, c, TT),
                                     col(cre, c), ALU.mult, ALU.add)
                e.tensor_tensor_scan(seg(csim, c, TT), onesf[:], seg(nimt, c, TT),
                                     col(cim, c), ALU.mult, ALU.add)
                if c == 0:
                    e.tensor_copy(cden[:], csden[:, TT - 1:TT])
                else:
                    e.tensor_copy(col(cre, c - 1),
                                  seg(csre, c - 1, TT)[:, TT - 1:TT])
                    e.tensor_copy(col(cim, c - 1),
                                  seg(csim, c - 1, TT)[:, TT - 1:TT])
            e.tensor_scalar_max(invden[:], csden[:], 1e-4)
            e.tensor_copy(col(cre, ND - 1), seg(csre, ND - 1, TT)[:, TT - 1:TT])
            e.tensor_copy(col(cim, ND - 1), seg(csim, ND - 1, TT)[:, TT - 1:TT])
            e.reciprocal(invden[:], invden[:])
            if (it + 1) % TPS == 0 and it + 1 < NTILES:
                e.memset(cden[:], 0.0)
                e.memset(cre[:], 0.0)
                e.memset(cim[:], 0.0)
        em.step("V", v_cs)
        if "idn" in dbg:
            em.step("D", lambda e, t0=t0: em.dma(
                e, dbg["idn"][:, t0:t0 + TT], invden[:]))

        def t_idb(e):
            for c in range(ND):
                bank = pb[4 + c // 2]
                off = (c % 2) * TT
                e.matmul(bank[:, off:off + TT], selinv[:, c * CH:(c + 1) * CH],
                         invden[:], start=True, stop=True)
        em.step("T", t_idb)

        def v_rei(e):
            for c in range(ND):
                bank = pb[4 + c // 2]
                off = (c % 2) * TT
                e.tensor_tensor(seg(nret, c, TT), seg(csre, c, TT),
                                bank[:, off:off + TT], ALU.mult)
                e.tensor_tensor(seg(nimt, c, TT), seg(csim, c, TT),
                                bank[:, off:off + TT], ALU.mult)
        em.step("V", v_rei)

        def s_sq5(e):
            for c in range(ND):
                e.activation(seg(csre, c, TT), seg(nret, c, TT), AF.Square)
                e.activation(seg(csim, c, TT), seg(nimt, c, TT), AF.Square)
        em.step("S", s_sq5)

        def t_ms(e):
            for c in range(ND):
                e.matmul(pb[0][0:32, 0:TT], seg(selsq, c, 32), seg(csre, c, TT),
                         start=(c == 0), stop=False)
            for c in range(ND):
                e.matmul(pb[0][0:32, 0:TT], seg(selsq, c, 32), seg(csim, c, TT),
                         start=False, stop=(c == ND - 1))
        em.step("T", t_ms)
        em.step("S", lambda e: e.activation(rst[:], pb[0][0:32, 0:TT], AF.Sqrt,
                                            bias=1e-5, scale=1.0 / 64))
        em.step("V", lambda e: e.reciprocal(csden[0:32, :], rst[:]))

        def t_rsb(e):
            for c in range(ND):
                bank = pb[4 + c // 2]
                off = (c % 2) * TT
                e.matmul(bank[:, off:off + TT], selinv[:, c * CH:(c + 1) * CH],
                         csden[0:32, :], start=True, stop=True)
        em.step("T", t_rsb)

        def v_ren(e):
            for c in range(ND):
                bank = pb[4 + c // 2]
                off = (c % 2) * TT
                e.scalar_tensor_tensor(seg(renb, c, TT), seg(nret, c, TT),
                                       col(nsre_c, c), bank[:, off:off + TT],
                                       ALU.mult, ALU.mult)
                e.scalar_tensor_tensor(seg(imnb, c, TT), seg(nimt, c, TT),
                                       col(nsim_c, c), bank[:, off:off + TT],
                                       ALU.mult, ALU.mult)
        em.step("V", v_ren)

        for g in range(2):
            def t_y(e, g=g):
                for mm in range(4):
                    c = g * 4 + mm
                    e.matmul(pb[4 + mm][:, 0:TT], bdiag_re[:], seg(renb, c, TT),
                             start=True, stop=False)
                    e.matmul(pb[4 + mm][:, 0:TT], bdiag_im[:], seg(imnb, c, TT),
                             start=False, stop=True)
            em.step("T", t_y)

            def v_yg(e, g=g):
                for mm in range(4):
                    c = g * 4 + mm
                    e.tensor_tensor(seg(ygb, c, TT), pb[4 + mm][:, 0:TT],
                                    seg(xg1, c, TT), ALU.mult)
            em.step("V", v_yg)
            if "yg" in dbg:
                def d_ygt(e, g=g, t0=t0):
                    for mm in range(4):
                        c = g * 4 + mm
                        em.dma(e, dbg["yg"][c * CH:(c + 1) * CH, t0:t0 + TT],
                               seg(ygb, c, TT))
                em.step("D", d_ygt)

        for g in range(2):
            def t_o(e, g=g):
                for mm in range(4):
                    m = g * 4 + mm
                    for k in range(ND):
                        e.matmul(pb[4 + mm][:, 0:TT],
                                 seg(bw1_out, k, D)[:, m * CH:(m + 1) * CH],
                                 seg(ygb, k, TT), start=(k == 0), stop=(k == ND - 1))
            em.step("T", t_o)

            def v_o(e, g=g):
                for mm in range(4):
                    m = g * 4 + mm
                    e.tensor_tensor(cz[:], pb[4 + mm][:, 0:TT], seg(g1l, m, TT),
                                    ALU.mult)
                    e.tensor_tensor(seg(outt, m, TT), seg(x1l, m, TT), cz[:],
                                    ALU.add)
            em.step("V", v_o)

        def d_out(e, t0=t0):
            for c in range(ND):
                em.dma(e, out_fm[c * CH:(c + 1) * CH, t0:t0 + TT], seg(outt, c, TT))
        em.step("D", d_out)

    p5.close()

    em.new_block()
    em.step("D", lambda e: e.wait_ge(em.progD.sem, em.progD.n))
    em.close()
    gs.close()
    return nc


def _selrow_host():
    a = np.zeros((32, 32 * 128), dtype=np.float32)
    for j in range(32):
        a[j, j * 128:(j + 1) * 128] = 1.0
    return a


def _selinv_host():
    a = np.zeros((32, 8 * 128), dtype=np.float32)
    for ci in range(8):
        for p in range(128):
            a[4 * ci + (p >> 5), ci * 128 + p] = 1.0
    return a


def _selsq_host():
    a = np.zeros((128, 8 * 32), dtype=np.float32)
    for ci in range(8):
        for c in range(128):
            a[c, ci * 32 + 4 * ci + (c >> 5)] = 1.0
    return a


def _chunk_cols(v, nch):
    v = np.asarray(v)
    if v.ndim == 1:
        v = v[:, None]
    k = v.shape[1]
    return np.ascontiguousarray(
        v.reshape(nch, 128, k).transpose(1, 0, 2).reshape(128, nch * k))


def prep_inputs(core, A):
    import ml_dtypes
    x = np.asarray(A["L0_attn_x"])
    xc = x[2 * core:2 * core + 2]
    d = {
        "x_fm": np.ascontiguousarray(xc.transpose(2, 0, 1).reshape(D, SEQ * T)),
        "w0_inT": np.ascontiguousarray(np.asarray(A["L0_attn_W_in"]).T),
        "conv_w0": _chunk_cols(A["L0_attn_conv_w"], NDI),
        "conv_b0": _chunk_cols(A["L0_attn_conv_b"], NDI),
        "w0_xprojT": np.ascontiguousarray(np.asarray(A["L0_attn_W_xproj"]).T),
        "w0_dtT": np.ascontiguousarray(np.asarray(A["L0_attn_W_dt"]).T),
        "b_dt0": _chunk_cols(A["L0_attn_b_dt"], NDI),
        "A_log0": _chunk_cols(A["L0_attn_A_log"], NDI),
        "D_skip0": _chunk_cols(A["L0_attn_D_skip"], NDI),
        "w0_outT": np.ascontiguousarray(np.asarray(A["L0_attn_W_out"]).T),
        "clip_x": np.asarray(A["L0_ffn_clip_x"]).reshape(1, 1),
        "clip_down": np.asarray(A["L0_ffn_clip_down"]).reshape(1, 1),
        "wg0": np.ascontiguousarray(A["L0_ffn_w_gate"]),
        "wu0": np.ascontiguousarray(A["L0_ffn_w_up"]),
        "wd0": np.ascontiguousarray(A["L0_ffn_w_down"]),
        "w1_in": np.ascontiguousarray(A["L1_attn_W_in"]),
        "conv_w1": _chunk_cols(
            np.concatenate([np.asarray(A["L1_attn_conv_kernel"]).reshape(4, ZC).T,
                            np.zeros((NZC * 128 - ZC, 4), np.float32)], axis=0), NZC),
        "theta1": _chunk_cols(np.asarray(A["L1_attn_theta"]).reshape(D), ND),
        "slopes1": np.concatenate([np.asarray(A["L1_attn_decay_slopes"]),
                                   np.asarray(A["L1_attn_anchor_slopes"])]
                                  ).reshape(32, 1).astype(np.float32),
        "score1": np.asarray(A["L1_attn_score_scale"]).reshape(32, 1),
        "amask": np.array([1.0] * 28 + [-1.0] * 4, np.float32).reshape(32, 1),
        "dmask": np.array([-float(T - 1)] * 28 + [0.0] * 4,
                          np.float32).reshape(32, 1),
        "w_re": np.ascontiguousarray(A["L1_attn_W_re"]),
        "w_im": np.ascontiguousarray(A["L1_attn_W_im"]),
        "ns_re": _chunk_cols(np.tile(np.asarray(A["L1_attn_norm_scale"])[:32], 32), ND),
        "ns_im": _chunk_cols(np.tile(np.asarray(A["L1_attn_norm_scale"])[32:], 32), ND),
        "w1_out": np.ascontiguousarray(A["L1_attn_W_out"]),
        "wg1": np.ascontiguousarray(A["L1_ffn_w_gate"]),
        "wu1": np.ascontiguousarray(A["L1_ffn_w_up"]),
        "wd1": np.ascontiguousarray(A["L1_ffn_w_down"]),
        "ident": np.eye(128, dtype=np.float32),
        "selrow_b": _selrow_host().astype(ml_dtypes.bfloat16),
        "selinv": _selinv_host(),
        "selsq": _selsq_host(),
        "ones1f": np.ones((1, 128), dtype=np.float32),
        "consts": np.tile(np.array(CONST_VALS + [0.0] * (16 - len(CONST_VALS)),
                                   np.float32), (128, 1)),
    }
    for k in d:
        if d[k].dtype == np.float64:
            d[k] = d[k].astype(np.float32)
    return d


def kernel(**inputs):
    nc = build_program()
    in_maps = [prep_inputs(c, inputs) for c in range(NCORES)]
    res = run_bass_kernel_spmd(nc, in_maps, core_ids=list(range(NCORES)))
    out = np.empty((16, T, D), dtype=np.float32)
    for c in range(NCORES):
        o = np.asarray(res.results[c]["out_fm"])
        out[2 * c:2 * c + 2] = o.reshape(D, SEQ, T).transpose(1, 2, 0)
    return out


if __name__ == "__main__":
    nc = build_program()
    print("built; instructions:", sum(1 for _ in nc.all_instructions()))


# version 57 (final)
# speedup vs baseline: 1.0351x; 1.0351x over previous
import contextlib
import numpy as np
import concourse.bass as bass
import concourse.mybir as mybir
from concourse.bass_utils import run_bass_kernel_spmd

F32 = mybir.dt.float32
BF16 = mybir.dt.bfloat16
I32 = mybir.dt.int32
AF = mybir.ActivationFunctionType
ALU = mybir.AluOpType
AX = mybir.AxisListType

NCORES = 8
SEQ = 2
T = 2048
TT = 256
TPS = T // TT
NTILES = SEQ * TPS
D = 1024
DI = 2048
NST = 16
DTR = 64
I0 = 2816
ZC = 2080
CH = 128
ND = D // CH
NDI = DI // CH
NIC = I0 // CH
NZC = 17
PI = float(np.pi)

ENG = {"T": "tensor", "V": "vector", "S": "scalar", "G": "gpsimd", "D": "sync"}
CONST_VALS = [0.0, 1.0, -1.0, 1e-5, 2.0, PI / 2, 1.0 / 64, 0.5, 127.0, PI, -PI,
              256.5, 1e-4, 1e-12]


def cdiv(a, b):
    return (a + b - 1) // b


class Prog:
    def __init__(self, nc, name, step=1):
        self.sem = nc.semaphore(name).__enter__()
        self.n = 0
        self.step = step

    def tick(self, instr, k=None):
        k = self.step if k is None else k
        instr.then_inc(self.sem, k)
        self.n += k
        return instr


class Rec:

    def __init__(self):
        self.calls = []

    def __getattr__(self, name):
        calls = self.calls

        def call(*a, **k):
            entry = [name, a, k, []]
            calls.append(entry)

            class H:
                def then_inc(self, sem, n):
                    entry[3].append((sem, n))
                    return self
            return H()
        return call

    def replay(self, e):
        for name, a, k, chain in self.calls:
            h = getattr(e, name)(*a, **k)
            for sem, n in chain:
                h = h.then_inc(sem, n)


class EM:

    MAXSTEPS = 60
    BATCH_LO = 1371
    BATCH_HI = 10**9

    def __init__(self, nc):
        self.nc = nc
        self.prog = {e: Prog(nc, "prog_" + e) for e in "TVSG"}
        self.progD = Prog(nc, "prog_D", 16)
        self.progW = Prog(nc, "prog_W", 16)
        self.progC = Prog(nc, "prog_C", 1)
        self.last_wait = {e: {} for e in "TVSGD"}
        self.pend = {e: [] for e in "TVSGD"}
        self.npend = 0
        self.afset = set()
        self.ndma = 0
        self.wmap = {}
        self.rmap = {}

    def flush(self):
        if self.npend == 0:
            return
        with self.nc.Block() as b:
            for eng in "DTVSG":
                recs = self.pend[eng]
                if not recs:
                    continue
                dec = getattr(b, ENG[eng])

                @dec
                def _(e, recs=recs):
                    for r in recs:
                        r.replay(e)
        self.pend = {e: [] for e in "TVSGD"}
        self.npend = 0
        self.afset = set()
        self.ndma = 0

    new_block = flush
    close = flush

    @staticmethod
    def _afs(rec):
        s = set()
        for name, a, k, chain in rec.calls:
            if name == "activation":
                s.add((str(a[2]), str(a[0].dtype), str(a[1].dtype)))
            elif name == "copy":
                s.add(("COPY", str(a[0].dtype), str(a[1].dtype)))
        return s

    def marks(self):
        m = {e: self.prog[e].n for e in "TVSG"}
        m["D"] = self.progD.n
        m["C"] = self.progC.n
        return m

    @staticmethod
    def _rw(rec):
        rd, wr = set(), set()

        def isap(x):
            return hasattr(x, "space") and hasattr(x, "dtype")

        for name, a, k, chain in rec.calls:
            if name in ("wait_ge", "sem_inc"):
                continue
            if name == "dma_start":
                out = k.get("out", a[0] if a else None)
                in_ = k.get("in_", a[1] if len(a) > 1 else None)
                if isap(out):
                    wr.add(out.name)
                if isap(in_):
                    rd.add(in_.name)
                continue
            if a and isap(a[0]):
                wr.add(a[0].name)
            for x in a[1:]:
                if isap(x):
                    rd.add(x.name)
            for x in k.values():
                if isap(x):
                    rd.add(x.name)
        return rd, wr

    def _autodeps(self, rd, wr, eng):
        deps = {}

        def need(e2, cnt):
            if cnt <= 0 or (e2 == eng and e2 != "D"):
                return
            if deps.get(e2, 0) < cnt:
                deps[e2] = cnt

        for nm in rd | wr:
            w = self.wmap.get(nm)
            if w:
                need(*w)
        for nm in wr:
            for e2, cnt in self.rmap.get(nm, {}).items():
                need(e2, cnt)
        return deps

    def _track(self, rd, wr, eng):
        cnt = self.progD.n if eng == "D" else self.prog[eng].n
        for nm in wr:
            self.wmap[nm] = (eng, cnt)
            self.rmap[nm] = {}
        for nm in rd:
            self.rmap.setdefault(nm, {})[eng] = cnt

    def step(self, eng, fn, deps=None):
        idx = getattr(self, "stepidx", 0) + 1
        inwin = self.BATCH_LO <= idx < self.BATCH_HI
        marks0 = self.marks()
        r = Rec()
        fn(r)
        rd, wr = self._rw(r)
        selfdep = False
        if deps is None:
            if inwin and eng != "G":
                deps = self._autodeps(rd, wr, eng)
                selfdep = True
            else:
                deps = marks0
        marks = dict(deps)
        w = Rec()
        dwait = False
        for d, cnt in marks.items():
            if cnt <= 0 or (d == eng and not (d == "D" and selfdep)):
                continue
            if self.last_wait[eng].get(d, -1) >= cnt:
                continue
            sem = (self.progD.sem if d == "D"
                   else self.progC.sem if d == "C" else self.prog[d].sem)
            w.wait_ge(sem, cnt)
            self.last_wait[eng][d] = cnt
            if d == "D":
                dwait = True
        if dwait and eng in ("T", "V", "S") and hasattr(self, "flushers"):
            fl = self.flushers.get(eng)
            if fl is not None:
                fl(w)
        r.calls = w.calls + r.calls
        if eng != "D":
            p = self.prog[eng]
            fl = self.flushers.get(eng) if hasattr(self, "flushers") else None
            if fl is not None:
                fl(r)
            tgt = None
            for entry in reversed(r.calls):
                if (entry[0] not in ("wait_ge", "sem_inc", "dma_start")
                        and not entry[3]):
                    tgt = entry
                    break
            if tgt is not None:
                tgt[3].append((p.sem, 1))
            else:
                r.sem_inc(p.sem, 1)
            p.n += 1
        self._track(rd, wr, eng)
        if eng == "S":
            af = self._afs(r)
            if len(self.afset | af) > 5:
                self.flush()
            self.afset |= af
        self.stepidx = getattr(self, "stepidx", 0) + 1
        if eng == "D":
            def big(c):
                if c[0] != "dma_start":
                    return False
                out = c[2].get("out")
                if str(out.space) != "MemorySpace.SBUF":
                    return False
                n = 1
                for s in out.shape[1:]:
                    n *= s
                return n * (2 if "bfloat16" in str(out.dtype) else 4) > 1024
            if any(big(c) for c in r.calls):
                self.flush()
                self.pend[eng].append(r)
                self.npend += 1
                self.flush()
                return
        self.pend[eng].append(r)
        self.npend += 1
        inwin = self.BATCH_LO <= self.stepidx < self.BATCH_HI
        if self.npend >= (self.MAXSTEPS if inwin else 1):
            self.flush()

    def dma(self, e, out, in_):
        self.progD.tick(e.dma_start(out=out, in_=in_))


def build_program(taps=()):
    nc = bass.Bass("TRN2", target_bir_lowering=False, debug=False)
    taps = set(taps)

    def din(name, shape, dt=F32):
        return nc.dram_tensor(name, shape, dt, kind="ExternalInput").ap()

    x_fm = din("x_fm", [D, SEQ * T])
    w0_inT = din("w0_inT", [D, 2 * DI])
    conv_w0 = din("conv_w0", [128, NDI * 4])
    conv_b0 = din("conv_b0", [128, NDI])
    w0_xprojT = din("w0_xprojT", [DI, 96])
    w0_dtT = din("w0_dtT", [DTR, DI])
    b_dt0 = din("b_dt0", [128, NDI])
    A_log0 = din("A_log0", [128, NDI * NST])
    D_skip0 = din("D_skip0", [128, NDI])
    w0_outT = din("w0_outT", [DI, D])
    clip_x = din("clip_x", [1, 1])
    clip_down = din("clip_down", [1, 1])
    wg0 = din("wg0", [I0, D])
    wu0 = din("wu0", [I0, D])
    wd0 = din("wd0", [D, I0])
    w1_in = din("w1_in", [D, ZC])
    conv_w1 = din("conv_w1", [128, NZC * 4])
    theta1 = din("theta1", [128, ND])
    slopes1 = din("slopes1", [32, 1])
    amask_i = din("amask", [32, 1])
    dmask_i = din("dmask", [32, 1])
    score1 = din("score1", [32, 1])
    w_re = din("w_re", [32, 32])
    w_im = din("w_im", [32, 32])
    ns_re = din("ns_re", [128, ND])
    ns_im = din("ns_im", [128, ND])
    w1_out = din("w1_out", [D, D])
    wg1 = din("wg1", [I0, D])
    wu1 = din("wu1", [I0, D])
    wd1 = din("wd1", [D, I0])
    ident_i = din("ident", [128, 128])
    selrow_b_i = din("selrow_b", [32, 32 * 128], BF16)
    selinv_i = din("selinv", [32, 8 * 128])
    selsq_i = din("selsq", [128, 8 * 32])
    ones1f_i = din("ones1f", [1, 128])
    consts_i = din("consts", [128, 16])

    out_fm = nc.dram_tensor("out_fm", [D, SEQ * T], F32, kind="ExternalOutput").ap()
    dbg = {}
    for tname, shape, dt in [("xn", [D, SEQ * T], F32), ("s0", [D, SEQ * T], BF16),
                             ("gmax", [1, 2], F32), ("hmid", [I0, SEQ * T], F32),
                             ("x1", [D, SEQ * T], F32), ("g1", [D, SEQ * T], BF16),
                             ("ym", [DI, SEQ * T], BF16),
                             ("kx", [D, SEQ * T], BF16), ("wqg", [D, I0], BF16),
                             ("swg", [I0, 1], F32), ("gt", [I0, SEQ * T], F32),
                             ("up", [I0, SEQ * T], F32),
                             ("xn2", [D, SEQ * T], F32), ("iq", [128, 2], F32),
                             ("wab", [32, 8], F32), ("pw", [32, SEQ * T], F32),
                             ("xv", [D, SEQ * T], F32), ("yg", [D, SEQ * T], BF16),
                             ("idn", [32, SEQ * T], F32)]:
        if tname in taps:
            dbg[tname] = nc.dram_tensor("dbg_" + tname, shape, dt,
                                        kind="ExternalOutput").ap()

    xn_d = nc.dram_tensor("xn_d", [D, SEQ * T], F32).ap()
    ym_d = nc.dram_tensor("ym_d", [DI, SEQ * T], BF16).ap()
    s0_d = nc.dram_tensor("s0_d", [D, SEQ * T], BF16).ap()
    hmid_d = nc.dram_tensor("hmid_d", [I0, SEQ * T], F32).ap()
    x1_d = nc.dram_tensor("x1_d", [D, SEQ * T], F32).ap()
    xn1_d = nc.dram_tensor("xn1_d", [D, SEQ * T], F32).ap()
    g1_d = nc.dram_tensor("g1_d", [D, SEQ * T], BF16).ap()
    kh1_d = nc.dram_tensor("kh1_d", [SEQ * T, I0], BF16).ap()
    wqgT_d = nc.dram_tensor("wqgT_d", [D, I0], BF16).ap()
    wquT_d = nc.dram_tensor("wquT_d", [D, I0], BF16).ap()
    wqdT_d = nc.dram_tensor("wqdT_d", [I0, D], BF16).ap()
    swg_d = nc.dram_tensor("swg_d", [I0, 1], F32).ap()
    swu_d = nc.dram_tensor("swu_d", [I0, 1], F32).ap()
    swd_d = nc.dram_tensor("swd_d", [D, 1], F32).ap()
    wq4gT_d = nc.dram_tensor("wq4gT_d", [D, I0], BF16).ap()
    wq4uT_d = nc.dram_tensor("wq4uT_d", [D, I0], BF16).ap()
    wq4dT_d = nc.dram_tensor("wq4dT_d", [I0, D], BF16).ap()
    cc_in = nc.dram_tensor("cc_in", [1, 1], F32).ap()
    cc_out = nc.dram_tensor("cc_out", [1, 1], F32).ap()
    cc_in2 = nc.dram_tensor("cc_in2", [1, 1], F32).ap()
    cc_out2 = nc.dram_tensor("cc_out2", [1, 1], F32).ap()

    gs = contextlib.ExitStack()

    def g_sb(name, shape, dt=F32):
        return gs.enter_context(nc.sbuf_tensor(name, shape, dt))

    sident = g_sb("sident", [128, 128])
    sidentb = g_sb("sidentb", [128, 128], BF16)
    s1f = g_sb("s1f", [1, 128])
    selrow_b = g_sb("sb_selrow_b", [32, 32 * 128], BF16)
    selinv = g_sb("sb_selinv", [32, 8 * 128])
    selsq = g_sb("sb_selsq", [128, 8 * 32])
    convw0 = g_sb("convw0", [128, NDI * 4])
    convb0 = g_sb("convb0", [128, NDI])
    bdt0 = g_sb("bdt0", [128, NDI])
    negA = g_sb("negA", [128, NDI * NST])
    dskip = g_sb("dskip", [128, NDI])
    sclipx = g_sb("sclipx", [1, 1])
    sclipd = g_sb("sclipd", [1, 1])
    convw1 = g_sb("convw1", [128, NZC * 4])
    thcol = g_sb("thcol", [128, ND])
    th2pi = g_sb("th2pi", [128, ND])
    nsre_c = g_sb("nsre_c", [128, ND])
    nsim_c = g_sb("nsim_c", [128, ND])
    spd = g_sb("spd", [32, 1])
    nspd = g_sb("nspd", [32, 1])
    amask = g_sb("amask_s", [32, 1])
    dmask = g_sb("dmask_s", [32, 1])
    walpha = g_sb("walpha", [32, 1])
    wbeta = g_sb("wbeta", [32, 1])
    sco1 = g_sb("sco1", [32, 1])
    sl_a = g_sb("sl_a", [32, 1])
    sl_b = g_sb("sl_b", [32, 1])
    bdiag_re = g_sb("bdiag_re", [128, 128], BF16)
    bdiag_im = g_sb("bdiag_im", [128, 128], BF16)
    wre_st = g_sb("wre_st", [128, 32])
    wim_st = g_sb("wim_st", [128, 32])
    mx1buf = g_sb("mx1buf", [128, NTILES * ND])
    mx2buf = g_sb("mx2buf", [128, NIC * NTILES])
    vpad = g_sb("vpad", [128, 512])
    spad = g_sb("spad", [128, 512])
    iqst = g_sb("iqst", [128, 2])
    wabst = g_sb("wabst", [32, 8])
    hcarry = g_sb("hcarry", [128, NDI * NST])
    gmax1 = g_sb("gmax1", [1, 1])
    gmax2 = g_sb("gmax2", [1, 1])
    invq1 = g_sb("invq1", [128, 1])
    invq2 = g_sb("invq2", [128, 1])
    sxcol = g_sb("sxcol", [128, 1])
    shcol = g_sb("shcol", [128, 1])
    sq1 = g_sb("sq1", [1, 2])
    sq2 = g_sb("sq2", [1, 2])
    onescol = g_sb("onescol", [128, 1])
    mrow = g_sb("mrow", [1, TT])
    rrow = g_sb("rrow", [1, TT])
    qrow = g_sb("qrow", [1, TT])
    wsc_mx = g_sb("wsc_mx", [128, 1])

    sconsts = g_sb("sconsts", [128, 16])

    pb = [gs.enter_context(nc.psum_tensor(f"pb{i}", [128, 512], F32))
          for i in range(8)]

    em = EM(nc)
    em.flushers = {
        "V": lambda e: e.tensor_scalar_mul(vpad[:], vpad[:], 1.0),
        "S": lambda e: e.copy(spad[:], spad[:]),
        "T": lambda e: e.matmul(pb[2][0:1, 448:449], sident[0:1, 0:1],
                                sident[0:1, 0:1], start=True, stop=True),
    }
    for i, v in enumerate(CONST_VALS):
        nc.const_aps.register(F32, float(v), sconsts[:, i:i + 1])

    def seg(buf, c, W):
        return buf[:, c * W:(c + 1) * W]

    def col(buf, c):
        return buf[:, c:c + 1]

    def sep(e):
        e.tensor_scalar_mul(vpad[:], vpad[:], 1.0)

    def d_setup(e):
        for dst, src in [(sident, ident_i), (s1f, ones1f_i), (sconsts, consts_i),
                         (selrow_b, selrow_b_i), (selinv, selinv_i),
                         (selsq, selsq_i), (convw0, conv_w0), (convb0, conv_b0),
                         (bdt0, b_dt0), (negA, A_log0), (dskip, D_skip0),
                         (sclipx, clip_x), (sclipd, clip_down),
                         (convw1, conv_w1), (thcol, theta1), (spd, slopes1),
                         (amask, amask_i), (dmask, dmask_i),
                         (sco1, score1), (nsre_c, ns_re), (nsim_c, ns_im)]:
            em.dma(e, dst[:], src)
        for j in range(4):
            em.dma(e, wre_st[32 * j:32 * j + 32, :], w_re)
            em.dma(e, wim_st[32 * j:32 * j + 32, :], w_im)
    em.step("D", d_setup)

    def s_setup(e):
        if "wab" in dbg:
            e.copy(wabst[:, 6:7], spd[:])
        e.copy(sidentb[:], sident[:])
        e.activation(negA[:], negA[:], AF.Exp)
        e.activation(sl_a[:], spd[:], AF.Abs)
        e.activation(sl_a[:], sl_a[:], AF.Exp, bias=0.0, scale=-1.0)
        e.activation(sl_a[:], sl_a[:], AF.Ln, bias=1.0, scale=1.0)
        e.activation(sl_b[:], spd[:], AF.Relu)
    em.step("S", s_setup)

    def v_slopes(e):
        e.tensor_tensor(nspd[:], sl_a[:], sl_b[:], ALU.add)
        sep(e)
        e.tensor_tensor(walpha[:], nspd[:], amask[:], ALU.mult)
        e.tensor_tensor(wbeta[:], nspd[:], dmask[:], ALU.mult)
        sep(e)
    em.step("V", v_slopes)
    if "wab" in dbg:
        def s_wab(e):
            e.copy(wabst[:, 0:1], spd[:])
            e.copy(wabst[:, 1:2], nspd[:])
            e.copy(wabst[:, 2:3], walpha[:])
            e.copy(wabst[:, 3:4], wbeta[:])
            e.copy(wabst[:, 4:5], sl_a[:])
            e.copy(wabst[:, 5:6], sl_b[:])
            e.copy(wabst[:, 7:8], amask[:])
        em.step("S", s_wab)
        em.step("D", lambda e: em.dma(e, dbg["wab"], wabst[:]))

    def v_setup(e):
        e.tensor_scalar_mul(negA[:], negA[:], -1.0)
        e.tensor_scalar_mul(th2pi[:], thcol[:], 1.0 / (2.0 * PI))
        e.memset(vpad[:], 0.0)
        e.memset(hcarry[:], 0.0)
        e.memset(bdiag_re[:], 0.0)
        e.memset(bdiag_im[:], 0.0)
        e.memset(onescol[:], 1.0)
    em.step("V", v_setup)

    def s_bdiag(e):
        for j in range(4):
            e.copy(bdiag_re[32 * j:32 * j + 32, 32 * j:32 * j + 32],
                   wre_st[32 * j:32 * j + 32, :])
            e.copy(bdiag_im[32 * j:32 * j + 32, 32 * j:32 * j + 32],
                   wim_st[32 * j:32 * j + 32, :])
    em.step("S", s_bdiag)

    em.new_block()
    wb = contextlib.ExitStack()

    def wb_sb(name, shape, dt=BF16):
        return wb.enter_context(nc.sbuf_tensor(name, shape, dt))

    bw_in = wb_sb("bw_in", [128, ND * (2 * DI)])
    bw_xp = wb_sb("bw_xp", [128, NDI * 96])
    bw_dt = wb_sb("bw_dt", [DTR, DI])
    ws = contextlib.ExitStack()

    def ws_sb(name, shape, dt=F32):
        return ws.enter_context(nc.sbuf_tensor(name, shape, dt))

    wstage = ws_sb("wstage", [128, 4096])
    wrow = ws_sb("wrow", [128, I0], BF16)
    wsc = ws_sb("wsc", [128, 1])
    wrec = ws_sb("wrec", [128, 1])
    wint = ws_sb("wint", [128, I0], I32)
    wtmp = ws_sb("wtmp", [128, I0])
    wtm2 = ws_sb("wtm2", [128, I0])
    wtm3 = ws_sb("wtm3", [128, I0])
    wq4sc = ws_sb("wq4sc", [128, I0 // 16])
    wq4rc = ws_sb("wq4rc", [128, I0 // 16])

    def cast_weight(dst, src, R, C):
        for c in range(cdiv(R, CH)):
            n = min(CH, R - c * CH)
            em.step("D", lambda e, c=c, n=n: em.dma(
                e, wstage[0:n, 0:C], src[c * CH:c * CH + n, :]))
            em.step("S", lambda e, c=c, n=n: e.copy(
                seg(dst, c, C)[0:n, :], wstage[0:n, 0:C]))

    cast_weight(bw_in, w0_inT, D, 2 * DI)
    cast_weight(bw_xp, w0_xprojT, DI, 96)

    def d_bwdt(e):
        em.dma(e, wstage[0:DTR, 0:DI], w0_dtT)
    em.step("D", d_bwdt)
    em.step("S", lambda e: e.copy(bw_dt[:], wstage[0:DTR, 0:DI]))

    def transpose_store(srcf, n, C, dst_d, c):
        for j0 in range(0, cdiv(C, CH), 2):
            def tstep(e, j0=j0):
                for j in range(j0, min(j0 + 2, cdiv(C, CH))):
                    m = min(CH, C - j * CH)
                    e.transpose(pb[j % 2][0:m, 0:n],
                                srcf[0:n, j * CH:j * CH + m], sident[0:n, 0:n])
                e.matmul(pb[j0 % 2][0:1, 400:401], sident[0:1, 0:1],
                         sident[0:1, 0:1], start=True, stop=True)
            em.step("T", tstep)

            def sstep(e, j0=j0):
                for j in range(j0, min(j0 + 2, cdiv(C, CH))):
                    m = min(CH, C - j * CH)
                    e.copy(wrow[0:m, 0:n], pb[j % 2][0:m, 0:n])
                    em.progW.tick(e.dma_start(
                        out=dst_d[j * CH:j * CH + m, c * CH:c * CH + n],
                        in_=wrow[0:m, 0:n]))
                    e.wait_ge(em.progW.sem, em.progW.n)
            em.step("S", sstep)

    def quant_int8_weight(src, R, C, dstT_d, scale_d):
        for c in range(cdiv(R, CH)):
            n = min(CH, R - c * CH)
            em.step("D", lambda e, c=c, n=n: em.dma(
                e, wstage[0:n, 0:C], src[c * CH:c * CH + n, :]))

            def v1(e, c=c, n=n):
                e.tensor_reduce(wsc[0:n, :], wstage[0:n, 0:C], AX.X, ALU.max,
                                apply_absolute_value=True)
                sep(e)
                e.reciprocal(wrec[0:n, :], wsc[0:n, :])
                e.tensor_scalar_mul(wsc[0:n, :], wsc[0:n, :], 1.0 / 127.0)
                sep(e)
                e.tensor_scalar(wtmp[0:n, 0:C], wstage[0:n, 0:C],
                                wrec[0:n, 0:1], 127.0, ALU.mult, ALU.mult)
                e.tensor_scalar(wtmp[0:n, 0:C], wtmp[0:n, 0:C], 1.0, 0.5,
                                ALU.mult, ALU.add)
                e.tensor_scalar(wint[0:n, 0:C], wtmp[0:n, 0:C], 1.0, 0.0,
                                ALU.mult, ALU.add)
                e.tensor_scalar(wtm3[0:n, 0:C], wint[0:n, 0:C], 1.0, 0.0,
                                ALU.mult, ALU.add)
                e.tensor_tensor(wtm2[0:n, 0:C], wtm3[0:n, 0:C], wtmp[0:n, 0:C],
                                ALU.is_gt)
                e.tensor_tensor(wtm3[0:n, 0:C], wtm3[0:n, 0:C], wtm2[0:n, 0:C],
                                ALU.subtract)
                e.tensor_scalar(wtm2[0:n, 0:C], wtm3[0:n, 0:C], 127.0, -128.0,
                                ALU.min, ALU.max)
            em.step("V", v1)
            em.step("D", lambda e, c=c, n=n: em.dma(
                e, scale_d[c * CH:c * CH + n, :], wsc[0:n, :]))
            transpose_store(wtm2, n, C, dstT_d, c)

    quant_int8_weight(wg0, I0, D, wqgT_d, swg_d)
    if "wqg" in dbg:
        em.step("D", lambda e: em.dma(e, dbg["wqg"], wqgT_d))
        em.step("D", lambda e: em.dma(e, dbg["swg"], swg_d))
    quant_int8_weight(wu0, I0, D, wquT_d, swu_d)
    quant_int8_weight(wd0, D, I0, wqdT_d, swd_d)

    def fp4_core(e, zb, n, C, scb, rcb, ib, sgn, ab, fb, outb):
        nb = C // 16
        z = zb[0:n, 0:C]
        sg = sgn[0:n, 0:C]
        a = ab[0:n, 0:C]
        f = fb[0:n, 0:C]
        ii = ib[0:n, 0:C]
        e.tensor_reduce(scb[0:n, 0:nb].rearrange("p (g o) -> p g o", o=1),
                        z.rearrange("p (g k) -> p g k", k=16),
                        AX.X, ALU.max, apply_absolute_value=True)
        e.tensor_single_scalar(sg, z, 0.0, ALU.is_gt)
        e.tensor_scalar(sg, sg, 2.0, -1.0, ALU.mult, ALU.add)
        e.tensor_scalar(scb[0:n, 0:nb], scb[0:n, 0:nb], 1.0 / 6.0, 1e-12,
                        ALU.mult, ALU.max)
        e.tensor_scalar_mul(a, z, -1.0)
        e.tensor_tensor(a, a, z, ALU.max)
        e.reciprocal(rcb[0:n, 0:nb], scb[0:n, 0:nb])
        e.tensor_scalar_mul(a, a, 2.0)
        rb = rcb[0:n, 0:nb].unsqueeze(2).broadcast_to([n, nb, 16])
        e.tensor_tensor(a.rearrange("p (g k) -> p g k", k=16),
                        a.rearrange("p (g k) -> p g k", k=16), rb, ALU.mult)
        e.tensor_scalar(ii, a, 1.0, 0.0, ALU.mult, ALU.add)
        e.tensor_scalar(f, ii, 1.0, 0.0, ALU.mult, ALU.add)
        e.tensor_tensor(z, a, f, ALU.subtract)
        e.tensor_single_scalar(z, z, 0.5, ALU.is_gt)
        e.tensor_tensor(f, f, z, ALU.add)
        e.tensor_scalar(f, f, 0.5, 2.0, ALU.mult, ALU.min)
        e.tensor_single_scalar(z, a, 10.0, ALU.is_gt)
        e.scalar_tensor_tensor(f, z, 2.0, f, ALU.mult, ALU.add)
        e.tensor_scalar_mul(a, a, 0.5)
        e.tensor_scalar(ii, a, 1.0, 0.0, ALU.mult, ALU.add)
        e.tensor_scalar(z, ii, 1.0, 0.0, ALU.mult, ALU.add)
        e.tensor_tensor(a, a, z, ALU.subtract)
        e.tensor_single_scalar(a, a, 0.5, ALU.is_gt)
        e.tensor_scalar(z, z, 1.0, -2.0, ALU.mult, ALU.add)
        e.tensor_tensor(z, z, a, ALU.add)
        e.tensor_scalar(z, z, 0.0, 2.0, ALU.max, ALU.min)
        e.tensor_tensor(f, f, z, ALU.add)
        e.tensor_tensor(f, f, sg, ALU.mult)
        sb_ = scb[0:n, 0:nb].unsqueeze(2).broadcast_to([n, nb, 16])
        e.tensor_tensor(outb[0:n, 0:C].rearrange("p (g k) -> p g k", k=16),
                        f.rearrange("p (g k) -> p g k", k=16), sb_, ALU.mult)

    def quant_fp4_weight(src, R, C, dstT_d):
        for c in range(cdiv(R, CH)):
            n = min(CH, R - c * CH)
            em.step("D", lambda e, c=c, n=n: em.dma(
                e, wstage[0:n, 0:C], src[c * CH:c * CH + n, :]))
            em.step("V", lambda e, c=c, n=n: fp4_core(
                e, wstage, n, C, wq4sc, wq4rc, wint, wtmp, wtm2, wtm3, wtm3))
            transpose_store(wtm3, n, C, dstT_d, c)

    quant_fp4_weight(wg1, I0, D, wq4gT_d)
    quant_fp4_weight(wu1, I0, D, wq4uT_d)
    quant_fp4_weight(wd1, D, I0, wq4dT_d)

    ws.close()

    def emit_ln(srcbuf, dstbuf, sqbuf, it_for_max=None, mxbuf=None, dst_bf=None):
        def s_sq(e):
            for c in range(ND):
                e.activation(seg(sqbuf, c, TT), seg(srcbuf, c, TT), AF.Square)
        em.step("S", s_sq)

        def t_sums(e):
            for c in range(ND):
                e.matmul(pb[0][0:1, 0:TT], onescol[:], seg(srcbuf, c, TT),
                         start=(c == 0), stop=(c == ND - 1))
            for c in range(ND):
                e.matmul(pb[1][0:1, 0:TT], onescol[:], seg(sqbuf, c, TT),
                         start=(c == 0), stop=(c == ND - 1))
        em.step("T", t_sums)

        def v_stats(e):
            e.tensor_scalar_mul(mrow[:], pb[0][0:1, 0:TT], 1.0 / D)
            e.tensor_scalar_mul(rrow[:], pb[1][0:1, 0:TT], 1.0 / D)
            e.tensor_tensor(qrow[:], mrow[:], mrow[:], ALU.mult)
            e.tensor_tensor(rrow[:], rrow[:], qrow[:], ALU.subtract)
        em.step("V", v_stats)
        em.step("S", lambda e: e.activation(rrow[:], rrow[:], AF.Sqrt,
                                            bias=1e-5, scale=1.0))

        def v_recip(e):
            e.reciprocal(qrow[:], rrow[:])
            e.tensor_tensor(mrow[:], mrow[:], qrow[:], ALU.mult)
        em.step("V", v_recip)

        def t_bcast(e):
            e.matmul(pb[2][:, 0:TT], s1f[:], qrow[:], start=True, stop=True)
            e.matmul(pb[3][:, 0:TT], s1f[:], mrow[:], start=True, stop=True)
        em.step("T", t_bcast)

        def v_xn(e):
            W = ND * TT
            e.tensor_tensor(
                dstbuf[:, 0:W].rearrange("p (c t) -> p c t", t=TT),
                srcbuf[:, 0:W].rearrange("p (c t) -> p c t", t=TT),
                pb[2][:, 0:TT].unsqueeze(1).broadcast_to([128, ND, TT]),
                ALU.mult)
            e.tensor_tensor(
                dstbuf[:, 0:W].rearrange("p (c t) -> p c t", t=TT),
                dstbuf[:, 0:W].rearrange("p (c t) -> p c t", t=TT),
                pb[3][:, 0:TT].unsqueeze(1).broadcast_to([128, ND, TT]),
                ALU.subtract)
            if it_for_max is not None:
                e.tensor_reduce(
                    mxbuf[:, it_for_max * ND:(it_for_max + 1) * ND]
                    .rearrange("p (c o) -> p c o", o=1),
                    dstbuf[:, 0:W].rearrange("p (c t) -> p c t", t=TT),
                    AX.X, ALU.max, apply_absolute_value=True)
        em.step("V", v_xn)
        if dst_bf is not None:
            def s_cast(e):
                for c in range(ND):
                    e.copy(seg(dst_bf, c, TT), seg(dstbuf, c, TT))
            em.step("S", s_cast)

    em.new_block()
    p1 = contextlib.ExitStack()

    def p1sb(name, shape, dt=F32):
        return p1.enter_context(nc.sbuf_tensor(name, shape, dt))

    xpool = p1sb("xpool", [128, ND * TT])
    xt = xpool
    xinbuf = xpool.bitcast(BF16)
    XW = TT + 3
    xpad = p1sb("xpad", [128, 24])
    xnt = p1sb("xnt", [128, ND * TT])
    spa = xnt
    xnb = p1sb("xnb", [128, ND * TT], BF16)
    cacc = p1sb("cacc", [128, NDI * TT])
    ubuf = p1sb("ubuf", [128, NDI * TT], BF16)
    zraw = p1sb("zraw", [128, NDI * TT], BF16)
    dtlo_b = p1sb("dtlo_b", [DTR, TT], BF16)
    bcrow = p1sb("bcrow", [32, TT], BF16)
    Bbc = p1sb("Bbc", [128, NST * TT], BF16)
    Cbc = p1sb("Cbc", [128, NST * TT], BF16)
    dAb = p1sb("dAb", [128, NST * TT], BF16)
    dBb = p1sb("dBb", [128, NST * TT], BF16)
    hbb = p1sb("hbb", [128, NST * TT], BF16)
    dub = p1sb("dub", [128, TT], BF16)
    szk = p1sb("szk", [128, TT], BF16)
    ybuf = p1sb("ybuf", [128, TT])
    ymb = p1sb("ymb", [128, NDI * TT], BF16)
    dtf = cacc
    xtail = p1sb("xtail", [128, NDI * 3], BF16)

    def xin(c):
        return xinbuf[:, c * TT:(c + 1) * TT]

    for it in range(NTILES):
        t0 = it * TT
        seq_start = (it % TPS == 0)

        def d_x(e, t0=t0):
            for c in range(ND):
                em.dma(e, seg(xt, c, TT), x_fm[c * CH:(c + 1) * CH, t0:t0 + TT])
        em.step("D", d_x)

        emit_ln(xt, xnt, cacc, it_for_max=it, mxbuf=mx1buf, dst_bf=xnb)

        def d_xn(e, t0=t0):
            for c in range(ND):
                em.dma(e, xn_d[c * CH:(c + 1) * CH, t0:t0 + TT], seg(xnt, c, TT))
                if "xn" in dbg:
                    em.dma(e, dbg["xn"][c * CH:(c + 1) * CH, t0:t0 + TT],
                           seg(xnt, c, TT))
        em.step("D", d_xn)

        for g in range(8):
            def t_xz(e, g=g):
                for mm in range(4):
                    m = g * 4 + mm
                    for k in range(ND):
                        e.matmul(pb[4 + mm][:, 0:TT],
                                 seg(bw_in, k, 2 * DI)[:, m * CH:(m + 1) * CH],
                                 seg(xnb, k, TT), start=(k == 0), stop=(k == ND - 1))
            em.step("T", t_xz)

            def s_xz(e, g=g):
                for mm in range(4):
                    m = g * 4 + mm
                    if m < NDI:
                        e.copy(xin(m), pb[4 + mm][:, 0:TT])
                    else:
                        e.copy(seg(zraw, m - NDI, TT), pb[4 + mm][:, 0:TT])
            em.step("S", s_xz)

        def v_conv(e, seq_start=seq_start):
            for c in range(NDI):
                tl = seg(xtail, c, 3)
                xb_ = xin(c)
                if seq_start:
                    e.tensor_scalar(seg(cacc, c, TT)[:, 0:3], xb_[:, 0:3],
                                    0.0, col(convb0, c), ALU.mult, ALU.add)
                else:
                    e.tensor_scalar(seg(cacc, c, TT)[:, 0:3], tl,
                                    col(convw0, c * 4 + 0), col(convb0, c),
                                    ALU.mult, ALU.add)
                e.tensor_scalar(seg(cacc, c, TT)[:, 3:TT], xb_[:, 0:TT - 3],
                                col(convw0, c * 4 + 0), col(convb0, c),
                                ALU.mult, ALU.add)
                for j in (1, 2, 3):
                    sh = 3 - j
                    if sh > 0 and not seq_start:
                        e.scalar_tensor_tensor(seg(cacc, c, TT)[:, 0:sh],
                                               tl[:, j:3], col(convw0, c * 4 + j),
                                               seg(cacc, c, TT)[:, 0:sh],
                                               ALU.mult, ALU.add)
                    e.scalar_tensor_tensor(seg(cacc, c, TT)[:, sh:TT],
                                           xb_[:, 0:TT - sh], col(convw0, c * 4 + j),
                                           seg(cacc, c, TT)[:, sh:TT],
                                           ALU.mult, ALU.add)
            for c in range(NDI):
                e.tensor_copy(seg(xtail, c, 3), xin(c)[:, TT - 3:TT])
        em.step("V", v_conv)

        def s_silu_u(e):
            for c in range(NDI):
                e.activation(seg(ubuf, c, TT), seg(cacc, c, TT), AF.Silu)
        em.step("S", s_silu_u)

        def t_dbc(e):
            for k in range(NDI):
                e.matmul(pb[0][0:64, 0:TT], seg(bw_xp, k, 96)[:, 0:64],
                         seg(ubuf, k, TT), start=(k == 0), stop=(k == NDI - 1))
            for k in range(NDI):
                e.matmul(pb[1][0:32, 0:TT], seg(bw_xp, k, 96)[:, 64:96],
                         seg(ubuf, k, TT), start=(k == 0), stop=(k == NDI - 1))
        em.step("T", t_dbc)

        def s_dbc(e):
            e.copy(dtlo_b[:], pb[0][0:64, 0:TT])
            e.copy(bcrow[:], pb[1][0:32, 0:TT])
        em.step("S", s_dbc)

        for g in range(4):
            def t_bc(e, g=g):
                for q in range(8):
                    n8 = g * 8 + q
                    bank = pb[4 + q // 2]
                    off = (q % 2) * TT
                    e.matmul(bank[:, off:off + TT],
                             selrow_b[:, n8 * 128:(n8 + 1) * 128],
                             bcrow[:], start=True, stop=True)
            em.step("T", t_bc)

            def s_bc(e, g=g):
                for q in range(8):
                    n8 = g * 8 + q
                    bank = pb[4 + q // 2]
                    off = (q % 2) * TT
                    dst = Bbc if n8 < 16 else Cbc
                    e.copy(seg(dst, n8 % 16, TT), bank[:, off:off + TT])
            em.step("S", s_bc)

        for g in range(4):
            def t_dt(e, g=g):
                for q in range(4):
                    c = g * 4 + q
                    e.matmul(pb[4 + q][:, 0:TT], bw_dt[:, c * CH:(c + 1) * CH],
                             dtlo_b[:], start=True, stop=True)
            em.step("T", t_dt)

            def s_dt(e, g=g):
                for q in range(4):
                    c = g * 4 + q
                    e.copy(seg(cacc, g * 4 + q, TT), pb[4 + q][:, 0:TT])
            em.step("S", s_dt)

        for g in range(4):
            def s_sp(e, g=g):
                for q in range(4):
                    c = g * 4 + q
                    e.activation(seg(spa, q, TT), seg(cacc, c, TT), AF.Abs,
                                 bias=col(bdt0, c), scale=1.0)
                    e.activation(seg(spa, q, TT), seg(spa, q, TT), AF.Exp,
                                 bias=0.0, scale=-1.0)
                    e.activation(seg(spa, q, TT), seg(spa, q, TT), AF.Ln,
                                 bias=1.0, scale=1.0)
                    e.activation(seg(spa, 4 + q, TT), seg(cacc, c, TT), AF.Relu,
                                 bias=col(bdt0, c), scale=1.0)
            em.step("S", s_sp)

            def v_sp(e, g=g):
                for q in range(4):
                    c = g * 4 + q
                    e.tensor_tensor(seg(dtf, c, TT), seg(spa, q, TT),
                                    seg(spa, 4 + q, TT), ALU.add)
            em.step("V", v_sp)

        if seq_start:
            em.step("V", lambda e: e.memset(hcarry[:], 0.0))

        for c in range(NDI):
            def v_dAr(e, c=c):
                e.tensor_tensor(
                    dAb[:].rearrange("p (n t) -> p n t", t=TT),
                    seg(dtf, c, TT).unsqueeze(1).broadcast_to([128, NST, TT]),
                    negA[:, c * NST:(c + 1) * NST].unsqueeze(2)
                    .broadcast_to([128, NST, TT]),
                    ALU.mult)
            em.step("V", v_dAr)

            def s_dA(e, c=c):
                e.activation(dAb[:], dAb[:], AF.Exp)
                e.activation(szk[:], seg(zraw, c, TT), AF.Silu)
            em.step("S", s_dA)

            def v_pre(e, c=c):
                e.tensor_tensor(dub[:], seg(dtf, c, TT), seg(ubuf, c, TT), ALU.mult)
                e.tensor_tensor(
                    dBb[:].rearrange("p (n t) -> p n t", t=TT),
                    dub[:].unsqueeze(1).broadcast_to([128, NST, TT]),
                    Bbc[:].rearrange("p (n t) -> p n t", t=TT),
                    ALU.mult)
            em.step("V", v_pre)

            def v_scan(e, c=c):
                for n in range(NST):
                    e.tensor_tensor_scan(seg(hbb, n, TT), seg(dAb, n, TT),
                                         seg(dBb, n, TT),
                                         col(hcarry, c * NST + n),
                                         ALU.mult, ALU.add)
                e.tensor_tensor(dBb[:], hbb[:], Cbc[:], ALU.mult)
                e.tensor_copy(hcarry[:, c * NST:(c + 1) * NST],
                              hbb[:].rearrange("p (n t) -> p n t", t=TT)[:, :, TT - 1])
                e.tensor_reduce(ybuf[:].rearrange("p (t o) -> p t o", o=1),
                                dBb[:].rearrange("p (n t) -> p t n", t=TT),
                                AX.X, ALU.add)
                e.scalar_tensor_tensor(ybuf[:], seg(ubuf, c, TT), col(dskip, c),
                                       ybuf[:], ALU.mult, ALU.add)
                e.tensor_tensor(seg(ymb, c, TT), ybuf[:], szk[:], ALU.mult)
            em.step("V", v_scan)

        def d_ym(e, t0=t0):
            for c in range(NDI):
                em.dma(e, ym_d[c * CH:(c + 1) * CH, t0:t0 + TT], seg(ymb, c, TT))
                if "ym" in dbg:
                    em.dma(e, dbg["ym"][c * CH:(c + 1) * CH, t0:t0 + TT],
                           seg(ymb, c, TT))
        em.step("D", d_ym)

    em.new_block()
    def v_gm1(e):
        sep(e)
        e.tensor_reduce(wsc_mx[:], mx1buf[:], AX.X, ALU.max)
    em.step("V", v_gm1)
    em.step("T", lambda e: e.matmul(pb[0][0:1, 0:128], wsc_mx[:], sident[:],
                                    start=True, stop=True))
    em.step("V", lambda e: e.tensor_reduce(gmax1[:], pb[0][0:1, 0:128],
                                           AX.X, ALU.max))
    em.step("D", lambda e: em.dma(e, cc_in, gmax1[:]))

    def g_cc1(e):
        em.progC.tick(e.collective_compute(
            "AllReduce", ALU.max, replica_groups=[list(range(NCORES))],
            ins=[cc_in.opt()], outs=[cc_out.opt()]))
    em.new_block()
    em.step("G", g_cc1)
    em.new_block()
    em.step("D", lambda e: em.dma(e, gmax1[:], cc_out), deps=em.marks())
    em.step("S", lambda e: e.activation(sq1[:, 0:1], sclipx[:], AF.Sigmoid))

    def v_sq1(e):
        e.tensor_tensor(sq1[:, 0:1], sq1[:, 0:1], gmax1[:], ALU.mult)
        sep(e)
        e.reciprocal(sq1[:, 1:2], sq1[:, 0:1])
        sep(e)
        e.tensor_scalar_mul(sq1[:, 1:2], sq1[:, 1:2], 127.0)
        e.tensor_scalar_mul(sq1[:, 0:1], sq1[:, 0:1], 1.0 / 127.0)
        sep(e)
    em.step("V", v_sq1)
    em.step("T", lambda e: e.matmul(pb[0][:, 0:2], s1f[:], sq1[:, 0:2],
                                    start=True, stop=True))

    def v_iq1(e):
        e.tensor_copy(sxcol[:], pb[0][:, 0:1])
        e.tensor_copy(invq1[:], pb[0][:, 1:2])
        sep(e)
    em.step("V", v_iq1)
    if "gmax" in dbg:
        em.step("D", lambda e: em.dma(e, dbg["gmax"][:, 0:1], gmax1[:]))
    if "iq" in dbg:
        def v_iqt(e):
            e.tensor_copy(iqst[:, 0:1], sxcol[:])
            e.tensor_copy(iqst[:, 1:2], invq1[:])
            sep(e)
        em.step("V", v_iqt)
        em.step("D", lambda e: em.dma(e, dbg["iq"], iqst[:]))

    p1.close()
    wb.close()

    em.new_block()
    p2 = contextlib.ExitStack()

    def p2sb(name, shape, dt=F32):
        return p2.enter_context(nc.sbuf_tensor(name, shape, dt))

    wqgT = p2sb("wqgT", [128, ND * I0], BF16)
    wquT = p2sb("wquT", [128, ND * I0], BF16)
    bw_out2 = p2sb("bw_out2", [128, NDI * D], BF16)
    swgc = p2sb("swgc", [128, NIC])
    swuc = p2sb("swuc", [128, NIC])
    scolg = p2sb("scolg", [128, NIC])
    scolu = p2sb("scolu", [128, NIC])
    xnt2 = p2sb("xnt2", [128, ND * TT])
    kxb = p2sb("kxb", [128, ND * TT], BF16)
    qi2 = p2sb("qi2", [128, TT], I32)
    qt2 = p2sb("qt2", [128, TT])
    qv2 = p2sb("qv2", [128, TT])
    qn2 = p2sb("qn2", [128, TT])
    hsil = p2sb("hsil", [128, TT])
    hm2 = p2sb("hm2", [128, 2 * TT])
    ym2 = p2sb("ym2", [128, NDI * TT], BF16)
    sev = p2sb("sev", [128, ND * TT], BF16)
    wst2 = p2sb("wst2", [128, 4096])
    if "gt" in dbg:
        gsc = p2sb("gsc", [128, 2 * TT])
        usc = p2sb("usc", [128, 2 * TT])

    em.step("D", lambda e: [
        [em.dma(e, seg(wqgT, k, I0), wqgT_d[k * CH:(k + 1) * CH, :]),
         em.dma(e, seg(wquT, k, I0), wquT_d[k * CH:(k + 1) * CH, :])]
        for k in range(ND)] and None)
    for k in range(NDI):
        em.step("D", lambda e, k=k: em.dma(e, wst2[0:CH, 0:D],
                                           w0_outT[k * CH:(k + 1) * CH, :]))
        em.step("S", lambda e, k=k: e.copy(seg(bw_out2, k, D), wst2[0:CH, 0:D]))

    def d_sw(e):
        for c in range(NIC):
            em.dma(e, col(swgc, c), swg_d[c * CH:(c + 1) * CH, :])
            em.dma(e, col(swuc, c), swu_d[c * CH:(c + 1) * CH, :])
    em.step("D", d_sw)

    def v_scol(e):
        for c in range(NIC):
            e.tensor_tensor(col(scolg, c), col(swgc, c), sxcol[:], ALU.mult)
            e.tensor_tensor(col(scolu, c), col(swuc, c), sxcol[:], ALU.mult)
    em.step("V", v_scol)

    for it in range(NTILES):
        t0 = it * TT

        def d_xn2(e, t0=t0):
            for c in range(ND):
                em.dma(e, seg(xnt2, c, TT), xn_d[c * CH:(c + 1) * CH, t0:t0 + TT])
            for c in range(NDI):
                em.dma(e, seg(ym2, c, TT), ym_d[c * CH:(c + 1) * CH, t0:t0 + TT])
        em.step("D", d_xn2)
        if "xn2" in dbg:
            def d_xn2t(e, t0=t0):
                e.wait_ge(em.progD.sem, em.progD.n)
                for c in range(ND):
                    em.dma(e, dbg["xn2"][c * CH:(c + 1) * CH, t0:t0 + TT],
                           seg(xnt2, c, TT))
            em.step("D", d_xn2t)

        def v_q2(e):
            W = 4 * TT
            qv = wst2[:, 0:W]
            qi = wst2.bitcast(I32)[:, W:2 * W]
            qf = wst2[:, 2 * W:3 * W]
            qn = wst2[:, 3 * W:4 * W]
            for g in range(2):
                xs = xnt2[:, g * W:(g + 1) * W]
                e.tensor_scalar(qv, xs, invq1[:], 0.5, ALU.mult, ALU.add)
                e.tensor_scalar(qi, qv, 1.0, 0.0, ALU.mult, ALU.add)
                e.tensor_scalar(qf, qi, 1.0, 0.0, ALU.mult, ALU.add)
                e.tensor_tensor(qn, qf, qv, ALU.is_gt)
                e.tensor_tensor(qf, qf, qn, ALU.subtract)
                e.tensor_scalar(kxb[:, g * W:(g + 1) * W], qf, 127.0, -128.0,
                                ALU.min, ALU.max)
        em.step("V", v_q2)
        if "kx" in dbg:
            def d_kx(e, t0=t0):
                for c in range(ND):
                    em.dma(e, dbg["kx"][c * CH:(c + 1) * CH, t0:t0 + TT],
                           seg(kxb, c, TT))
            em.step("D", d_kx)

        for m in range(NIC):
            def t_gu(e, m=m):
                for k in range(ND):
                    e.matmul(pb[4][:, 0:TT], seg(wqgT, k, I0)[:, m * CH:(m + 1) * CH],
                             seg(kxb, k, TT), start=(k == 0), stop=(k == ND - 1))
                for k in range(ND):
                    e.matmul(pb[5][:, 0:TT], seg(wquT, k, I0)[:, m * CH:(m + 1) * CH],
                             seg(kxb, k, TT), start=(k == 0), stop=(k == ND - 1))
            em.step("T", t_gu)
            em.step("S", lambda e, m=m: e.activation(
                hsil[:], pb[4][:, 0:TT], AF.Silu, bias=0.0, scale=col(scolg, m)))
            if "gt" in dbg:
                def v_gt(e, m=m):
                    e.tensor_scalar(seg(gsc, m % 2, TT), pb[4][:, 0:TT],
                                    col(scolg, m), 0.0, ALU.mult, ALU.add)
                    e.tensor_scalar(seg(usc, m % 2, TT), pb[5][:, 0:TT],
                                    col(scolu, m), 0.0, ALU.mult, ALU.add)
                em.step("V", v_gt)

                def d_gt(e, m=m, t0=t0):
                    em.dma(e, dbg["gt"][m * CH:(m + 1) * CH, t0:t0 + TT],
                           seg(gsc, m % 2, TT))
                    em.dma(e, dbg["up"][m * CH:(m + 1) * CH, t0:t0 + TT],
                           seg(usc, m % 2, TT))
                em.step("D", d_gt)

            def v_hm(e, m=m, it=it):
                e.scalar_tensor_tensor(seg(hm2, m % 2, TT), pb[5][:, 0:TT],
                                       col(scolu, m), hsil[:], ALU.mult, ALU.mult)
                e.tensor_reduce(col(mx2buf, it * NIC + m), seg(hm2, m % 2, TT),
                                AX.X, ALU.max, apply_absolute_value=True)
            em.step("V", v_hm)

            def d_hm(e, m=m, t0=t0):
                em.dma(e, hmid_d[m * CH:(m + 1) * CH, t0:t0 + TT],
                       seg(hm2, m % 2, TT))
                if "hmid" in dbg:
                    em.dma(e, dbg["hmid"][m * CH:(m + 1) * CH, t0:t0 + TT],
                           seg(hm2, m % 2, TT))
            em.step("D", d_hm)

        for g in range(2):
            def t_wo(e, g=g):
                for mm in range(4):
                    m = g * 4 + mm
                    for k in range(NDI):
                        e.matmul(pb[4 + mm][:, 0:TT],
                                 seg(bw_out2, k, D)[:, m * CH:(m + 1) * CH],
                                 seg(ym2, k, TT), start=(k == 0), stop=(k == NDI - 1))
            em.step("T", t_wo)

            def s_wo(e, g=g):
                for mm in range(4):
                    m = g * 4 + mm
                    e.copy(seg(sev, m, TT), pb[4 + mm][:, 0:TT])
            em.step("S", s_wo)

        def d_s0(e, t0=t0):
            for c in range(ND):
                em.dma(e, s0_d[c * CH:(c + 1) * CH, t0:t0 + TT], seg(sev, c, TT))
                if "s0" in dbg:
                    em.dma(e, dbg["s0"][c * CH:(c + 1) * CH, t0:t0 + TT],
                           seg(sev, c, TT))
        em.step("D", d_s0)

    em.new_block()
    def v_gm2(e):
        sep(e)
        e.tensor_reduce(wsc_mx[:], mx2buf[:], AX.X, ALU.max)
    em.step("V", v_gm2)
    em.step("T", lambda e: e.matmul(pb[0][0:1, 0:128], wsc_mx[:], sident[:],
                                    start=True, stop=True))
    em.step("V", lambda e: e.tensor_reduce(gmax2[:], pb[0][0:1, 0:128],
                                           AX.X, ALU.max))
    em.step("D", lambda e: em.dma(e, cc_in2, gmax2[:]))

    def g_cc2(e):
        em.progC.tick(e.collective_compute(
            "AllReduce", ALU.max, replica_groups=[list(range(NCORES))],
            ins=[cc_in2.opt()], outs=[cc_out2.opt()]))
    em.new_block()
    em.step("G", g_cc2)
    em.new_block()
    em.step("D", lambda e: em.dma(e, gmax2[:], cc_out2), deps=em.marks())
    em.step("S", lambda e: e.activation(sq2[:, 0:1], sclipd[:], AF.Sigmoid))

    def v_sq2(e):
        e.tensor_tensor(sq2[:, 0:1], sq2[:, 0:1], gmax2[:], ALU.mult)
        sep(e)
        e.reciprocal(sq2[:, 1:2], sq2[:, 0:1])
        sep(e)
        e.tensor_scalar_mul(sq2[:, 1:2], sq2[:, 1:2], 127.0)
        e.tensor_scalar_mul(sq2[:, 0:1], sq2[:, 0:1], 1.0 / 127.0)
        sep(e)
    em.step("V", v_sq2)
    em.step("T", lambda e: e.matmul(pb[0][:, 0:2], s1f[:], sq2[:, 0:2],
                                    start=True, stop=True))

    def v_iq2(e):
        e.tensor_copy(shcol[:], pb[0][:, 0:1])
        e.tensor_copy(invq2[:], pb[0][:, 1:2])
        sep(e)
    em.step("V", v_iq2)
    if "gmax" in dbg:
        em.step("D", lambda e: em.dma(e, dbg["gmax"][:, 1:2], gmax2[:]))

    p2.close()

    em.new_block()
    p3 = contextlib.ExitStack()

    def p3sb(name, shape, dt=F32):
        return p3.enter_context(nc.sbuf_tensor(name, shape, dt))

    wqdT = p3sb("wqdT", [128, NIC * D], BF16)
    swdc = p3sb("swdc", [128, ND])
    scold = p3sb("scold", [128, ND])
    hmt3 = p3sb("hmt3", [128, NIC * TT])
    khb = p3sb("khb", [128, NIC * TT], BF16)
    q3i = p3sb("q3i", [128, 4 * TT], I32)
    q3t = p3sb("q3t", [128, 4 * TT])
    q3v = p3sb("q3v", [128, 4 * TT])
    q3n = p3sb("q3n", [128, 4 * TT])
    xt3 = p3sb("xt3", [128, ND * TT])
    st3 = p3sb("st3", [128, ND * TT], BF16)
    gev = p3sb("gev", [128, ND * TT], BF16)
    m3 = p3sb("m3", [128, TT])
    x1t = p3sb("x1t", [128, ND * TT])
    xn1t = p3sb("xn1t", [128, ND * TT])
    sq3 = p3sb("sq3", [128, ND * TT])

    def d_p3w(e):
        for k in range(NIC):
            em.dma(e, seg(wqdT, k, D), wqdT_d[k * CH:(k + 1) * CH, :])
        for c in range(ND):
            em.dma(e, col(swdc, c), swd_d[c * CH:(c + 1) * CH, :])
    em.step("D", d_p3w)

    def v_scold(e):
        for c in range(ND):
            e.tensor_tensor(col(scold, c), col(swdc, c), shcol[:], ALU.mult)
    em.step("V", v_scold)

    for it in range(NTILES):
        t0 = it * TT

        def d_p3l(e, t0=t0):
            for c in range(NIC):
                em.dma(e, seg(hmt3, c, TT), hmid_d[c * CH:(c + 1) * CH, t0:t0 + TT])
            for c in range(ND):
                em.dma(e, seg(xt3, c, TT), x_fm[c * CH:(c + 1) * CH, t0:t0 + TT])
                em.dma(e, seg(st3, c, TT), s0_d[c * CH:(c + 1) * CH, t0:t0 + TT])
        em.step("D", d_p3l)

        def v_q3(e):
            for g in range(6):
                c0, c1 = 4 * g, min(4 * g + 4, NIC)
                W = (c1 - c0) * TT
                qv = q3v[:, 0:W]
                qi = q3i[:, 0:W]
                qf = q3t[:, 0:W]
                qn = q3n[:, 0:W]
                hs = hmt3[:, c0 * TT:c1 * TT]
                e.tensor_scalar(qv, hs, invq2[:], 0.5, ALU.mult, ALU.add)
                e.tensor_scalar(qi, qv, 1.0, 0.0, ALU.mult, ALU.add)
                e.tensor_scalar(qf, qi, 1.0, 0.0, ALU.mult, ALU.add)
                e.tensor_tensor(qn, qf, qv, ALU.is_gt)
                e.tensor_tensor(qf, qf, qn, ALU.subtract)
                e.tensor_scalar(khb[:, c0 * TT:c1 * TT], qf, 127.0, -128.0,
                                ALU.min, ALU.max)
        em.step("V", v_q3)

        for g in range(2):
            def t_dn(e, g=g):
                for mm in range(4):
                    m = g * 4 + mm
                    for k in range(NIC):
                        e.matmul(pb[4 + mm][:, 0:TT],
                                 seg(wqdT, k, D)[:, m * CH:(m + 1) * CH],
                                 seg(khb, k, TT), start=(k == 0), stop=(k == NIC - 1))
            em.step("T", t_dn)

            def s_dn(e, g=g):
                for mm in range(4):
                    m = g * 4 + mm
                    e.activation(seg(gev, m, TT), pb[4 + mm][:, 0:TT], AF.Silu,
                                 bias=0.0, scale=col(scold, m))
            em.step("S", s_dn)

        def v_x1(e):
            for c in range(ND):
                e.tensor_tensor(m3[:], seg(st3, c, TT), seg(gev, c, TT), ALU.mult)
                e.tensor_tensor(seg(x1t, c, TT), seg(xt3, c, TT), m3[:], ALU.add)
        em.step("V", v_x1)

        emit_ln(x1t, xn1t, sq3)

        def d_p3s(e, t0=t0):
            for c in range(ND):
                em.dma(e, x1_d[c * CH:(c + 1) * CH, t0:t0 + TT], seg(x1t, c, TT))
                em.dma(e, xn1_d[c * CH:(c + 1) * CH, t0:t0 + TT], seg(xn1t, c, TT))
                if "x1" in dbg:
                    em.dma(e, dbg["x1"][c * CH:(c + 1) * CH, t0:t0 + TT],
                           seg(x1t, c, TT))
        em.step("D", d_p3s)

    p3.close()

    em.new_block()
    NTCH = TT // CH
    p4 = contextlib.ExitStack()

    def p4sb(name, shape, dt=F32):
        return p4.enter_context(nc.sbuf_tensor(name, shape, dt))

    wq4gT = p4sb("wq4gT", [128, ND * I0], BF16)
    wq4uT = p4sb("wq4uT", [128, ND * I0], BF16)
    xn4 = p4sb("xn4", [128, ND * TT])
    xn4tok = p4sb("xn4tok", [128, NTCH * D])
    xq4fm = p4sb("xq4fm", [128, ND * TT], BF16)
    f4sc = p4sb("f4sc", [128, I0 // 16])
    f4rc = p4sb("f4rc", [128, I0 // 16])
    f4i = p4sb("f4i", [128, I0], I32)
    f4t1 = p4sb("f4t1", [128, I0])
    f4t2 = p4sb("f4t2", [128, I0])
    f4z = p4sb("f4z", [128, I0])
    h1tok = p4sb("h1tok", [128, I0])
    h1sil = p4sb("h1sil", [128, 512])
    kh1tok = p4sb("kh1tok", [128, I0], BF16)

    def d_p4w(e):
        for k in range(ND):
            em.dma(e, seg(wq4gT, k, I0), wq4gT_d[k * CH:(k + 1) * CH, :])
            em.dma(e, seg(wq4uT, k, I0), wq4uT_d[k * CH:(k + 1) * CH, :])
    em.step("D", d_p4w)

    for it in range(NTILES):
        t0 = it * TT

        def d_xn4(e, t0=t0):
            for c in range(ND):
                em.dma(e, seg(xn4, c, TT), xn1_d[c * CH:(c + 1) * CH, t0:t0 + TT])
        em.step("D", d_xn4)

        for tc in range(NTCH):
            for c0 in range(0, ND, 4):
                def t_tr(e, tc=tc, c0=c0):
                    for c in range(c0, min(c0 + 4, ND)):
                        e.transpose(pb[c % 4][0:CH, 0:CH],
                                    seg(xn4, c, TT)[:, tc * CH:(tc + 1) * CH],
                                    sident[:])
                    e.matmul(pb[c0 % 4][0:1, 400:401], sident[0:1, 0:1],
                             sident[0:1, 0:1], start=True, stop=True)
                em.step("T", t_tr)

                def s_tr(e, tc=tc, c0=c0):
                    for c in range(c0, min(c0 + 4, ND)):
                        e.copy(xn4tok[:, tc * D + c * CH: tc * D + (c + 1) * CH],
                               pb[c % 4][0:CH, 0:CH])
                em.step("S", s_tr)

        def v_q4x(e):
            fp4_core(e, xn4tok, 128, NTCH * D, f4sc, f4rc, f4i, f4t1, f4t2,
                     f4z, f4z)
        em.step("V", v_q4x)

        for tc in range(NTCH):
            for c0 in range(0, ND, 4):
                def t_tb(e, tc=tc, c0=c0):
                    for c in range(c0, min(c0 + 4, ND)):
                        e.transpose(pb[c % 4][0:CH, 0:CH],
                                    f4z[:, tc * D + c * CH: tc * D + (c + 1) * CH],
                                    sident[:])
                    e.matmul(pb[c0 % 4][0:1, 400:401], sident[0:1, 0:1],
                             sident[0:1, 0:1], start=True, stop=True)
                em.step("T", t_tb)

                def s_tb(e, tc=tc, c0=c0):
                    for c in range(c0, min(c0 + 4, ND)):
                        e.copy(seg(xq4fm, c, TT)[:, tc * CH:(tc + 1) * CH],
                               pb[c % 4][0:CH, 0:CH])
                em.step("S", s_tb)

        for tc in range(NTCH):
            for i0 in range(0, I0, 512):
                iw = min(512, I0 - i0)

                def t_g4(e, tc=tc, i0=i0, iw=iw):
                    for k in range(ND):
                        e.matmul(pb[4][:, 0:iw],
                                 seg(xq4fm, k, TT)[:, tc * CH:(tc + 1) * CH],
                                 seg(wq4gT, k, I0)[:, i0:i0 + iw],
                                 start=(k == 0), stop=(k == ND - 1))
                    for k in range(ND):
                        e.matmul(pb[5][:, 0:iw],
                                 seg(xq4fm, k, TT)[:, tc * CH:(tc + 1) * CH],
                                 seg(wq4uT, k, I0)[:, i0:i0 + iw],
                                 start=(k == 0), stop=(k == ND - 1))
                em.step("T", t_g4)
                em.step("S", lambda e, i0=i0, iw=iw: e.activation(
                    h1sil[:, 0:iw], pb[4][:, 0:iw], AF.Silu))
                em.step("V", lambda e, i0=i0, iw=iw: e.tensor_tensor(
                    h1tok[:, i0:i0 + iw], h1sil[:, 0:iw], pb[5][:, 0:iw],
                    ALU.mult))

            def v_q4h(e, tc=tc):
                fp4_core(e, h1tok, 128, I0, f4sc, f4rc, f4i, f4t1, f4t2, f4z,
                         kh1tok)
            em.step("V", v_q4h)

            def d_kh(e, tc=tc, t0=t0):
                em.dma(e, kh1_d[t0 + tc * CH:t0 + (tc + 1) * CH, :], kh1tok[:])
            em.step("D", d_kh)

    p4.close()

    em.new_block()
    p42 = contextlib.ExitStack()

    def p42sb(name, shape, dt=F32):
        return p42.enter_context(nc.sbuf_tensor(name, shape, dt))

    wq4dT = p42sb("wq4dT", [128, NIC * D], BF16)
    kh1ld = p42sb("kh1ld", [128, NTCH * I0], BF16)
    kh1f32 = p42sb("kh1f32", [128, I0])
    kh1fm = p42sb("kh1fm", [128, NIC * TT], BF16)
    g1t = p42sb("g1t", [128, ND * TT], BF16)

    def d_p42w(e):
        for k in range(NIC):
            em.dma(e, seg(wq4dT, k, D), wq4dT_d[k * CH:(k + 1) * CH, :])
    em.step("D", d_p42w)

    for it in range(NTILES):
        t0 = it * TT

        def d_kh1(e, t0=t0):
            for tc in range(NTCH):
                em.dma(e, seg(kh1ld, tc, I0), kh1_d[t0 + tc * CH:t0 + (tc + 1) * CH, :])
        em.step("D", d_kh1)

        for tc in range(NTCH):
            em.step("S", lambda e, tc=tc: e.copy(kh1f32[:], seg(kh1ld, tc, I0)))
            for c0 in range(0, NIC, 4):
                def t_tk(e, tc=tc, c0=c0):
                    for c in range(c0, min(c0 + 4, NIC)):
                        e.transpose(pb[c % 4][0:CH, 0:CH],
                                    kh1f32[:, c * CH:(c + 1) * CH],
                                    sident[:])
                    e.matmul(pb[c0 % 4][0:1, 400:401], sident[0:1, 0:1],
                             sident[0:1, 0:1], start=True, stop=True)
                em.step("T", t_tk)

                def s_tk(e, tc=tc, c0=c0):
                    for c in range(c0, min(c0 + 4, NIC)):
                        e.copy(seg(kh1fm, c, TT)[:, tc * CH:(tc + 1) * CH],
                               pb[c % 4][0:CH, 0:CH])
                em.step("S", s_tk)

        for g in range(2):
            def t_dn1(e, g=g):
                for mm in range(4):
                    m = g * 4 + mm
                    for k in range(NIC):
                        e.matmul(pb[4 + mm][:, 0:TT],
                                 seg(wq4dT, k, D)[:, m * CH:(m + 1) * CH],
                                 seg(kh1fm, k, TT), start=(k == 0),
                                 stop=(k == NIC - 1))
            em.step("T", t_dn1)

            def s_dn1(e, g=g):
                for mm in range(4):
                    m = g * 4 + mm
                    e.activation(seg(g1t, m, TT), pb[4 + mm][:, 0:TT], AF.Silu)
            em.step("S", s_dn1)

        def d_g1(e, t0=t0):
            for c in range(ND):
                em.dma(e, g1_d[c * CH:(c + 1) * CH, t0:t0 + TT], seg(g1t, c, TT))
                if "g1" in dbg:
                    em.dma(e, dbg["g1"][c * CH:(c + 1) * CH, t0:t0 + TT],
                           seg(g1t, c, TT))
        em.step("D", d_g1)

    p42.close()

    em.new_block()
    p5 = contextlib.ExitStack()

    def p5sb(name, shape, dt=F32):
        return p5.enter_context(nc.sbuf_tensor(name, shape, dt))

    bw1_in = p5sb("bw1_in", [128, ND * ZC], BF16)
    bw1_out = p5sb("bw1_out", [128, ND * D], BF16)
    x1l = p5sb("x1l", [128, ND * TT])
    xn1l = p5sb("xn1l", [128, ND * TT])
    xn1b = p5sb("xn1b", [128, ND * TT], BF16)
    zrawb = p5sb("zrawb", [128, NZC * TT], BF16)
    ztail = p5sb("ztail", [128, NZC * 3], BF16)
    cz = p5sb("cz", [128, TT])
    xval = p5sb("xval", [128, ND * TT])
    xg1 = p5sb("xg1", [128, ND * TT], BF16)
    czs = p5sb("czs", [32, TT])
    pwt = p5sb("pwt", [32, TT])
    post = p5sb("post", [32, TT])
    logw = p5sb("logw", [32, TT])
    phit = p5sb("phit", [128, TT])
    aqt = p5sb("aqt", [128, TT])
    kqt = p5sb("kqt", [128, TT], I32)
    sgt = p5sb("sgt", [128, TT])
    prt = p5sb("prt", [128, TT])
    sint = p5sb("sint", [128, TT])
    cost = p5sb("cost", [128, TT])
    nret = p5sb("nret", [128, ND * TT])
    nimt = p5sb("nimt", [128, ND * TT])
    csre = p5sb("csre", [128, ND * TT])
    csim = p5sb("csim", [128, ND * TT])
    csden = p5sb("csden", [32, TT])
    cden = p5sb("cden", [32, 1])
    cre = p5sb("cre", [128, ND])
    cim = p5sb("cim", [128, ND])
    invden = p5sb("invden", [32, TT])
    rst = p5sb("rst", [32, TT])
    renb = p5sb("renb", [128, ND * TT], BF16)
    imnb = p5sb("imnb", [128, ND * TT], BF16)
    ygb = p5sb("ygb", [128, ND * TT], BF16)
    g1l = p5sb("g1l", [128, ND * TT], BF16)
    outt = p5sb("outt", [128, ND * TT])
    onesf = p5sb("onesf", [128, TT])
    wst5 = p5sb("wst5", [128, ZC])

    for k in range(ND):
        em.step("D", lambda e, k=k: em.dma(e, wst5[0:CH, 0:ZC],
                                           w1_in[k * CH:(k + 1) * CH, :]))
        em.step("S", lambda e, k=k: e.copy(seg(bw1_in, k, ZC), wst5[0:CH, 0:ZC]))
    for k in range(ND):
        em.step("D", lambda e, k=k: em.dma(e, wst5[0:CH, 0:D],
                                           w1_out[k * CH:(k + 1) * CH, :]))
        em.step("S", lambda e, k=k: e.copy(seg(bw1_out, k, D), wst5[0:CH, 0:D]))
    def v_p5init(e):
        e.memset(onesf[:], 1.0)
        e.memset(cden[:], 0.0)
        e.memset(cre[:], 0.0)
        e.memset(cim[:], 0.0)
    em.step("V", v_p5init)

    for it in range(NTILES):
        t0 = it * TT
        seq_start = (it % TPS == 0)
        tloc = (it % TPS) * TT

        def d_p5l(e, t0=t0):
            for c in range(ND):
                em.dma(e, seg(x1l, c, TT), x1_d[c * CH:(c + 1) * CH, t0:t0 + TT])
                em.dma(e, seg(xn1l, c, TT), xn1_d[c * CH:(c + 1) * CH, t0:t0 + TT])
                em.dma(e, seg(g1l, c, TT), g1_d[c * CH:(c + 1) * CH, t0:t0 + TT])
        em.step("D", d_p5l)

        def s_cast5(e):
            for c in range(ND):
                e.copy(seg(xn1b, c, TT), seg(xn1l, c, TT))
        em.step("S", s_cast5)

        for g in range(5):
            def t_z(e, g=g):
                for mm in range(4):
                    m = g * 4 + mm
                    if m >= NZC:
                        continue
                    mw = 32 if m == NZC - 1 else CH
                    for k in range(ND):
                        e.matmul(pb[4 + mm][0:mw, 0:TT],
                                 seg(bw1_in, k, ZC)[:, m * CH:m * CH + mw],
                                 seg(xn1b, k, TT), start=(k == 0), stop=(k == ND - 1))
            em.step("T", t_z)

            def s_z(e, g=g):
                for mm in range(4):
                    m = g * 4 + mm
                    if m >= NZC:
                        continue
                    mw = 32 if m == NZC - 1 else CH
                    e.copy(seg(zrawb, m, TT)[0:mw, :], pb[4 + mm][0:mw, 0:TT])
            em.step("S", s_z)

        def v_conv1(e, seq_start=seq_start):
            for c in range(NZC):
                mw = 32 if c == NZC - 1 else CH
                tl = seg(ztail, c, 3)[0:mw, :]
                xb_ = seg(zrawb, c, TT)[0:mw, :]
                acc = cz[0:mw, :]
                if seq_start:
                    e.tensor_scalar(acc[:, 0:3], xb_[:, 0:3], 0.0, 0.0,
                                    ALU.mult, ALU.add)
                else:
                    e.tensor_scalar(acc[:, 0:3], tl,
                                    col(convw1, c * 4 + 0)[0:mw, :],
                                    0.0, ALU.mult, ALU.add)
                e.tensor_scalar(acc[:, 3:TT], xb_[:, 0:TT - 3],
                                col(convw1, c * 4 + 0)[0:mw, :], 0.0,
                                ALU.mult, ALU.add)
                for j in (1, 2, 3):
                    sh = 3 - j
                    if sh > 0 and not seq_start:
                        e.scalar_tensor_tensor(acc[:, 0:sh], tl[:, j:3],
                                               col(convw1, c * 4 + j)[0:mw, :],
                                               acc[:, 0:sh], ALU.mult, ALU.add)
                    e.scalar_tensor_tensor(acc[:, sh:TT], xb_[:, 0:TT - sh],
                                           col(convw1, c * 4 + j)[0:mw, :],
                                           acc[:, sh:TT], ALU.mult, ALU.add)
                e.tensor_copy(seg(ztail, c, 3)[0:mw, :], xb_[:, TT - 3:TT])
                if c < ND:
                    e.tensor_copy(seg(xval, c, TT), acc)
                elif c < 2 * ND:
                    e.tensor_copy(seg(csre, c - ND, TT), acc)
                else:
                    e.tensor_copy(czs[0:32, :], acc[0:32, :])
        em.step("V", v_conv1)

        def s_xg(e):
            for c in range(ND):
                e.activation(seg(xg1, c, TT), seg(csre, c, TT), AF.Silu)
        em.step("S", s_xg)

        em.step("G", lambda e, tloc=tloc: e.iota(
            post[:], [[1, TT]], base=tloc, channel_multiplier=0,
            allow_small_or_imprecise_dtypes=True))

        def v_pw(e):
            e.tensor_scalar(logw[:], post[:], walpha[:], wbeta[:],
                            ALU.mult, ALU.add)
            e.scalar_tensor_tensor(logw[:], czs[0:32, :], sco1[:], logw[:],
                                   ALU.mult, ALU.add)
        em.step("V", v_pw)
        em.step("S", lambda e: e.activation(pwt[:], logw[:], AF.Exp))
        if "pw" in dbg:
            em.step("D", lambda e, t0=t0: em.dma(
                e, dbg["pw"][:, t0:t0 + TT], pwt[:]))
        if "xv" in dbg:
            def d_xv(e, t0=t0):
                for c in range(ND):
                    em.dma(e, dbg["xv"][c * CH:(c + 1) * CH, t0:t0 + TT],
                           seg(xval, c, TT))
            em.step("D", d_xv)

        for c in range(ND):
            def s_tg1(e, c=c):
                e.activation(aqt[:], seg(xval, c, TT), AF.Abs, bias=0.0,
                             scale=col(th2pi, c))
                e.activation(sgt[:], seg(xval, c, TT), AF.Sign,
                             bias=0.0, scale=col(thcol, c))
                e.activation(phit[:], seg(xval, c, TT), AF.Identity, bias=0.0,
                             scale=col(thcol, c))
            em.step("S", s_tg1)

            def v_tg(e, c=c):
                e.tensor_scalar(aqt[:], aqt[:], 1.0, 0.5, ALU.mult, ALU.add)
                e.tensor_scalar(kqt[:], aqt[:], 1.0, 0.0, ALU.mult, ALU.add)
                e.tensor_scalar(sint[:], kqt[:], 1.0, 0.0, ALU.mult, ALU.add)
                e.tensor_tensor(cost[:], sint[:], aqt[:], ALU.is_gt)
                e.tensor_tensor(sint[:], sint[:], cost[:], ALU.subtract)
                e.tensor_tensor(prt[:], sint[:], sgt[:], ALU.mult)
                e.scalar_tensor_tensor(prt[:], prt[:], -2.0 * PI, phit[:],
                                       ALU.mult, ALU.add)
                e.tensor_scalar(prt[:], prt[:], PI, -PI, ALU.min, ALU.max)
            em.step("V", v_tg)

            def s_tg2(e, c=c):
                e.activation(sint[:], prt[:], AF.Sin)
                e.activation(aqt[:], prt[:], AF.Abs)
            em.step("S", s_tg2)
            em.step("S", lambda e, c=c: e.activation(
                cost[:], aqt[:], AF.Sin, bias=PI / 2, scale=-1.0))

            em.step("T", lambda e, c=c: e.matmul(
                pb[0][:, 0:TT], selinv[:, c * CH:(c + 1) * CH], pwt[:],
                start=True, stop=True))

            def v_num(e, c=c):
                e.tensor_tensor(seg(nret, c, TT), cost[:], pb[0][:, 0:TT], ALU.mult)
                e.tensor_tensor(seg(nimt, c, TT), sint[:], pb[0][:, 0:TT], ALU.mult)
            em.step("V", v_num)

        def v_cs(e, it=it):
            e.tensor_tensor_scan(csden[:], onesf[0:32, :], pwt[:], cden[:],
                                 ALU.mult, ALU.add)
            for c in range(ND):
                e.tensor_tensor_scan(seg(csre, c, TT), onesf[:], seg(nret, c, TT),
                                     col(cre, c), ALU.mult, ALU.add)
                e.tensor_tensor_scan(seg(csim, c, TT), onesf[:], seg(nimt, c, TT),
                                     col(cim, c), ALU.mult, ALU.add)
                if c == 0:
                    e.tensor_copy(cden[:], csden[:, TT - 1:TT])
                else:
                    e.tensor_copy(col(cre, c - 1),
                                  seg(csre, c - 1, TT)[:, TT - 1:TT])
                    e.tensor_copy(col(cim, c - 1),
                                  seg(csim, c - 1, TT)[:, TT - 1:TT])
            e.tensor_scalar_max(invden[:], csden[:], 1e-4)
            e.tensor_copy(col(cre, ND - 1), seg(csre, ND - 1, TT)[:, TT - 1:TT])
            e.tensor_copy(col(cim, ND - 1), seg(csim, ND - 1, TT)[:, TT - 1:TT])
            e.reciprocal(invden[:], invden[:])
            if (it + 1) % TPS == 0 and it + 1 < NTILES:
                e.memset(cden[:], 0.0)
                e.memset(cre[:], 0.0)
                e.memset(cim[:], 0.0)
        em.step("V", v_cs)
        if "idn" in dbg:
            em.step("D", lambda e, t0=t0: em.dma(
                e, dbg["idn"][:, t0:t0 + TT], invden[:]))

        def t_idb(e):
            for c in range(ND):
                bank = pb[4 + c // 2]
                off = (c % 2) * TT
                e.matmul(bank[:, off:off + TT], selinv[:, c * CH:(c + 1) * CH],
                         invden[:], start=True, stop=True)
        em.step("T", t_idb)

        def v_rei(e):
            for c in range(ND):
                bank = pb[4 + c // 2]
                off = (c % 2) * TT
                e.tensor_tensor(seg(nret, c, TT), seg(csre, c, TT),
                                bank[:, off:off + TT], ALU.mult)
                e.tensor_tensor(seg(nimt, c, TT), seg(csim, c, TT),
                                bank[:, off:off + TT], ALU.mult)
        em.step("V", v_rei)

        def s_sq5(e):
            for c in range(ND):
                e.activation(seg(csre, c, TT), seg(nret, c, TT), AF.Square)
                e.activation(seg(csim, c, TT), seg(nimt, c, TT), AF.Square)
        em.step("S", s_sq5)

        def t_ms(e):
            for c in range(ND):
                e.matmul(pb[0][0:32, 0:TT], seg(selsq, c, 32), seg(csre, c, TT),
                         start=(c == 0), stop=False)
            for c in range(ND):
                e.matmul(pb[0][0:32, 0:TT], seg(selsq, c, 32), seg(csim, c, TT),
                         start=False, stop=(c == ND - 1))
        em.step("T", t_ms)
        em.step("S", lambda e: e.activation(rst[:], pb[0][0:32, 0:TT], AF.Sqrt,
                                            bias=1e-5, scale=1.0 / 64))
        em.step("V", lambda e: e.reciprocal(csden[0:32, :], rst[:]))

        def t_rsb(e):
            for c in range(ND):
                bank = pb[4 + c // 2]
                off = (c % 2) * TT
                e.matmul(bank[:, off:off + TT], selinv[:, c * CH:(c + 1) * CH],
                         csden[0:32, :], start=True, stop=True)
        em.step("T", t_rsb)

        def v_ren(e):
            for c in range(ND):
                bank = pb[4 + c // 2]
                off = (c % 2) * TT
                e.scalar_tensor_tensor(seg(renb, c, TT), seg(nret, c, TT),
                                       col(nsre_c, c), bank[:, off:off + TT],
                                       ALU.mult, ALU.mult)
                e.scalar_tensor_tensor(seg(imnb, c, TT), seg(nimt, c, TT),
                                       col(nsim_c, c), bank[:, off:off + TT],
                                       ALU.mult, ALU.mult)
        em.step("V", v_ren)

        for g in range(2):
            def t_y(e, g=g):
                for mm in range(4):
                    c = g * 4 + mm
                    e.matmul(pb[4 + mm][:, 0:TT], bdiag_re[:], seg(renb, c, TT),
                             start=True, stop=False)
                    e.matmul(pb[4 + mm][:, 0:TT], bdiag_im[:], seg(imnb, c, TT),
                             start=False, stop=True)
            em.step("T", t_y)

            def v_yg(e, g=g):
                for mm in range(4):
                    c = g * 4 + mm
                    e.tensor_tensor(seg(ygb, c, TT), pb[4 + mm][:, 0:TT],
                                    seg(xg1, c, TT), ALU.mult)
            em.step("V", v_yg)
            if "yg" in dbg:
                def d_ygt(e, g=g, t0=t0):
                    for mm in range(4):
                        c = g * 4 + mm
                        em.dma(e, dbg["yg"][c * CH:(c + 1) * CH, t0:t0 + TT],
                               seg(ygb, c, TT))
                em.step("D", d_ygt)

        for g in range(2):
            def t_o(e, g=g):
                for mm in range(4):
                    m = g * 4 + mm
                    for k in range(ND):
                        e.matmul(pb[4 + mm][:, 0:TT],
                                 seg(bw1_out, k, D)[:, m * CH:(m + 1) * CH],
                                 seg(ygb, k, TT), start=(k == 0), stop=(k == ND - 1))
            em.step("T", t_o)

            def v_o(e, g=g):
                for mm in range(4):
                    m = g * 4 + mm
                    e.tensor_tensor(cz[:], pb[4 + mm][:, 0:TT], seg(g1l, m, TT),
                                    ALU.mult)
                    e.tensor_tensor(seg(outt, m, TT), seg(x1l, m, TT), cz[:],
                                    ALU.add)
            em.step("V", v_o)

        def d_out(e, t0=t0):
            for c in range(ND):
                em.dma(e, out_fm[c * CH:(c + 1) * CH, t0:t0 + TT], seg(outt, c, TT))
        em.step("D", d_out)

    p5.close()

    em.new_block()
    em.step("D", lambda e: e.wait_ge(em.progD.sem, em.progD.n))
    em.close()
    gs.close()
    return nc


def _selrow_host():
    a = np.zeros((32, 32 * 128), dtype=np.float32)
    for j in range(32):
        a[j, j * 128:(j + 1) * 128] = 1.0
    return a


def _selinv_host():
    a = np.zeros((32, 8 * 128), dtype=np.float32)
    for ci in range(8):
        for p in range(128):
            a[4 * ci + (p >> 5), ci * 128 + p] = 1.0
    return a


def _selsq_host():
    a = np.zeros((128, 8 * 32), dtype=np.float32)
    for ci in range(8):
        for c in range(128):
            a[c, ci * 32 + 4 * ci + (c >> 5)] = 1.0
    return a


def _chunk_cols(v, nch):
    v = np.asarray(v)
    if v.ndim == 1:
        v = v[:, None]
    k = v.shape[1]
    return np.ascontiguousarray(
        v.reshape(nch, 128, k).transpose(1, 0, 2).reshape(128, nch * k))


def prep_inputs(core, A):
    import ml_dtypes
    x = np.asarray(A["L0_attn_x"])
    xc = x[2 * core:2 * core + 2]
    d = {
        "x_fm": np.ascontiguousarray(xc.transpose(2, 0, 1).reshape(D, SEQ * T)),
        "w0_inT": np.ascontiguousarray(np.asarray(A["L0_attn_W_in"]).T),
        "conv_w0": _chunk_cols(A["L0_attn_conv_w"], NDI),
        "conv_b0": _chunk_cols(A["L0_attn_conv_b"], NDI),
        "w0_xprojT": np.ascontiguousarray(np.asarray(A["L0_attn_W_xproj"]).T),
        "w0_dtT": np.ascontiguousarray(np.asarray(A["L0_attn_W_dt"]).T),
        "b_dt0": _chunk_cols(A["L0_attn_b_dt"], NDI),
        "A_log0": _chunk_cols(A["L0_attn_A_log"], NDI),
        "D_skip0": _chunk_cols(A["L0_attn_D_skip"], NDI),
        "w0_outT": np.ascontiguousarray(np.asarray(A["L0_attn_W_out"]).T),
        "clip_x": np.asarray(A["L0_ffn_clip_x"]).reshape(1, 1),
        "clip_down": np.asarray(A["L0_ffn_clip_down"]).reshape(1, 1),
        "wg0": np.ascontiguousarray(A["L0_ffn_w_gate"]),
        "wu0": np.ascontiguousarray(A["L0_ffn_w_up"]),
        "wd0": np.ascontiguousarray(A["L0_ffn_w_down"]),
        "w1_in": np.ascontiguousarray(A["L1_attn_W_in"]),
        "conv_w1": _chunk_cols(
            np.concatenate([np.asarray(A["L1_attn_conv_kernel"]).reshape(4, ZC).T,
                            np.zeros((NZC * 128 - ZC, 4), np.float32)], axis=0), NZC),
        "theta1": _chunk_cols(np.asarray(A["L1_attn_theta"]).reshape(D), ND),
        "slopes1": np.concatenate([np.asarray(A["L1_attn_decay_slopes"]),
                                   np.asarray(A["L1_attn_anchor_slopes"])]
                                  ).reshape(32, 1).astype(np.float32),
        "score1": np.asarray(A["L1_attn_score_scale"]).reshape(32, 1),
        "amask": np.array([1.0] * 28 + [-1.0] * 4, np.float32).reshape(32, 1),
        "dmask": np.array([-float(T - 1)] * 28 + [0.0] * 4,
                          np.float32).reshape(32, 1),
        "w_re": np.ascontiguousarray(A["L1_attn_W_re"]),
        "w_im": np.ascontiguousarray(A["L1_attn_W_im"]),
        "ns_re": _chunk_cols(np.tile(np.asarray(A["L1_attn_norm_scale"])[:32], 32), ND),
        "ns_im": _chunk_cols(np.tile(np.asarray(A["L1_attn_norm_scale"])[32:], 32), ND),
        "w1_out": np.ascontiguousarray(A["L1_attn_W_out"]),
        "wg1": np.ascontiguousarray(A["L1_ffn_w_gate"]),
        "wu1": np.ascontiguousarray(A["L1_ffn_w_up"]),
        "wd1": np.ascontiguousarray(A["L1_ffn_w_down"]),
        "ident": np.eye(128, dtype=np.float32),
        "selrow_b": _selrow_host().astype(ml_dtypes.bfloat16),
        "selinv": _selinv_host(),
        "selsq": _selsq_host(),
        "ones1f": np.ones((1, 128), dtype=np.float32),
        "consts": np.tile(np.array(CONST_VALS + [0.0] * (16 - len(CONST_VALS)),
                                   np.float32), (128, 1)),
    }
    for k in d:
        if d[k].dtype == np.float64:
            d[k] = d[k].astype(np.float32)
    return d


def kernel(**inputs):
    nc = build_program()
    in_maps = [prep_inputs(c, inputs) for c in range(NCORES)]
    res = run_bass_kernel_spmd(nc, in_maps, core_ids=list(range(NCORES)))
    out = np.empty((16, T, D), dtype=np.float32)
    for c in range(NCORES):
        o = np.asarray(res.results[c]["out_fm"])
        out[2 * c:2 * c + 2] = o.reshape(D, SEQ, T).transpose(1, 2, 0)
    return out


if __name__ == "__main__":
    nc = build_program()
    print("built; instructions:", sum(1 for _ in nc.all_instructions()))


# version 58
# speedup vs baseline: 1.0367x; 1.0015x over previous
import contextlib
import numpy as np
import concourse.bass as bass
import concourse.mybir as mybir
from concourse.bass_utils import run_bass_kernel_spmd

F32 = mybir.dt.float32
BF16 = mybir.dt.bfloat16
I32 = mybir.dt.int32
AF = mybir.ActivationFunctionType
ALU = mybir.AluOpType
AX = mybir.AxisListType

NCORES = 8
SEQ = 2
T = 2048
TT = 256
TPS = T // TT
NTILES = SEQ * TPS
D = 1024
DI = 2048
NST = 16
DTR = 64
I0 = 2816
ZC = 2080
CH = 128
ND = D // CH
NDI = DI // CH
NIC = I0 // CH
NZC = 17
PI = float(np.pi)

ENG = {"T": "tensor", "V": "vector", "S": "scalar", "G": "gpsimd", "D": "sync"}
CONST_VALS = [0.0, 1.0, -1.0, 1e-5, 2.0, PI / 2, 1.0 / 64, 0.5, 127.0, PI, -PI,
              256.5, 1e-4, 1e-12]


def cdiv(a, b):
    return (a + b - 1) // b


class Prog:
    def __init__(self, nc, name, step=1):
        self.sem = nc.semaphore(name).__enter__()
        self.n = 0
        self.step = step

    def tick(self, instr, k=None):
        k = self.step if k is None else k
        instr.then_inc(self.sem, k)
        self.n += k
        return instr


class Rec:

    def __init__(self):
        self.calls = []

    def __getattr__(self, name):
        calls = self.calls

        def call(*a, **k):
            entry = [name, a, k, []]
            calls.append(entry)

            class H:
                def then_inc(self, sem, n):
                    entry[3].append((sem, n))
                    return self
            return H()
        return call

    def replay(self, e):
        for name, a, k, chain in self.calls:
            h = getattr(e, name)(*a, **k)
            for sem, n in chain:
                h = h.then_inc(sem, n)


class EM:

    MAXSTEPS = 150
    BATCH_LO = 1371
    BATCH_HI = 10**9

    def __init__(self, nc):
        self.nc = nc
        self.prog = {e: Prog(nc, "prog_" + e) for e in "TVSG"}
        self.progD = Prog(nc, "prog_D", 16)
        self.progW = Prog(nc, "prog_W", 16)
        self.progC = Prog(nc, "prog_C", 1)
        self.last_wait = {e: {} for e in "TVSGD"}
        self.pend = {e: [] for e in "TVSGD"}
        self.npend = 0
        self.afset = set()
        self.ndma = 0
        self.wmap = {}
        self.rmap = {}

    def flush(self):
        if self.npend == 0:
            return
        with self.nc.Block() as b:
            for eng in "DTVSG":
                recs = self.pend[eng]
                if not recs:
                    continue
                dec = getattr(b, ENG[eng])

                @dec
                def _(e, recs=recs):
                    for r in recs:
                        r.replay(e)
        self.pend = {e: [] for e in "TVSGD"}
        self.npend = 0
        self.afset = set()
        self.ndma = 0

    new_block = flush
    close = flush

    @staticmethod
    def _afs(rec):
        s = set()
        for name, a, k, chain in rec.calls:
            if name == "activation":
                s.add((str(a[2]), str(a[0].dtype), str(a[1].dtype)))
            elif name == "copy":
                s.add(("COPY", str(a[0].dtype), str(a[1].dtype)))
        return s

    def marks(self):
        m = {e: self.prog[e].n for e in "TVSG"}
        m["D"] = self.progD.n
        m["C"] = self.progC.n
        return m

    @staticmethod
    def _rw(rec):
        rd, wr = set(), set()

        def isap(x):
            return hasattr(x, "space") and hasattr(x, "dtype")

        for name, a, k, chain in rec.calls:
            if name in ("wait_ge", "sem_inc"):
                continue
            if name == "dma_start":
                out = k.get("out", a[0] if a else None)
                in_ = k.get("in_", a[1] if len(a) > 1 else None)
                if isap(out):
                    wr.add(out.name)
                if isap(in_):
                    rd.add(in_.name)
                continue
            if a and isap(a[0]):
                wr.add(a[0].name)
            for x in a[1:]:
                if isap(x):
                    rd.add(x.name)
            for x in k.values():
                if isap(x):
                    rd.add(x.name)
        return rd, wr

    def _autodeps(self, rd, wr, eng):
        deps = {}

        def need(e2, cnt):
            if cnt <= 0 or (e2 == eng and e2 != "D"):
                return
            if deps.get(e2, 0) < cnt:
                deps[e2] = cnt

        for nm in rd | wr:
            w = self.wmap.get(nm)
            if w:
                need(*w)
        for nm in wr:
            for e2, cnt in self.rmap.get(nm, {}).items():
                need(e2, cnt)
        return deps

    def _track(self, rd, wr, eng):
        cnt = self.progD.n if eng == "D" else self.prog[eng].n
        for nm in wr:
            self.wmap[nm] = (eng, cnt)
            self.rmap[nm] = {}
        for nm in rd:
            self.rmap.setdefault(nm, {})[eng] = cnt

    def step(self, eng, fn, deps=None):
        idx = getattr(self, "stepidx", 0) + 1
        inwin = self.BATCH_LO <= idx < self.BATCH_HI
        marks0 = self.marks()
        r = Rec()
        fn(r)
        rd, wr = self._rw(r)
        selfdep = False
        if deps is None:
            if inwin and eng != "G":
                deps = self._autodeps(rd, wr, eng)
                selfdep = True
            else:
                deps = marks0
        marks = dict(deps)
        w = Rec()
        dwait = False
        for d, cnt in marks.items():
            if cnt <= 0 or (d == eng and not (d == "D" and selfdep)):
                continue
            if self.last_wait[eng].get(d, -1) >= cnt:
                continue
            sem = (self.progD.sem if d == "D"
                   else self.progC.sem if d == "C" else self.prog[d].sem)
            w.wait_ge(sem, cnt)
            self.last_wait[eng][d] = cnt
            if d == "D":
                dwait = True
        if dwait and eng in ("T", "V", "S") and hasattr(self, "flushers"):
            fl = self.flushers.get(eng)
            if fl is not None:
                fl(w)
        r.calls = w.calls + r.calls
        if eng != "D":
            p = self.prog[eng]
            fl = self.flushers.get(eng) if hasattr(self, "flushers") else None
            if fl is not None:
                fl(r)
            tgt = None
            for entry in reversed(r.calls):
                if (entry[0] not in ("wait_ge", "sem_inc", "dma_start")
                        and not entry[3]):
                    tgt = entry
                    break
            if tgt is not None:
                tgt[3].append((p.sem, 1))
            else:
                r.sem_inc(p.sem, 1)
            p.n += 1
        self._track(rd, wr, eng)
        if eng == "S":
            af = self._afs(r)
            if len(self.afset | af) > 5:
                self.flush()
            self.afset |= af
        self.stepidx = getattr(self, "stepidx", 0) + 1
        if eng == "D":
            def big(c):
                if c[0] != "dma_start":
                    return False
                out = c[2].get("out")
                if str(out.space) != "MemorySpace.SBUF":
                    return False
                n = 1
                for s in out.shape[1:]:
                    n *= s
                return n * (2 if "bfloat16" in str(out.dtype) else 4) > 1024
            if any(big(c) for c in r.calls):
                self.flush()
                self.pend[eng].append(r)
                self.npend += 1
                self.flush()
                return
        self.pend[eng].append(r)
        self.npend += 1
        inwin = self.BATCH_LO <= self.stepidx < self.BATCH_HI
        if self.npend >= (self.MAXSTEPS if inwin else 1):
            self.flush()

    def dma(self, e, out, in_):
        self.progD.tick(e.dma_start(out=out, in_=in_))


def build_program(taps=()):
    nc = bass.Bass("TRN2", target_bir_lowering=False, debug=False)
    taps = set(taps)

    def din(name, shape, dt=F32):
        return nc.dram_tensor(name, shape, dt, kind="ExternalInput").ap()

    x_fm = din("x_fm", [D, SEQ * T])
    w0_inT = din("w0_inT", [D, 2 * DI])
    conv_w0 = din("conv_w0", [128, NDI * 4])
    conv_b0 = din("conv_b0", [128, NDI])
    w0_xprojT = din("w0_xprojT", [DI, 96])
    w0_dtT = din("w0_dtT", [DTR, DI])
    b_dt0 = din("b_dt0", [128, NDI])
    A_log0 = din("A_log0", [128, NDI * NST])
    D_skip0 = din("D_skip0", [128, NDI])
    w0_outT = din("w0_outT", [DI, D])
    clip_x = din("clip_x", [1, 1])
    clip_down = din("clip_down", [1, 1])
    wg0 = din("wg0", [I0, D])
    wu0 = din("wu0", [I0, D])
    wd0 = din("wd0", [D, I0])
    w1_in = din("w1_in", [D, ZC])
    conv_w1 = din("conv_w1", [128, NZC * 4])
    theta1 = din("theta1", [128, ND])
    slopes1 = din("slopes1", [32, 1])
    amask_i = din("amask", [32, 1])
    dmask_i = din("dmask", [32, 1])
    score1 = din("score1", [32, 1])
    w_re = din("w_re", [32, 32])
    w_im = din("w_im", [32, 32])
    ns_re = din("ns_re", [128, ND])
    ns_im = din("ns_im", [128, ND])
    w1_out = din("w1_out", [D, D])
    wg1 = din("wg1", [I0, D])
    wu1 = din("wu1", [I0, D])
    wd1 = din("wd1", [D, I0])
    ident_i = din("ident", [128, 128])
    selrow_b_i = din("selrow_b", [32, 32 * 128], BF16)
    selinv_i = din("selinv", [32, 8 * 128])
    selsq_i = din("selsq", [128, 8 * 32])
    ones1f_i = din("ones1f", [1, 128])
    consts_i = din("consts", [128, 16])

    out_fm = nc.dram_tensor("out_fm", [D, SEQ * T], F32, kind="ExternalOutput").ap()
    dbg = {}
    for tname, shape, dt in [("xn", [D, SEQ * T], F32), ("s0", [D, SEQ * T], BF16),
                             ("gmax", [1, 2], F32), ("hmid", [I0, SEQ * T], F32),
                             ("x1", [D, SEQ * T], F32), ("g1", [D, SEQ * T], BF16),
                             ("ym", [DI, SEQ * T], BF16),
                             ("kx", [D, SEQ * T], BF16), ("wqg", [D, I0], BF16),
                             ("swg", [I0, 1], F32), ("gt", [I0, SEQ * T], F32),
                             ("up", [I0, SEQ * T], F32),
                             ("xn2", [D, SEQ * T], F32), ("iq", [128, 2], F32),
                             ("wab", [32, 8], F32), ("pw", [32, SEQ * T], F32),
                             ("xv", [D, SEQ * T], F32), ("yg", [D, SEQ * T], BF16),
                             ("idn", [32, SEQ * T], F32)]:
        if tname in taps:
            dbg[tname] = nc.dram_tensor("dbg_" + tname, shape, dt,
                                        kind="ExternalOutput").ap()

    xn_d = nc.dram_tensor("xn_d", [D, SEQ * T], F32).ap()
    ym_d = nc.dram_tensor("ym_d", [DI, SEQ * T], BF16).ap()
    s0_d = nc.dram_tensor("s0_d", [D, SEQ * T], BF16).ap()
    hmid_d = nc.dram_tensor("hmid_d", [I0, SEQ * T], F32).ap()
    x1_d = nc.dram_tensor("x1_d", [D, SEQ * T], F32).ap()
    xn1_d = nc.dram_tensor("xn1_d", [D, SEQ * T], F32).ap()
    g1_d = nc.dram_tensor("g1_d", [D, SEQ * T], BF16).ap()
    kh1_d = nc.dram_tensor("kh1_d", [SEQ * T, I0], BF16).ap()
    wqgT_d = nc.dram_tensor("wqgT_d", [D, I0], BF16).ap()
    wquT_d = nc.dram_tensor("wquT_d", [D, I0], BF16).ap()
    wqdT_d = nc.dram_tensor("wqdT_d", [I0, D], BF16).ap()
    swg_d = nc.dram_tensor("swg_d", [I0, 1], F32).ap()
    swu_d = nc.dram_tensor("swu_d", [I0, 1], F32).ap()
    swd_d = nc.dram_tensor("swd_d", [D, 1], F32).ap()
    wq4gT_d = nc.dram_tensor("wq4gT_d", [D, I0], BF16).ap()
    wq4uT_d = nc.dram_tensor("wq4uT_d", [D, I0], BF16).ap()
    wq4dT_d = nc.dram_tensor("wq4dT_d", [I0, D], BF16).ap()
    cc_in = nc.dram_tensor("cc_in", [1, 1], F32).ap()
    cc_out = nc.dram_tensor("cc_out", [1, 1], F32).ap()
    cc_in2 = nc.dram_tensor("cc_in2", [1, 1], F32).ap()
    cc_out2 = nc.dram_tensor("cc_out2", [1, 1], F32).ap()

    gs = contextlib.ExitStack()

    def g_sb(name, shape, dt=F32):
        return gs.enter_context(nc.sbuf_tensor(name, shape, dt))

    sident = g_sb("sident", [128, 128])
    sidentb = g_sb("sidentb", [128, 128], BF16)
    s1f = g_sb("s1f", [1, 128])
    selrow_b = g_sb("sb_selrow_b", [32, 32 * 128], BF16)
    selinv = g_sb("sb_selinv", [32, 8 * 128])
    selsq = g_sb("sb_selsq", [128, 8 * 32])
    convw0 = g_sb("convw0", [128, NDI * 4])
    convb0 = g_sb("convb0", [128, NDI])
    bdt0 = g_sb("bdt0", [128, NDI])
    negA = g_sb("negA", [128, NDI * NST])
    dskip = g_sb("dskip", [128, NDI])
    sclipx = g_sb("sclipx", [1, 1])
    sclipd = g_sb("sclipd", [1, 1])
    convw1 = g_sb("convw1", [128, NZC * 4])
    thcol = g_sb("thcol", [128, ND])
    th2pi = g_sb("th2pi", [128, ND])
    nsre_c = g_sb("nsre_c", [128, ND])
    nsim_c = g_sb("nsim_c", [128, ND])
    spd = g_sb("spd", [32, 1])
    nspd = g_sb("nspd", [32, 1])
    amask = g_sb("amask_s", [32, 1])
    dmask = g_sb("dmask_s", [32, 1])
    walpha = g_sb("walpha", [32, 1])
    wbeta = g_sb("wbeta", [32, 1])
    sco1 = g_sb("sco1", [32, 1])
    sl_a = g_sb("sl_a", [32, 1])
    sl_b = g_sb("sl_b", [32, 1])
    bdiag_re = g_sb("bdiag_re", [128, 128], BF16)
    bdiag_im = g_sb("bdiag_im", [128, 128], BF16)
    wre_st = g_sb("wre_st", [128, 32])
    wim_st = g_sb("wim_st", [128, 32])
    mx1buf = g_sb("mx1buf", [128, NTILES * ND])
    mx2buf = g_sb("mx2buf", [128, NIC * NTILES])
    vpad = g_sb("vpad", [128, 512])
    spad = g_sb("spad", [128, 512])
    iqst = g_sb("iqst", [128, 2])
    wabst = g_sb("wabst", [32, 8])
    hcarry = g_sb("hcarry", [128, NDI * NST])
    gmax1 = g_sb("gmax1", [1, 1])
    gmax2 = g_sb("gmax2", [1, 1])
    invq1 = g_sb("invq1", [128, 1])
    invq2 = g_sb("invq2", [128, 1])
    sxcol = g_sb("sxcol", [128, 1])
    shcol = g_sb("shcol", [128, 1])
    sq1 = g_sb("sq1", [1, 2])
    sq2 = g_sb("sq2", [1, 2])
    onescol = g_sb("onescol", [128, 1])
    mrow = g_sb("mrow", [1, TT])
    rrow = g_sb("rrow", [1, TT])
    qrow = g_sb("qrow", [1, TT])
    wsc_mx = g_sb("wsc_mx", [128, 1])

    sconsts = g_sb("sconsts", [128, 16])

    pb = [gs.enter_context(nc.psum_tensor(f"pb{i}", [128, 512], F32))
          for i in range(8)]

    em = EM(nc)
    em.flushers = {
        "V": lambda e: e.tensor_scalar_mul(vpad[:], vpad[:], 1.0),
        "S": lambda e: e.copy(spad[:], spad[:]),
        "T": lambda e: e.matmul(pb[2][0:1, 448:449], sident[0:1, 0:1],
                                sident[0:1, 0:1], start=True, stop=True),
    }
    for i, v in enumerate(CONST_VALS):
        nc.const_aps.register(F32, float(v), sconsts[:, i:i + 1])

    def seg(buf, c, W):
        return buf[:, c * W:(c + 1) * W]

    def col(buf, c):
        return buf[:, c:c + 1]

    def sep(e):
        e.tensor_scalar_mul(vpad[:], vpad[:], 1.0)

    def d_setup(e):
        for dst, src in [(sident, ident_i), (s1f, ones1f_i), (sconsts, consts_i),
                         (selrow_b, selrow_b_i), (selinv, selinv_i),
                         (selsq, selsq_i), (convw0, conv_w0), (convb0, conv_b0),
                         (bdt0, b_dt0), (negA, A_log0), (dskip, D_skip0),
                         (sclipx, clip_x), (sclipd, clip_down),
                         (convw1, conv_w1), (thcol, theta1), (spd, slopes1),
                         (amask, amask_i), (dmask, dmask_i),
                         (sco1, score1), (nsre_c, ns_re), (nsim_c, ns_im)]:
            em.dma(e, dst[:], src)
        for j in range(4):
            em.dma(e, wre_st[32 * j:32 * j + 32, :], w_re)
            em.dma(e, wim_st[32 * j:32 * j + 32, :], w_im)
    em.step("D", d_setup)

    def s_setup(e):
        if "wab" in dbg:
            e.copy(wabst[:, 6:7], spd[:])
        e.copy(sidentb[:], sident[:])
        e.activation(negA[:], negA[:], AF.Exp)
        e.activation(sl_a[:], spd[:], AF.Abs)
        e.activation(sl_a[:], sl_a[:], AF.Exp, bias=0.0, scale=-1.0)
        e.activation(sl_a[:], sl_a[:], AF.Ln, bias=1.0, scale=1.0)
        e.activation(sl_b[:], spd[:], AF.Relu)
    em.step("S", s_setup)

    def v_slopes(e):
        e.tensor_tensor(nspd[:], sl_a[:], sl_b[:], ALU.add)
        sep(e)
        e.tensor_tensor(walpha[:], nspd[:], amask[:], ALU.mult)
        e.tensor_tensor(wbeta[:], nspd[:], dmask[:], ALU.mult)
        sep(e)
    em.step("V", v_slopes)
    if "wab" in dbg:
        def s_wab(e):
            e.copy(wabst[:, 0:1], spd[:])
            e.copy(wabst[:, 1:2], nspd[:])
            e.copy(wabst[:, 2:3], walpha[:])
            e.copy(wabst[:, 3:4], wbeta[:])
            e.copy(wabst[:, 4:5], sl_a[:])
            e.copy(wabst[:, 5:6], sl_b[:])
            e.copy(wabst[:, 7:8], amask[:])
        em.step("S", s_wab)
        em.step("D", lambda e: em.dma(e, dbg["wab"], wabst[:]))

    def v_setup(e):
        e.tensor_scalar_mul(negA[:], negA[:], -1.0)
        e.tensor_scalar_mul(th2pi[:], thcol[:], 1.0 / (2.0 * PI))
        e.memset(vpad[:], 0.0)
        e.memset(hcarry[:], 0.0)
        e.memset(bdiag_re[:], 0.0)
        e.memset(bdiag_im[:], 0.0)
        e.memset(onescol[:], 1.0)
    em.step("V", v_setup)

    def s_bdiag(e):
        for j in range(4):
            e.copy(bdiag_re[32 * j:32 * j + 32, 32 * j:32 * j + 32],
                   wre_st[32 * j:32 * j + 32, :])
            e.copy(bdiag_im[32 * j:32 * j + 32, 32 * j:32 * j + 32],
                   wim_st[32 * j:32 * j + 32, :])
    em.step("S", s_bdiag)

    em.new_block()
    wb = contextlib.ExitStack()

    def wb_sb(name, shape, dt=BF16):
        return wb.enter_context(nc.sbuf_tensor(name, shape, dt))

    bw_in = wb_sb("bw_in", [128, ND * (2 * DI)])
    bw_xp = wb_sb("bw_xp", [128, NDI * 96])
    bw_dt = wb_sb("bw_dt", [DTR, DI])
    ws = contextlib.ExitStack()

    def ws_sb(name, shape, dt=F32):
        return ws.enter_context(nc.sbuf_tensor(name, shape, dt))

    wstage = ws_sb("wstage", [128, 4096])
    wrow = ws_sb("wrow", [128, I0], BF16)
    wsc = ws_sb("wsc", [128, 1])
    wrec = ws_sb("wrec", [128, 1])
    wint = ws_sb("wint", [128, I0], I32)
    wtmp = ws_sb("wtmp", [128, I0])
    wtm2 = ws_sb("wtm2", [128, I0])
    wtm3 = ws_sb("wtm3", [128, I0])
    wq4sc = ws_sb("wq4sc", [128, I0 // 16])
    wq4rc = ws_sb("wq4rc", [128, I0 // 16])

    def cast_weight(dst, src, R, C):
        for c in range(cdiv(R, CH)):
            n = min(CH, R - c * CH)
            em.step("D", lambda e, c=c, n=n: em.dma(
                e, wstage[0:n, 0:C], src[c * CH:c * CH + n, :]))
            em.step("S", lambda e, c=c, n=n: e.copy(
                seg(dst, c, C)[0:n, :], wstage[0:n, 0:C]))

    cast_weight(bw_in, w0_inT, D, 2 * DI)
    cast_weight(bw_xp, w0_xprojT, DI, 96)

    def d_bwdt(e):
        em.dma(e, wstage[0:DTR, 0:DI], w0_dtT)
    em.step("D", d_bwdt)
    em.step("S", lambda e: e.copy(bw_dt[:], wstage[0:DTR, 0:DI]))

    def transpose_store(srcf, n, C, dst_d, c):
        for j0 in range(0, cdiv(C, CH), 2):
            def tstep(e, j0=j0):
                for j in range(j0, min(j0 + 2, cdiv(C, CH))):
                    m = min(CH, C - j * CH)
                    e.transpose(pb[j % 2][0:m, 0:n],
                                srcf[0:n, j * CH:j * CH + m], sident[0:n, 0:n])
                e.matmul(pb[j0 % 2][0:1, 400:401], sident[0:1, 0:1],
                         sident[0:1, 0:1], start=True, stop=True)
            em.step("T", tstep)

            def sstep(e, j0=j0):
                for j in range(j0, min(j0 + 2, cdiv(C, CH))):
                    m = min(CH, C - j * CH)
                    e.copy(wrow[0:m, 0:n], pb[j % 2][0:m, 0:n])
                    em.progW.tick(e.dma_start(
                        out=dst_d[j * CH:j * CH + m, c * CH:c * CH + n],
                        in_=wrow[0:m, 0:n]))
                    e.wait_ge(em.progW.sem, em.progW.n)
            em.step("S", sstep)

    def quant_int8_weight(src, R, C, dstT_d, scale_d):
        for c in range(cdiv(R, CH)):
            n = min(CH, R - c * CH)
            em.step("D", lambda e, c=c, n=n: em.dma(
                e, wstage[0:n, 0:C], src[c * CH:c * CH + n, :]))

            def v1(e, c=c, n=n):
                e.tensor_reduce(wsc[0:n, :], wstage[0:n, 0:C], AX.X, ALU.max,
                                apply_absolute_value=True)
                sep(e)
                e.reciprocal(wrec[0:n, :], wsc[0:n, :])
                e.tensor_scalar_mul(wsc[0:n, :], wsc[0:n, :], 1.0 / 127.0)
                sep(e)
                e.tensor_scalar(wtmp[0:n, 0:C], wstage[0:n, 0:C],
                                wrec[0:n, 0:1], 127.0, ALU.mult, ALU.mult)
                e.tensor_scalar(wtmp[0:n, 0:C], wtmp[0:n, 0:C], 1.0, 0.5,
                                ALU.mult, ALU.add)
                e.tensor_scalar(wint[0:n, 0:C], wtmp[0:n, 0:C], 1.0, 0.0,
                                ALU.mult, ALU.add)
                e.tensor_scalar(wtm3[0:n, 0:C], wint[0:n, 0:C], 1.0, 0.0,
                                ALU.mult, ALU.add)
                e.tensor_tensor(wtm2[0:n, 0:C], wtm3[0:n, 0:C], wtmp[0:n, 0:C],
                                ALU.is_gt)
                e.tensor_tensor(wtm3[0:n, 0:C], wtm3[0:n, 0:C], wtm2[0:n, 0:C],
                                ALU.subtract)
                e.tensor_scalar(wtm2[0:n, 0:C], wtm3[0:n, 0:C], 127.0, -128.0,
                                ALU.min, ALU.max)
            em.step("V", v1)
            em.step("D", lambda e, c=c, n=n: em.dma(
                e, scale_d[c * CH:c * CH + n, :], wsc[0:n, :]))
            transpose_store(wtm2, n, C, dstT_d, c)

    quant_int8_weight(wg0, I0, D, wqgT_d, swg_d)
    if "wqg" in dbg:
        em.step("D", lambda e: em.dma(e, dbg["wqg"], wqgT_d))
        em.step("D", lambda e: em.dma(e, dbg["swg"], swg_d))
    quant_int8_weight(wu0, I0, D, wquT_d, swu_d)
    quant_int8_weight(wd0, D, I0, wqdT_d, swd_d)

    def fp4_core(e, zb, n, C, scb, rcb, ib, sgn, ab, fb, outb):
        nb = C // 16
        z = zb[0:n, 0:C]
        sg = sgn[0:n, 0:C]
        a = ab[0:n, 0:C]
        f = fb[0:n, 0:C]
        ii = ib[0:n, 0:C]
        e.tensor_reduce(scb[0:n, 0:nb].rearrange("p (g o) -> p g o", o=1),
                        z.rearrange("p (g k) -> p g k", k=16),
                        AX.X, ALU.max, apply_absolute_value=True)
        e.tensor_single_scalar(sg, z, 0.0, ALU.is_gt)
        e.tensor_scalar(sg, sg, 2.0, -1.0, ALU.mult, ALU.add)
        e.tensor_scalar(scb[0:n, 0:nb], scb[0:n, 0:nb], 1.0 / 6.0, 1e-12,
                        ALU.mult, ALU.max)
        e.tensor_scalar_mul(a, z, -1.0)
        e.tensor_tensor(a, a, z, ALU.max)
        e.reciprocal(rcb[0:n, 0:nb], scb[0:n, 0:nb])
        e.tensor_scalar_mul(a, a, 2.0)
        rb = rcb[0:n, 0:nb].unsqueeze(2).broadcast_to([n, nb, 16])
        e.tensor_tensor(a.rearrange("p (g k) -> p g k", k=16),
                        a.rearrange("p (g k) -> p g k", k=16), rb, ALU.mult)
        e.tensor_scalar(ii, a, 1.0, 0.0, ALU.mult, ALU.add)
        e.tensor_scalar(f, ii, 1.0, 0.0, ALU.mult, ALU.add)
        e.tensor_tensor(z, a, f, ALU.subtract)
        e.tensor_single_scalar(z, z, 0.5, ALU.is_gt)
        e.tensor_tensor(f, f, z, ALU.add)
        e.tensor_scalar(f, f, 0.5, 2.0, ALU.mult, ALU.min)
        e.tensor_single_scalar(z, a, 10.0, ALU.is_gt)
        e.scalar_tensor_tensor(f, z, 2.0, f, ALU.mult, ALU.add)
        e.tensor_scalar_mul(a, a, 0.5)
        e.tensor_scalar(ii, a, 1.0, 0.0, ALU.mult, ALU.add)
        e.tensor_scalar(z, ii, 1.0, 0.0, ALU.mult, ALU.add)
        e.tensor_tensor(a, a, z, ALU.subtract)
        e.tensor_single_scalar(a, a, 0.5, ALU.is_gt)
        e.tensor_scalar(z, z, 1.0, -2.0, ALU.mult, ALU.add)
        e.tensor_tensor(z, z, a, ALU.add)
        e.tensor_scalar(z, z, 0.0, 2.0, ALU.max, ALU.min)
        e.tensor_tensor(f, f, z, ALU.add)
        e.tensor_tensor(f, f, sg, ALU.mult)
        sb_ = scb[0:n, 0:nb].unsqueeze(2).broadcast_to([n, nb, 16])
        e.tensor_tensor(outb[0:n, 0:C].rearrange("p (g k) -> p g k", k=16),
                        f.rearrange("p (g k) -> p g k", k=16), sb_, ALU.mult)

    def quant_fp4_weight(src, R, C, dstT_d):
        for c in range(cdiv(R, CH)):
            n = min(CH, R - c * CH)
            em.step("D", lambda e, c=c, n=n: em.dma(
                e, wstage[0:n, 0:C], src[c * CH:c * CH + n, :]))
            em.step("V", lambda e, c=c, n=n: fp4_core(
                e, wstage, n, C, wq4sc, wq4rc, wint, wtmp, wtm2, wtm3, wtm3))
            transpose_store(wtm3, n, C, dstT_d, c)

    quant_fp4_weight(wg1, I0, D, wq4gT_d)
    quant_fp4_weight(wu1, I0, D, wq4uT_d)
    quant_fp4_weight(wd1, D, I0, wq4dT_d)

    ws.close()

    def emit_ln(srcbuf, dstbuf, sqbuf, it_for_max=None, mxbuf=None, dst_bf=None):
        def s_sq(e):
            for c in range(ND):
                e.activation(seg(sqbuf, c, TT), seg(srcbuf, c, TT), AF.Square)
        em.step("S", s_sq)

        def t_sums(e):
            for c in range(ND):
                e.matmul(pb[0][0:1, 0:TT], onescol[:], seg(srcbuf, c, TT),
                         start=(c == 0), stop=(c == ND - 1))
            for c in range(ND):
                e.matmul(pb[1][0:1, 0:TT], onescol[:], seg(sqbuf, c, TT),
                         start=(c == 0), stop=(c == ND - 1))
        em.step("T", t_sums)

        def v_stats(e):
            e.tensor_scalar_mul(mrow[:], pb[0][0:1, 0:TT], 1.0 / D)
            e.tensor_scalar_mul(rrow[:], pb[1][0:1, 0:TT], 1.0 / D)
            e.tensor_tensor(qrow[:], mrow[:], mrow[:], ALU.mult)
            e.tensor_tensor(rrow[:], rrow[:], qrow[:], ALU.subtract)
        em.step("V", v_stats)
        em.step("S", lambda e: e.activation(rrow[:], rrow[:], AF.Sqrt,
                                            bias=1e-5, scale=1.0))

        def v_recip(e):
            e.reciprocal(qrow[:], rrow[:])
            e.tensor_tensor(mrow[:], mrow[:], qrow[:], ALU.mult)
        em.step("V", v_recip)

        def t_bcast(e):
            e.matmul(pb[2][:, 0:TT], s1f[:], qrow[:], start=True, stop=True)
            e.matmul(pb[3][:, 0:TT], s1f[:], mrow[:], start=True, stop=True)
        em.step("T", t_bcast)

        def v_xn(e):
            W = ND * TT
            e.tensor_tensor(
                dstbuf[:, 0:W].rearrange("p (c t) -> p c t", t=TT),
                srcbuf[:, 0:W].rearrange("p (c t) -> p c t", t=TT),
                pb[2][:, 0:TT].unsqueeze(1).broadcast_to([128, ND, TT]),
                ALU.mult)
            e.tensor_tensor(
                dstbuf[:, 0:W].rearrange("p (c t) -> p c t", t=TT),
                dstbuf[:, 0:W].rearrange("p (c t) -> p c t", t=TT),
                pb[3][:, 0:TT].unsqueeze(1).broadcast_to([128, ND, TT]),
                ALU.subtract)
            if it_for_max is not None:
                e.tensor_reduce(
                    mxbuf[:, it_for_max * ND:(it_for_max + 1) * ND]
                    .rearrange("p (c o) -> p c o", o=1),
                    dstbuf[:, 0:W].rearrange("p (c t) -> p c t", t=TT),
                    AX.X, ALU.max, apply_absolute_value=True)
        em.step("V", v_xn)
        if dst_bf is not None:
            def s_cast(e):
                for c in range(ND):
                    e.copy(seg(dst_bf, c, TT), seg(dstbuf, c, TT))
            em.step("S", s_cast)

    em.new_block()
    p1 = contextlib.ExitStack()

    def p1sb(name, shape, dt=F32):
        return p1.enter_context(nc.sbuf_tensor(name, shape, dt))

    xpool = p1sb("xpool", [128, ND * TT])
    xt = xpool
    xinbuf = xpool.bitcast(BF16)
    XW = TT + 3
    xpad = p1sb("xpad", [128, 24])
    xnt = p1sb("xnt", [128, ND * TT])
    spa = xnt
    xnb = p1sb("xnb", [128, ND * TT], BF16)
    cacc = p1sb("cacc", [128, NDI * TT])
    ubuf = p1sb("ubuf", [128, NDI * TT], BF16)
    zraw = p1sb("zraw", [128, NDI * TT], BF16)
    dtlo_b = p1sb("dtlo_b", [DTR, TT], BF16)
    bcrow = p1sb("bcrow", [32, TT], BF16)
    Bbc = p1sb("Bbc", [128, NST * TT], BF16)
    Cbc = p1sb("Cbc", [128, NST * TT], BF16)
    dAb = p1sb("dAb", [128, NST * TT], BF16)
    dBb = p1sb("dBb", [128, NST * TT], BF16)
    hbb = p1sb("hbb", [128, NST * TT], BF16)
    dub = p1sb("dub", [128, TT], BF16)
    szk = p1sb("szk", [128, TT], BF16)
    ybuf = p1sb("ybuf", [128, TT])
    ymb = p1sb("ymb", [128, NDI * TT], BF16)
    dtf = cacc
    xtail = p1sb("xtail", [128, NDI * 3], BF16)

    def xin(c):
        return xinbuf[:, c * TT:(c + 1) * TT]

    for it in range(NTILES):
        t0 = it * TT
        seq_start = (it % TPS == 0)

        def d_x(e, t0=t0):
            for c in range(ND):
                em.dma(e, seg(xt, c, TT), x_fm[c * CH:(c + 1) * CH, t0:t0 + TT])
        em.step("D", d_x)

        emit_ln(xt, xnt, cacc, it_for_max=it, mxbuf=mx1buf, dst_bf=xnb)

        def d_xn(e, t0=t0):
            for c in range(ND):
                em.dma(e, xn_d[c * CH:(c + 1) * CH, t0:t0 + TT], seg(xnt, c, TT))
                if "xn" in dbg:
                    em.dma(e, dbg["xn"][c * CH:(c + 1) * CH, t0:t0 + TT],
                           seg(xnt, c, TT))
        em.step("D", d_xn)

        for g in range(8):
            def t_xz(e, g=g):
                for mm in range(4):
                    m = g * 4 + mm
                    for k in range(ND):
                        e.matmul(pb[4 + mm][:, 0:TT],
                                 seg(bw_in, k, 2 * DI)[:, m * CH:(m + 1) * CH],
                                 seg(xnb, k, TT), start=(k == 0), stop=(k == ND - 1))
            em.step("T", t_xz)

            def s_xz(e, g=g):
                for mm in range(4):
                    m = g * 4 + mm
                    if m < NDI:
                        e.copy(xin(m), pb[4 + mm][:, 0:TT])
                    else:
                        e.copy(seg(zraw, m - NDI, TT), pb[4 + mm][:, 0:TT])
            em.step("S", s_xz)

        def v_conv(e, seq_start=seq_start):
            for c in range(NDI):
                tl = seg(xtail, c, 3)
                xb_ = xin(c)
                if seq_start:
                    e.tensor_scalar(seg(cacc, c, TT)[:, 0:3], xb_[:, 0:3],
                                    0.0, col(convb0, c), ALU.mult, ALU.add)
                else:
                    e.tensor_scalar(seg(cacc, c, TT)[:, 0:3], tl,
                                    col(convw0, c * 4 + 0), col(convb0, c),
                                    ALU.mult, ALU.add)
                e.tensor_scalar(seg(cacc, c, TT)[:, 3:TT], xb_[:, 0:TT - 3],
                                col(convw0, c * 4 + 0), col(convb0, c),
                                ALU.mult, ALU.add)
                for j in (1, 2, 3):
                    sh = 3 - j
                    if sh > 0 and not seq_start:
                        e.scalar_tensor_tensor(seg(cacc, c, TT)[:, 0:sh],
                                               tl[:, j:3], col(convw0, c * 4 + j),
                                               seg(cacc, c, TT)[:, 0:sh],
                                               ALU.mult, ALU.add)
                    e.scalar_tensor_tensor(seg(cacc, c, TT)[:, sh:TT],
                                           xb_[:, 0:TT - sh], col(convw0, c * 4 + j),
                                           seg(cacc, c, TT)[:, sh:TT],
                                           ALU.mult, ALU.add)
            for c in range(NDI):
                e.tensor_copy(seg(xtail, c, 3), xin(c)[:, TT - 3:TT])
        em.step("V", v_conv)

        def s_silu_u(e):
            for c in range(NDI):
                e.activation(seg(ubuf, c, TT), seg(cacc, c, TT), AF.Silu)
        em.step("S", s_silu_u)

        def t_dbc(e):
            for k in range(NDI):
                e.matmul(pb[0][0:64, 0:TT], seg(bw_xp, k, 96)[:, 0:64],
                         seg(ubuf, k, TT), start=(k == 0), stop=(k == NDI - 1))
            for k in range(NDI):
                e.matmul(pb[1][0:32, 0:TT], seg(bw_xp, k, 96)[:, 64:96],
                         seg(ubuf, k, TT), start=(k == 0), stop=(k == NDI - 1))
        em.step("T", t_dbc)

        def s_dbc(e):
            e.copy(dtlo_b[:], pb[0][0:64, 0:TT])
            e.copy(bcrow[:], pb[1][0:32, 0:TT])
        em.step("S", s_dbc)

        for g in range(4):
            def t_bc(e, g=g):
                for q in range(8):
                    n8 = g * 8 + q
                    bank = pb[4 + q // 2]
                    off = (q % 2) * TT
                    e.matmul(bank[:, off:off + TT],
                             selrow_b[:, n8 * 128:(n8 + 1) * 128],
                             bcrow[:], start=True, stop=True)
            em.step("T", t_bc)

            def s_bc(e, g=g):
                for q in range(8):
                    n8 = g * 8 + q
                    bank = pb[4 + q // 2]
                    off = (q % 2) * TT
                    dst = Bbc if n8 < 16 else Cbc
                    e.copy(seg(dst, n8 % 16, TT), bank[:, off:off + TT])
            em.step("S", s_bc)

        for g in range(4):
            def t_dt(e, g=g):
                for q in range(4):
                    c = g * 4 + q
                    e.matmul(pb[4 + q][:, 0:TT], bw_dt[:, c * CH:(c + 1) * CH],
                             dtlo_b[:], start=True, stop=True)
            em.step("T", t_dt)

            def s_dt(e, g=g):
                for q in range(4):
                    c = g * 4 + q
                    e.copy(seg(cacc, g * 4 + q, TT), pb[4 + q][:, 0:TT])
            em.step("S", s_dt)

        for g in range(4):
            def s_sp(e, g=g):
                for q in range(4):
                    c = g * 4 + q
                    e.activation(seg(spa, q, TT), seg(cacc, c, TT), AF.Abs,
                                 bias=col(bdt0, c), scale=1.0)
                    e.activation(seg(spa, q, TT), seg(spa, q, TT), AF.Exp,
                                 bias=0.0, scale=-1.0)
                    e.activation(seg(spa, q, TT), seg(spa, q, TT), AF.Ln,
                                 bias=1.0, scale=1.0)
                    e.activation(seg(spa, 4 + q, TT), seg(cacc, c, TT), AF.Relu,
                                 bias=col(bdt0, c), scale=1.0)
            em.step("S", s_sp)

            def v_sp(e, g=g):
                for q in range(4):
                    c = g * 4 + q
                    e.tensor_tensor(seg(dtf, c, TT), seg(spa, q, TT),
                                    seg(spa, 4 + q, TT), ALU.add)
            em.step("V", v_sp)

        if seq_start:
            em.step("V", lambda e: e.memset(hcarry[:], 0.0))

        for c in range(NDI):
            def v_dAr(e, c=c):
                e.tensor_tensor(
                    dAb[:].rearrange("p (n t) -> p n t", t=TT),
                    seg(dtf, c, TT).unsqueeze(1).broadcast_to([128, NST, TT]),
                    negA[:, c * NST:(c + 1) * NST].unsqueeze(2)
                    .broadcast_to([128, NST, TT]),
                    ALU.mult)
            em.step("V", v_dAr)

            def s_dA(e, c=c):
                e.activation(dAb[:], dAb[:], AF.Exp)
                e.activation(szk[:], seg(zraw, c, TT), AF.Silu)
            em.step("S", s_dA)

            def v_pre(e, c=c):
                e.tensor_tensor(dub[:], seg(dtf, c, TT), seg(ubuf, c, TT), ALU.mult)
                e.tensor_tensor(
                    dBb[:].rearrange("p (n t) -> p n t", t=TT),
                    dub[:].unsqueeze(1).broadcast_to([128, NST, TT]),
                    Bbc[:].rearrange("p (n t) -> p n t", t=TT),
                    ALU.mult)
            em.step("V", v_pre)

            def v_scan(e, c=c):
                for n in range(NST):
                    e.tensor_tensor_scan(seg(hbb, n, TT), seg(dAb, n, TT),
                                         seg(dBb, n, TT),
                                         col(hcarry, c * NST + n),
                                         ALU.mult, ALU.add)
                e.tensor_tensor(dBb[:], hbb[:], Cbc[:], ALU.mult)
                e.tensor_copy(hcarry[:, c * NST:(c + 1) * NST],
                              hbb[:].rearrange("p (n t) -> p n t", t=TT)[:, :, TT - 1])
                e.tensor_reduce(ybuf[:].rearrange("p (t o) -> p t o", o=1),
                                dBb[:].rearrange("p (n t) -> p t n", t=TT),
                                AX.X, ALU.add)
                e.scalar_tensor_tensor(ybuf[:], seg(ubuf, c, TT), col(dskip, c),
                                       ybuf[:], ALU.mult, ALU.add)
                e.tensor_tensor(seg(ymb, c, TT), ybuf[:], szk[:], ALU.mult)
            em.step("V", v_scan)

        def d_ym(e, t0=t0):
            for c in range(NDI):
                em.dma(e, ym_d[c * CH:(c + 1) * CH, t0:t0 + TT], seg(ymb, c, TT))
                if "ym" in dbg:
                    em.dma(e, dbg["ym"][c * CH:(c + 1) * CH, t0:t0 + TT],
                           seg(ymb, c, TT))
        em.step("D", d_ym)

    em.new_block()
    def v_gm1(e):
        sep(e)
        e.tensor_reduce(wsc_mx[:], mx1buf[:], AX.X, ALU.max)
    em.step("V", v_gm1)
    em.step("T", lambda e: e.matmul(pb[0][0:1, 0:128], wsc_mx[:], sident[:],
                                    start=True, stop=True))
    em.step("V", lambda e: e.tensor_reduce(gmax1[:], pb[0][0:1, 0:128],
                                           AX.X, ALU.max))
    em.step("D", lambda e: em.dma(e, cc_in, gmax1[:]))

    def g_cc1(e):
        em.progC.tick(e.collective_compute(
            "AllReduce", ALU.max, replica_groups=[list(range(NCORES))],
            ins=[cc_in.opt()], outs=[cc_out.opt()]))
    em.new_block()
    em.step("G", g_cc1)
    em.new_block()
    em.step("D", lambda e: em.dma(e, gmax1[:], cc_out), deps=em.marks())
    em.step("S", lambda e: e.activation(sq1[:, 0:1], sclipx[:], AF.Sigmoid))

    def v_sq1(e):
        e.tensor_tensor(sq1[:, 0:1], sq1[:, 0:1], gmax1[:], ALU.mult)
        sep(e)
        e.reciprocal(sq1[:, 1:2], sq1[:, 0:1])
        sep(e)
        e.tensor_scalar_mul(sq1[:, 1:2], sq1[:, 1:2], 127.0)
        e.tensor_scalar_mul(sq1[:, 0:1], sq1[:, 0:1], 1.0 / 127.0)
        sep(e)
    em.step("V", v_sq1)
    em.step("T", lambda e: e.matmul(pb[0][:, 0:2], s1f[:], sq1[:, 0:2],
                                    start=True, stop=True))

    def v_iq1(e):
        e.tensor_copy(sxcol[:], pb[0][:, 0:1])
        e.tensor_copy(invq1[:], pb[0][:, 1:2])
        sep(e)
    em.step("V", v_iq1)
    if "gmax" in dbg:
        em.step("D", lambda e: em.dma(e, dbg["gmax"][:, 0:1], gmax1[:]))
    if "iq" in dbg:
        def v_iqt(e):
            e.tensor_copy(iqst[:, 0:1], sxcol[:])
            e.tensor_copy(iqst[:, 1:2], invq1[:])
            sep(e)
        em.step("V", v_iqt)
        em.step("D", lambda e: em.dma(e, dbg["iq"], iqst[:]))

    p1.close()
    wb.close()

    em.new_block()
    p2 = contextlib.ExitStack()

    def p2sb(name, shape, dt=F32):
        return p2.enter_context(nc.sbuf_tensor(name, shape, dt))

    wqgT = p2sb("wqgT", [128, ND * I0], BF16)
    wquT = p2sb("wquT", [128, ND * I0], BF16)
    bw_out2 = p2sb("bw_out2", [128, NDI * D], BF16)
    swgc = p2sb("swgc", [128, NIC])
    swuc = p2sb("swuc", [128, NIC])
    scolg = p2sb("scolg", [128, NIC])
    scolu = p2sb("scolu", [128, NIC])
    xnt2 = p2sb("xnt2", [128, ND * TT])
    kxb = p2sb("kxb", [128, ND * TT], BF16)
    qi2 = p2sb("qi2", [128, TT], I32)
    qt2 = p2sb("qt2", [128, TT])
    qv2 = p2sb("qv2", [128, TT])
    qn2 = p2sb("qn2", [128, TT])
    hsil = p2sb("hsil", [128, TT])
    hm2 = p2sb("hm2", [128, 2 * TT])
    ym2 = p2sb("ym2", [128, NDI * TT], BF16)
    sev = p2sb("sev", [128, ND * TT], BF16)
    wst2 = p2sb("wst2", [128, 4096])
    if "gt" in dbg:
        gsc = p2sb("gsc", [128, 2 * TT])
        usc = p2sb("usc", [128, 2 * TT])

    em.step("D", lambda e: [
        [em.dma(e, seg(wqgT, k, I0), wqgT_d[k * CH:(k + 1) * CH, :]),
         em.dma(e, seg(wquT, k, I0), wquT_d[k * CH:(k + 1) * CH, :])]
        for k in range(ND)] and None)
    for k in range(NDI):
        em.step("D", lambda e, k=k: em.dma(e, wst2[0:CH, 0:D],
                                           w0_outT[k * CH:(k + 1) * CH, :]))
        em.step("S", lambda e, k=k: e.copy(seg(bw_out2, k, D), wst2[0:CH, 0:D]))

    def d_sw(e):
        for c in range(NIC):
            em.dma(e, col(swgc, c), swg_d[c * CH:(c + 1) * CH, :])
            em.dma(e, col(swuc, c), swu_d[c * CH:(c + 1) * CH, :])
    em.step("D", d_sw)

    def v_scol(e):
        for c in range(NIC):
            e.tensor_tensor(col(scolg, c), col(swgc, c), sxcol[:], ALU.mult)
            e.tensor_tensor(col(scolu, c), col(swuc, c), sxcol[:], ALU.mult)
    em.step("V", v_scol)

    for it in range(NTILES):
        t0 = it * TT

        def d_xn2(e, t0=t0):
            for c in range(ND):
                em.dma(e, seg(xnt2, c, TT), xn_d[c * CH:(c + 1) * CH, t0:t0 + TT])
            for c in range(NDI):
                em.dma(e, seg(ym2, c, TT), ym_d[c * CH:(c + 1) * CH, t0:t0 + TT])
        em.step("D", d_xn2)
        if "xn2" in dbg:
            def d_xn2t(e, t0=t0):
                e.wait_ge(em.progD.sem, em.progD.n)
                for c in range(ND):
                    em.dma(e, dbg["xn2"][c * CH:(c + 1) * CH, t0:t0 + TT],
                           seg(xnt2, c, TT))
            em.step("D", d_xn2t)

        def v_q2(e):
            W = 4 * TT
            qv = wst2[:, 0:W]
            qi = wst2.bitcast(I32)[:, W:2 * W]
            qf = wst2[:, 2 * W:3 * W]
            qn = wst2[:, 3 * W:4 * W]
            for g in range(2):
                xs = xnt2[:, g * W:(g + 1) * W]
                e.tensor_scalar(qv, xs, invq1[:], 0.5, ALU.mult, ALU.add)
                e.tensor_scalar(qi, qv, 1.0, 0.0, ALU.mult, ALU.add)
                e.tensor_scalar(qf, qi, 1.0, 0.0, ALU.mult, ALU.add)
                e.tensor_tensor(qn, qf, qv, ALU.is_gt)
                e.tensor_tensor(qf, qf, qn, ALU.subtract)
                e.tensor_scalar(kxb[:, g * W:(g + 1) * W], qf, 127.0, -128.0,
                                ALU.min, ALU.max)
        em.step("V", v_q2)
        if "kx" in dbg:
            def d_kx(e, t0=t0):
                for c in range(ND):
                    em.dma(e, dbg["kx"][c * CH:(c + 1) * CH, t0:t0 + TT],
                           seg(kxb, c, TT))
            em.step("D", d_kx)

        for m in range(NIC):
            def t_gu(e, m=m):
                for k in range(ND):
                    e.matmul(pb[4][:, 0:TT], seg(wqgT, k, I0)[:, m * CH:(m + 1) * CH],
                             seg(kxb, k, TT), start=(k == 0), stop=(k == ND - 1))
                for k in range(ND):
                    e.matmul(pb[5][:, 0:TT], seg(wquT, k, I0)[:, m * CH:(m + 1) * CH],
                             seg(kxb, k, TT), start=(k == 0), stop=(k == ND - 1))
            em.step("T", t_gu)
            em.step("S", lambda e, m=m: e.activation(
                hsil[:], pb[4][:, 0:TT], AF.Silu, bias=0.0, scale=col(scolg, m)))
            if "gt" in dbg:
                def v_gt(e, m=m):
                    e.tensor_scalar(seg(gsc, m % 2, TT), pb[4][:, 0:TT],
                                    col(scolg, m), 0.0, ALU.mult, ALU.add)
                    e.tensor_scalar(seg(usc, m % 2, TT), pb[5][:, 0:TT],
                                    col(scolu, m), 0.0, ALU.mult, ALU.add)
                em.step("V", v_gt)

                def d_gt(e, m=m, t0=t0):
                    em.dma(e, dbg["gt"][m * CH:(m + 1) * CH, t0:t0 + TT],
                           seg(gsc, m % 2, TT))
                    em.dma(e, dbg["up"][m * CH:(m + 1) * CH, t0:t0 + TT],
                           seg(usc, m % 2, TT))
                em.step("D", d_gt)

            def v_hm(e, m=m, it=it):
                e.scalar_tensor_tensor(seg(hm2, m % 2, TT), pb[5][:, 0:TT],
                                       col(scolu, m), hsil[:], ALU.mult, ALU.mult)
                e.tensor_reduce(col(mx2buf, it * NIC + m), seg(hm2, m % 2, TT),
                                AX.X, ALU.max, apply_absolute_value=True)
            em.step("V", v_hm)

            def d_hm(e, m=m, t0=t0):
                em.dma(e, hmid_d[m * CH:(m + 1) * CH, t0:t0 + TT],
                       seg(hm2, m % 2, TT))
                if "hmid" in dbg:
                    em.dma(e, dbg["hmid"][m * CH:(m + 1) * CH, t0:t0 + TT],
                           seg(hm2, m % 2, TT))
            em.step("D", d_hm)

        for g in range(2):
            def t_wo(e, g=g):
                for mm in range(4):
                    m = g * 4 + mm
                    for k in range(NDI):
                        e.matmul(pb[4 + mm][:, 0:TT],
                                 seg(bw_out2, k, D)[:, m * CH:(m + 1) * CH],
                                 seg(ym2, k, TT), start=(k == 0), stop=(k == NDI - 1))
            em.step("T", t_wo)

            def s_wo(e, g=g):
                for mm in range(4):
                    m = g * 4 + mm
                    e.copy(seg(sev, m, TT), pb[4 + mm][:, 0:TT])
            em.step("S", s_wo)

        def d_s0(e, t0=t0):
            for c in range(ND):
                em.dma(e, s0_d[c * CH:(c + 1) * CH, t0:t0 + TT], seg(sev, c, TT))
                if "s0" in dbg:
                    em.dma(e, dbg["s0"][c * CH:(c + 1) * CH, t0:t0 + TT],
                           seg(sev, c, TT))
        em.step("D", d_s0)

    em.new_block()
    def v_gm2(e):
        sep(e)
        e.tensor_reduce(wsc_mx[:], mx2buf[:], AX.X, ALU.max)
    em.step("V", v_gm2)
    em.step("T", lambda e: e.matmul(pb[0][0:1, 0:128], wsc_mx[:], sident[:],
                                    start=True, stop=True))
    em.step("V", lambda e: e.tensor_reduce(gmax2[:], pb[0][0:1, 0:128],
                                           AX.X, ALU.max))
    em.step("D", lambda e: em.dma(e, cc_in2, gmax2[:]))

    def g_cc2(e):
        em.progC.tick(e.collective_compute(
            "AllReduce", ALU.max, replica_groups=[list(range(NCORES))],
            ins=[cc_in2.opt()], outs=[cc_out2.opt()]))
    em.new_block()
    em.step("G", g_cc2)
    em.new_block()
    em.step("D", lambda e: em.dma(e, gmax2[:], cc_out2), deps=em.marks())
    em.step("S", lambda e: e.activation(sq2[:, 0:1], sclipd[:], AF.Sigmoid))

    def v_sq2(e):
        e.tensor_tensor(sq2[:, 0:1], sq2[:, 0:1], gmax2[:], ALU.mult)
        sep(e)
        e.reciprocal(sq2[:, 1:2], sq2[:, 0:1])
        sep(e)
        e.tensor_scalar_mul(sq2[:, 1:2], sq2[:, 1:2], 127.0)
        e.tensor_scalar_mul(sq2[:, 0:1], sq2[:, 0:1], 1.0 / 127.0)
        sep(e)
    em.step("V", v_sq2)
    em.step("T", lambda e: e.matmul(pb[0][:, 0:2], s1f[:], sq2[:, 0:2],
                                    start=True, stop=True))

    def v_iq2(e):
        e.tensor_copy(shcol[:], pb[0][:, 0:1])
        e.tensor_copy(invq2[:], pb[0][:, 1:2])
        sep(e)
    em.step("V", v_iq2)
    if "gmax" in dbg:
        em.step("D", lambda e: em.dma(e, dbg["gmax"][:, 1:2], gmax2[:]))

    p2.close()

    em.new_block()
    p3 = contextlib.ExitStack()

    def p3sb(name, shape, dt=F32):
        return p3.enter_context(nc.sbuf_tensor(name, shape, dt))

    wqdT = p3sb("wqdT", [128, NIC * D], BF16)
    swdc = p3sb("swdc", [128, ND])
    scold = p3sb("scold", [128, ND])
    hmt3 = p3sb("hmt3", [128, NIC * TT])
    khb = p3sb("khb", [128, NIC * TT], BF16)
    q3i = p3sb("q3i", [128, 4 * TT], I32)
    q3t = p3sb("q3t", [128, 4 * TT])
    q3v = p3sb("q3v", [128, 4 * TT])
    q3n = p3sb("q3n", [128, 4 * TT])
    xt3 = p3sb("xt3", [128, ND * TT])
    st3 = p3sb("st3", [128, ND * TT], BF16)
    gev = p3sb("gev", [128, ND * TT], BF16)
    m3 = p3sb("m3", [128, TT])
    x1t = p3sb("x1t", [128, ND * TT])
    xn1t = p3sb("xn1t", [128, ND * TT])
    sq3 = p3sb("sq3", [128, ND * TT])

    def d_p3w(e):
        for k in range(NIC):
            em.dma(e, seg(wqdT, k, D), wqdT_d[k * CH:(k + 1) * CH, :])
        for c in range(ND):
            em.dma(e, col(swdc, c), swd_d[c * CH:(c + 1) * CH, :])
    em.step("D", d_p3w)

    def v_scold(e):
        for c in range(ND):
            e.tensor_tensor(col(scold, c), col(swdc, c), shcol[:], ALU.mult)
    em.step("V", v_scold)

    for it in range(NTILES):
        t0 = it * TT

        def d_p3l(e, t0=t0):
            for c in range(NIC):
                em.dma(e, seg(hmt3, c, TT), hmid_d[c * CH:(c + 1) * CH, t0:t0 + TT])
            for c in range(ND):
                em.dma(e, seg(xt3, c, TT), x_fm[c * CH:(c + 1) * CH, t0:t0 + TT])
                em.dma(e, seg(st3, c, TT), s0_d[c * CH:(c + 1) * CH, t0:t0 + TT])
        em.step("D", d_p3l)

        def v_q3(e):
            for g in range(6):
                c0, c1 = 4 * g, min(4 * g + 4, NIC)
                W = (c1 - c0) * TT
                qv = q3v[:, 0:W]
                qi = q3i[:, 0:W]
                qf = q3t[:, 0:W]
                qn = q3n[:, 0:W]
                hs = hmt3[:, c0 * TT:c1 * TT]
                e.tensor_scalar(qv, hs, invq2[:], 0.5, ALU.mult, ALU.add)
                e.tensor_scalar(qi, qv, 1.0, 0.0, ALU.mult, ALU.add)
                e.tensor_scalar(qf, qi, 1.0, 0.0, ALU.mult, ALU.add)
                e.tensor_tensor(qn, qf, qv, ALU.is_gt)
                e.tensor_tensor(qf, qf, qn, ALU.subtract)
                e.tensor_scalar(khb[:, c0 * TT:c1 * TT], qf, 127.0, -128.0,
                                ALU.min, ALU.max)
        em.step("V", v_q3)

        for g in range(2):
            def t_dn(e, g=g):
                for mm in range(4):
                    m = g * 4 + mm
                    for k in range(NIC):
                        e.matmul(pb[4 + mm][:, 0:TT],
                                 seg(wqdT, k, D)[:, m * CH:(m + 1) * CH],
                                 seg(khb, k, TT), start=(k == 0), stop=(k == NIC - 1))
            em.step("T", t_dn)

            def s_dn(e, g=g):
                for mm in range(4):
                    m = g * 4 + mm
                    e.activation(seg(gev, m, TT), pb[4 + mm][:, 0:TT], AF.Silu,
                                 bias=0.0, scale=col(scold, m))
            em.step("S", s_dn)

        def v_x1(e):
            for c in range(ND):
                e.tensor_tensor(m3[:], seg(st3, c, TT), seg(gev, c, TT), ALU.mult)
                e.tensor_tensor(seg(x1t, c, TT), seg(xt3, c, TT), m3[:], ALU.add)
        em.step("V", v_x1)

        emit_ln(x1t, xn1t, sq3)

        def d_p3s(e, t0=t0):
            for c in range(ND):
                em.dma(e, x1_d[c * CH:(c + 1) * CH, t0:t0 + TT], seg(x1t, c, TT))
                em.dma(e, xn1_d[c * CH:(c + 1) * CH, t0:t0 + TT], seg(xn1t, c, TT))
                if "x1" in dbg:
                    em.dma(e, dbg["x1"][c * CH:(c + 1) * CH, t0:t0 + TT],
                           seg(x1t, c, TT))
        em.step("D", d_p3s)

    p3.close()

    em.new_block()
    NTCH = TT // CH
    p4 = contextlib.ExitStack()

    def p4sb(name, shape, dt=F32):
        return p4.enter_context(nc.sbuf_tensor(name, shape, dt))

    wq4gT = p4sb("wq4gT", [128, ND * I0], BF16)
    wq4uT = p4sb("wq4uT", [128, ND * I0], BF16)
    xn4 = p4sb("xn4", [128, ND * TT])
    xn4tok = p4sb("xn4tok", [128, NTCH * D])
    xq4fm = p4sb("xq4fm", [128, ND * TT], BF16)
    f4sc = p4sb("f4sc", [128, I0 // 16])
    f4rc = p4sb("f4rc", [128, I0 // 16])
    f4i = p4sb("f4i", [128, I0], I32)
    f4t1 = p4sb("f4t1", [128, I0])
    f4t2 = p4sb("f4t2", [128, I0])
    f4z = p4sb("f4z", [128, I0])
    h1tok = p4sb("h1tok", [128, I0])
    h1sil = p4sb("h1sil", [128, 512])
    kh1tok = p4sb("kh1tok", [128, I0], BF16)

    def d_p4w(e):
        for k in range(ND):
            em.dma(e, seg(wq4gT, k, I0), wq4gT_d[k * CH:(k + 1) * CH, :])
            em.dma(e, seg(wq4uT, k, I0), wq4uT_d[k * CH:(k + 1) * CH, :])
    em.step("D", d_p4w)

    for it in range(NTILES):
        t0 = it * TT

        def d_xn4(e, t0=t0):
            for c in range(ND):
                em.dma(e, seg(xn4, c, TT), xn1_d[c * CH:(c + 1) * CH, t0:t0 + TT])
        em.step("D", d_xn4)

        for tc in range(NTCH):
            for c0 in range(0, ND, 4):
                def t_tr(e, tc=tc, c0=c0):
                    for c in range(c0, min(c0 + 4, ND)):
                        e.transpose(pb[c % 4][0:CH, 0:CH],
                                    seg(xn4, c, TT)[:, tc * CH:(tc + 1) * CH],
                                    sident[:])
                    e.matmul(pb[c0 % 4][0:1, 400:401], sident[0:1, 0:1],
                             sident[0:1, 0:1], start=True, stop=True)
                em.step("T", t_tr)

                def s_tr(e, tc=tc, c0=c0):
                    for c in range(c0, min(c0 + 4, ND)):
                        e.copy(xn4tok[:, tc * D + c * CH: tc * D + (c + 1) * CH],
                               pb[c % 4][0:CH, 0:CH])
                em.step("S", s_tr)

        def v_q4x(e):
            fp4_core(e, xn4tok, 128, NTCH * D, f4sc, f4rc, f4i, f4t1, f4t2,
                     f4z, f4z)
        em.step("V", v_q4x)

        for tc in range(NTCH):
            for c0 in range(0, ND, 4):
                def t_tb(e, tc=tc, c0=c0):
                    for c in range(c0, min(c0 + 4, ND)):
                        e.transpose(pb[c % 4][0:CH, 0:CH],
                                    f4z[:, tc * D + c * CH: tc * D + (c + 1) * CH],
                                    sident[:])
                    e.matmul(pb[c0 % 4][0:1, 400:401], sident[0:1, 0:1],
                             sident[0:1, 0:1], start=True, stop=True)
                em.step("T", t_tb)

                def s_tb(e, tc=tc, c0=c0):
                    for c in range(c0, min(c0 + 4, ND)):
                        e.copy(seg(xq4fm, c, TT)[:, tc * CH:(tc + 1) * CH],
                               pb[c % 4][0:CH, 0:CH])
                em.step("S", s_tb)

        for tc in range(NTCH):
            for i0 in range(0, I0, 512):
                iw = min(512, I0 - i0)

                def t_g4(e, tc=tc, i0=i0, iw=iw):
                    for k in range(ND):
                        e.matmul(pb[4][:, 0:iw],
                                 seg(xq4fm, k, TT)[:, tc * CH:(tc + 1) * CH],
                                 seg(wq4gT, k, I0)[:, i0:i0 + iw],
                                 start=(k == 0), stop=(k == ND - 1))
                    for k in range(ND):
                        e.matmul(pb[5][:, 0:iw],
                                 seg(xq4fm, k, TT)[:, tc * CH:(tc + 1) * CH],
                                 seg(wq4uT, k, I0)[:, i0:i0 + iw],
                                 start=(k == 0), stop=(k == ND - 1))
                em.step("T", t_g4)
                em.step("S", lambda e, i0=i0, iw=iw: e.activation(
                    h1sil[:, 0:iw], pb[4][:, 0:iw], AF.Silu))
                em.step("V", lambda e, i0=i0, iw=iw: e.tensor_tensor(
                    h1tok[:, i0:i0 + iw], h1sil[:, 0:iw], pb[5][:, 0:iw],
                    ALU.mult))

            def v_q4h(e, tc=tc):
                fp4_core(e, h1tok, 128, I0, f4sc, f4rc, f4i, f4t1, f4t2, f4z,
                         kh1tok)
            em.step("V", v_q4h)

            def d_kh(e, tc=tc, t0=t0):
                em.dma(e, kh1_d[t0 + tc * CH:t0 + (tc + 1) * CH, :], kh1tok[:])
            em.step("D", d_kh)

    p4.close()

    em.new_block()
    p42 = contextlib.ExitStack()

    def p42sb(name, shape, dt=F32):
        return p42.enter_context(nc.sbuf_tensor(name, shape, dt))

    wq4dT = p42sb("wq4dT", [128, NIC * D], BF16)
    kh1ld = p42sb("kh1ld", [128, NTCH * I0], BF16)
    kh1f32 = p42sb("kh1f32", [128, I0])
    kh1fm = p42sb("kh1fm", [128, NIC * TT], BF16)
    g1t = p42sb("g1t", [128, ND * TT], BF16)

    def d_p42w(e):
        for k in range(NIC):
            em.dma(e, seg(wq4dT, k, D), wq4dT_d[k * CH:(k + 1) * CH, :])
    em.step("D", d_p42w)

    for it in range(NTILES):
        t0 = it * TT

        def d_kh1(e, t0=t0):
            for tc in range(NTCH):
                em.dma(e, seg(kh1ld, tc, I0), kh1_d[t0 + tc * CH:t0 + (tc + 1) * CH, :])
        em.step("D", d_kh1)

        for tc in range(NTCH):
            em.step("S", lambda e, tc=tc: e.copy(kh1f32[:], seg(kh1ld, tc, I0)))
            for c0 in range(0, NIC, 4):
                def t_tk(e, tc=tc, c0=c0):
                    for c in range(c0, min(c0 + 4, NIC)):
                        e.transpose(pb[c % 4][0:CH, 0:CH],
                                    kh1f32[:, c * CH:(c + 1) * CH],
                                    sident[:])
                    e.matmul(pb[c0 % 4][0:1, 400:401], sident[0:1, 0:1],
                             sident[0:1, 0:1], start=True, stop=True)
                em.step("T", t_tk)

                def s_tk(e, tc=tc, c0=c0):
                    for c in range(c0, min(c0 + 4, NIC)):
                        e.copy(seg(kh1fm, c, TT)[:, tc * CH:(tc + 1) * CH],
                               pb[c % 4][0:CH, 0:CH])
                em.step("S", s_tk)

        for g in range(2):
            def t_dn1(e, g=g):
                for mm in range(4):
                    m = g * 4 + mm
                    for k in range(NIC):
                        e.matmul(pb[4 + mm][:, 0:TT],
                                 seg(wq4dT, k, D)[:, m * CH:(m + 1) * CH],
                                 seg(kh1fm, k, TT), start=(k == 0),
                                 stop=(k == NIC - 1))
            em.step("T", t_dn1)

            def s_dn1(e, g=g):
                for mm in range(4):
                    m = g * 4 + mm
                    e.activation(seg(g1t, m, TT), pb[4 + mm][:, 0:TT], AF.Silu)
            em.step("S", s_dn1)

        def d_g1(e, t0=t0):
            for c in range(ND):
                em.dma(e, g1_d[c * CH:(c + 1) * CH, t0:t0 + TT], seg(g1t, c, TT))
                if "g1" in dbg:
                    em.dma(e, dbg["g1"][c * CH:(c + 1) * CH, t0:t0 + TT],
                           seg(g1t, c, TT))
        em.step("D", d_g1)

    p42.close()

    em.new_block()
    p5 = contextlib.ExitStack()

    def p5sb(name, shape, dt=F32):
        return p5.enter_context(nc.sbuf_tensor(name, shape, dt))

    bw1_in = p5sb("bw1_in", [128, ND * ZC], BF16)
    bw1_out = p5sb("bw1_out", [128, ND * D], BF16)
    x1l = p5sb("x1l", [128, ND * TT])
    xn1l = p5sb("xn1l", [128, ND * TT])
    xn1b = p5sb("xn1b", [128, ND * TT], BF16)
    zrawb = p5sb("zrawb", [128, NZC * TT], BF16)
    ztail = p5sb("ztail", [128, NZC * 3], BF16)
    cz = p5sb("cz", [128, TT])
    xval = p5sb("xval", [128, ND * TT])
    xg1 = p5sb("xg1", [128, ND * TT], BF16)
    czs = p5sb("czs", [32, TT])
    pwt = p5sb("pwt", [32, TT])
    post = p5sb("post", [32, TT])
    logw = p5sb("logw", [32, TT])
    phit = p5sb("phit", [128, TT])
    aqt = p5sb("aqt", [128, TT])
    kqt = p5sb("kqt", [128, TT], I32)
    sgt = p5sb("sgt", [128, TT])
    prt = p5sb("prt", [128, TT])
    sint = p5sb("sint", [128, TT])
    cost = p5sb("cost", [128, TT])
    nret = p5sb("nret", [128, ND * TT])
    nimt = p5sb("nimt", [128, ND * TT])
    csre = p5sb("csre", [128, ND * TT])
    csim = p5sb("csim", [128, ND * TT])
    csden = p5sb("csden", [32, TT])
    cden = p5sb("cden", [32, 1])
    cre = p5sb("cre", [128, ND])
    cim = p5sb("cim", [128, ND])
    invden = p5sb("invden", [32, TT])
    rst = p5sb("rst", [32, TT])
    renb = p5sb("renb", [128, ND * TT], BF16)
    imnb = p5sb("imnb", [128, ND * TT], BF16)
    ygb = p5sb("ygb", [128, ND * TT], BF16)
    g1l = p5sb("g1l", [128, ND * TT], BF16)
    outt = p5sb("outt", [128, ND * TT])
    onesf = p5sb("onesf", [128, TT])
    wst5 = p5sb("wst5", [128, ZC])

    for k in range(ND):
        em.step("D", lambda e, k=k: em.dma(e, wst5[0:CH, 0:ZC],
                                           w1_in[k * CH:(k + 1) * CH, :]))
        em.step("S", lambda e, k=k: e.copy(seg(bw1_in, k, ZC), wst5[0:CH, 0:ZC]))
    for k in range(ND):
        em.step("D", lambda e, k=k: em.dma(e, wst5[0:CH, 0:D],
                                           w1_out[k * CH:(k + 1) * CH, :]))
        em.step("S", lambda e, k=k: e.copy(seg(bw1_out, k, D), wst5[0:CH, 0:D]))
    def v_p5init(e):
        e.memset(onesf[:], 1.0)
        e.memset(cden[:], 0.0)
        e.memset(cre[:], 0.0)
        e.memset(cim[:], 0.0)
    em.step("V", v_p5init)

    for it in range(NTILES):
        t0 = it * TT
        seq_start = (it % TPS == 0)
        tloc = (it % TPS) * TT

        def d_p5l(e, t0=t0):
            for c in range(ND):
                em.dma(e, seg(x1l, c, TT), x1_d[c * CH:(c + 1) * CH, t0:t0 + TT])
                em.dma(e, seg(xn1l, c, TT), xn1_d[c * CH:(c + 1) * CH, t0:t0 + TT])
                em.dma(e, seg(g1l, c, TT), g1_d[c * CH:(c + 1) * CH, t0:t0 + TT])
        em.step("D", d_p5l)

        def s_cast5(e):
            for c in range(ND):
                e.copy(seg(xn1b, c, TT), seg(xn1l, c, TT))
        em.step("S", s_cast5)

        for g in range(5):
            def t_z(e, g=g):
                for mm in range(4):
                    m = g * 4 + mm
                    if m >= NZC:
                        continue
                    mw = 32 if m == NZC - 1 else CH
                    for k in range(ND):
                        e.matmul(pb[4 + mm][0:mw, 0:TT],
                                 seg(bw1_in, k, ZC)[:, m * CH:m * CH + mw],
                                 seg(xn1b, k, TT), start=(k == 0), stop=(k == ND - 1))
            em.step("T", t_z)

            def s_z(e, g=g):
                for mm in range(4):
                    m = g * 4 + mm
                    if m >= NZC:
                        continue
                    mw = 32 if m == NZC - 1 else CH
                    e.copy(seg(zrawb, m, TT)[0:mw, :], pb[4 + mm][0:mw, 0:TT])
            em.step("S", s_z)

        def v_conv1(e, seq_start=seq_start):
            for c in range(NZC):
                mw = 32 if c == NZC - 1 else CH
                tl = seg(ztail, c, 3)[0:mw, :]
                xb_ = seg(zrawb, c, TT)[0:mw, :]
                acc = cz[0:mw, :]
                if seq_start:
                    e.tensor_scalar(acc[:, 0:3], xb_[:, 0:3], 0.0, 0.0,
                                    ALU.mult, ALU.add)
                else:
                    e.tensor_scalar(acc[:, 0:3], tl,
                                    col(convw1, c * 4 + 0)[0:mw, :],
                                    0.0, ALU.mult, ALU.add)
                e.tensor_scalar(acc[:, 3:TT], xb_[:, 0:TT - 3],
                                col(convw1, c * 4 + 0)[0:mw, :], 0.0,
                                ALU.mult, ALU.add)
                for j in (1, 2, 3):
                    sh = 3 - j
                    if sh > 0 and not seq_start:
                        e.scalar_tensor_tensor(acc[:, 0:sh], tl[:, j:3],
                                               col(convw1, c * 4 + j)[0:mw, :],
                                               acc[:, 0:sh], ALU.mult, ALU.add)
                    e.scalar_tensor_tensor(acc[:, sh:TT], xb_[:, 0:TT - sh],
                                           col(convw1, c * 4 + j)[0:mw, :],
                                           acc[:, sh:TT], ALU.mult, ALU.add)
                e.tensor_copy(seg(ztail, c, 3)[0:mw, :], xb_[:, TT - 3:TT])
                if c < ND:
                    e.tensor_copy(seg(xval, c, TT), acc)
                elif c < 2 * ND:
                    e.tensor_copy(seg(csre, c - ND, TT), acc)
                else:
                    e.tensor_copy(czs[0:32, :], acc[0:32, :])
        em.step("V", v_conv1)

        def s_xg(e):
            for c in range(ND):
                e.activation(seg(xg1, c, TT), seg(csre, c, TT), AF.Silu)
        em.step("S", s_xg)

        em.step("G", lambda e, tloc=tloc: e.iota(
            post[:], [[1, TT]], base=tloc, channel_multiplier=0,
            allow_small_or_imprecise_dtypes=True))

        def v_pw(e):
            e.tensor_scalar(logw[:], post[:], walpha[:], wbeta[:],
                            ALU.mult, ALU.add)
            e.scalar_tensor_tensor(logw[:], czs[0:32, :], sco1[:], logw[:],
                                   ALU.mult, ALU.add)
        em.step("V", v_pw)
        em.step("S", lambda e: e.activation(pwt[:], logw[:], AF.Exp))
        if "pw" in dbg:
            em.step("D", lambda e, t0=t0: em.dma(
                e, dbg["pw"][:, t0:t0 + TT], pwt[:]))
        if "xv" in dbg:
            def d_xv(e, t0=t0):
                for c in range(ND):
                    em.dma(e, dbg["xv"][c * CH:(c + 1) * CH, t0:t0 + TT],
                           seg(xval, c, TT))
            em.step("D", d_xv)

        for c in range(ND):
            def s_tg1(e, c=c):
                e.activation(aqt[:], seg(xval, c, TT), AF.Abs, bias=0.0,
                             scale=col(th2pi, c))
                e.activation(sgt[:], seg(xval, c, TT), AF.Sign,
                             bias=0.0, scale=col(thcol, c))
                e.activation(phit[:], seg(xval, c, TT), AF.Identity, bias=0.0,
                             scale=col(thcol, c))
            em.step("S", s_tg1)

            def v_tg(e, c=c):
                e.tensor_scalar(aqt[:], aqt[:], 1.0, 0.5, ALU.mult, ALU.add)
                e.tensor_scalar(kqt[:], aqt[:], 1.0, 0.0, ALU.mult, ALU.add)
                e.tensor_scalar(sint[:], kqt[:], 1.0, 0.0, ALU.mult, ALU.add)
                e.tensor_tensor(cost[:], sint[:], aqt[:], ALU.is_gt)
                e.tensor_tensor(sint[:], sint[:], cost[:], ALU.subtract)
                e.tensor_tensor(prt[:], sint[:], sgt[:], ALU.mult)
                e.scalar_tensor_tensor(prt[:], prt[:], -2.0 * PI, phit[:],
                                       ALU.mult, ALU.add)
                e.tensor_scalar(prt[:], prt[:], PI, -PI, ALU.min, ALU.max)
            em.step("V", v_tg)

            def s_tg2(e, c=c):
                e.activation(sint[:], prt[:], AF.Sin)
                e.activation(aqt[:], prt[:], AF.Abs)
            em.step("S", s_tg2)
            em.step("S", lambda e, c=c: e.activation(
                cost[:], aqt[:], AF.Sin, bias=PI / 2, scale=-1.0))

            em.step("T", lambda e, c=c: e.matmul(
                pb[0][:, 0:TT], selinv[:, c * CH:(c + 1) * CH], pwt[:],
                start=True, stop=True))

            def v_num(e, c=c):
                e.tensor_tensor(seg(nret, c, TT), cost[:], pb[0][:, 0:TT], ALU.mult)
                e.tensor_tensor(seg(nimt, c, TT), sint[:], pb[0][:, 0:TT], ALU.mult)
            em.step("V", v_num)

        def v_cs(e, it=it):
            e.tensor_tensor_scan(csden[:], onesf[0:32, :], pwt[:], cden[:],
                                 ALU.mult, ALU.add)
            for c in range(ND):
                e.tensor_tensor_scan(seg(csre, c, TT), onesf[:], seg(nret, c, TT),
                                     col(cre, c), ALU.mult, ALU.add)
                e.tensor_tensor_scan(seg(csim, c, TT), onesf[:], seg(nimt, c, TT),
                                     col(cim, c), ALU.mult, ALU.add)
                if c == 0:
                    e.tensor_copy(cden[:], csden[:, TT - 1:TT])
                else:
                    e.tensor_copy(col(cre, c - 1),
                                  seg(csre, c - 1, TT)[:, TT - 1:TT])
                    e.tensor_copy(col(cim, c - 1),
                                  seg(csim, c - 1, TT)[:, TT - 1:TT])
            e.tensor_scalar_max(invden[:], csden[:], 1e-4)
            e.tensor_copy(col(cre, ND - 1), seg(csre, ND - 1, TT)[:, TT - 1:TT])
            e.tensor_copy(col(cim, ND - 1), seg(csim, ND - 1, TT)[:, TT - 1:TT])
            e.reciprocal(invden[:], invden[:])
            if (it + 1) % TPS == 0 and it + 1 < NTILES:
                e.memset(cden[:], 0.0)
                e.memset(cre[:], 0.0)
                e.memset(cim[:], 0.0)
        em.step("V", v_cs)
        if "idn" in dbg:
            em.step("D", lambda e, t0=t0: em.dma(
                e, dbg["idn"][:, t0:t0 + TT], invden[:]))

        def t_idb(e):
            for c in range(ND):
                bank = pb[4 + c // 2]
                off = (c % 2) * TT
                e.matmul(bank[:, off:off + TT], selinv[:, c * CH:(c + 1) * CH],
                         invden[:], start=True, stop=True)
        em.step("T", t_idb)

        def v_rei(e):
            for c in range(ND):
                bank = pb[4 + c // 2]
                off = (c % 2) * TT
                e.tensor_tensor(seg(nret, c, TT), seg(csre, c, TT),
                                bank[:, off:off + TT], ALU.mult)
                e.tensor_tensor(seg(nimt, c, TT), seg(csim, c, TT),
                                bank[:, off:off + TT], ALU.mult)
        em.step("V", v_rei)

        def s_sq5(e):
            for c in range(ND):
                e.activation(seg(csre, c, TT), seg(nret, c, TT), AF.Square)
                e.activation(seg(csim, c, TT), seg(nimt, c, TT), AF.Square)
        em.step("S", s_sq5)

        def t_ms(e):
            for c in range(ND):
                e.matmul(pb[0][0:32, 0:TT], seg(selsq, c, 32), seg(csre, c, TT),
                         start=(c == 0), stop=False)
            for c in range(ND):
                e.matmul(pb[0][0:32, 0:TT], seg(selsq, c, 32), seg(csim, c, TT),
                         start=False, stop=(c == ND - 1))
        em.step("T", t_ms)
        em.step("S", lambda e: e.activation(rst[:], pb[0][0:32, 0:TT], AF.Sqrt,
                                            bias=1e-5, scale=1.0 / 64))
        em.step("V", lambda e: e.reciprocal(csden[0:32, :], rst[:]))

        def t_rsb(e):
            for c in range(ND):
                bank = pb[4 + c // 2]
                off = (c % 2) * TT
                e.matmul(bank[:, off:off + TT], selinv[:, c * CH:(c + 1) * CH],
                         csden[0:32, :], start=True, stop=True)
        em.step("T", t_rsb)

        def v_ren(e):
            for c in range(ND):
                bank = pb[4 + c // 2]
                off = (c % 2) * TT
                e.scalar_tensor_tensor(seg(renb, c, TT), seg(nret, c, TT),
                                       col(nsre_c, c), bank[:, off:off + TT],
                                       ALU.mult, ALU.mult)
                e.scalar_tensor_tensor(seg(imnb, c, TT), seg(nimt, c, TT),
                                       col(nsim_c, c), bank[:, off:off + TT],
                                       ALU.mult, ALU.mult)
        em.step("V", v_ren)

        for g in range(2):
            def t_y(e, g=g):
                for mm in range(4):
                    c = g * 4 + mm
                    e.matmul(pb[4 + mm][:, 0:TT], bdiag_re[:], seg(renb, c, TT),
                             start=True, stop=False)
                    e.matmul(pb[4 + mm][:, 0:TT], bdiag_im[:], seg(imnb, c, TT),
                             start=False, stop=True)
            em.step("T", t_y)

            def v_yg(e, g=g):
                for mm in range(4):
                    c = g * 4 + mm
                    e.tensor_tensor(seg(ygb, c, TT), pb[4 + mm][:, 0:TT],
                                    seg(xg1, c, TT), ALU.mult)
            em.step("V", v_yg)
            if "yg" in dbg:
                def d_ygt(e, g=g, t0=t0):
                    for mm in range(4):
                        c = g * 4 + mm
                        em.dma(e, dbg["yg"][c * CH:(c + 1) * CH, t0:t0 + TT],
                               seg(ygb, c, TT))
                em.step("D", d_ygt)

        for g in range(2):
            def t_o(e, g=g):
                for mm in range(4):
                    m = g * 4 + mm
                    for k in range(ND):
                        e.matmul(pb[4 + mm][:, 0:TT],
                                 seg(bw1_out, k, D)[:, m * CH:(m + 1) * CH],
                                 seg(ygb, k, TT), start=(k == 0), stop=(k == ND - 1))
            em.step("T", t_o)

            def v_o(e, g=g):
                for mm in range(4):
                    m = g * 4 + mm
                    e.tensor_tensor(cz[:], pb[4 + mm][:, 0:TT], seg(g1l, m, TT),
                                    ALU.mult)
                    e.tensor_tensor(seg(outt, m, TT), seg(x1l, m, TT), cz[:],
                                    ALU.add)
            em.step("V", v_o)

        def d_out(e, t0=t0):
            for c in range(ND):
                em.dma(e, out_fm[c * CH:(c + 1) * CH, t0:t0 + TT], seg(outt, c, TT))
        em.step("D", d_out)

    p5.close()

    em.new_block()
    em.step("D", lambda e: e.wait_ge(em.progD.sem, em.progD.n))
    em.close()
    gs.close()
    return nc


def _selrow_host():
    a = np.zeros((32, 32 * 128), dtype=np.float32)
    for j in range(32):
        a[j, j * 128:(j + 1) * 128] = 1.0
    return a


def _selinv_host():
    a = np.zeros((32, 8 * 128), dtype=np.float32)
    for ci in range(8):
        for p in range(128):
            a[4 * ci + (p >> 5), ci * 128 + p] = 1.0
    return a


def _selsq_host():
    a = np.zeros((128, 8 * 32), dtype=np.float32)
    for ci in range(8):
        for c in range(128):
            a[c, ci * 32 + 4 * ci + (c >> 5)] = 1.0
    return a


def _chunk_cols(v, nch):
    v = np.asarray(v)
    if v.ndim == 1:
        v = v[:, None]
    k = v.shape[1]
    return np.ascontiguousarray(
        v.reshape(nch, 128, k).transpose(1, 0, 2).reshape(128, nch * k))


def prep_inputs(core, A):
    import ml_dtypes
    x = np.asarray(A["L0_attn_x"])
    xc = x[2 * core:2 * core + 2]
    d = {
        "x_fm": np.ascontiguousarray(xc.transpose(2, 0, 1).reshape(D, SEQ * T)),
        "w0_inT": np.ascontiguousarray(np.asarray(A["L0_attn_W_in"]).T),
        "conv_w0": _chunk_cols(A["L0_attn_conv_w"], NDI),
        "conv_b0": _chunk_cols(A["L0_attn_conv_b"], NDI),
        "w0_xprojT": np.ascontiguousarray(np.asarray(A["L0_attn_W_xproj"]).T),
        "w0_dtT": np.ascontiguousarray(np.asarray(A["L0_attn_W_dt"]).T),
        "b_dt0": _chunk_cols(A["L0_attn_b_dt"], NDI),
        "A_log0": _chunk_cols(A["L0_attn_A_log"], NDI),
        "D_skip0": _chunk_cols(A["L0_attn_D_skip"], NDI),
        "w0_outT": np.ascontiguousarray(np.asarray(A["L0_attn_W_out"]).T),
        "clip_x": np.asarray(A["L0_ffn_clip_x"]).reshape(1, 1),
        "clip_down": np.asarray(A["L0_ffn_clip_down"]).reshape(1, 1),
        "wg0": np.ascontiguousarray(A["L0_ffn_w_gate"]),
        "wu0": np.ascontiguousarray(A["L0_ffn_w_up"]),
        "wd0": np.ascontiguousarray(A["L0_ffn_w_down"]),
        "w1_in": np.ascontiguousarray(A["L1_attn_W_in"]),
        "conv_w1": _chunk_cols(
            np.concatenate([np.asarray(A["L1_attn_conv_kernel"]).reshape(4, ZC).T,
                            np.zeros((NZC * 128 - ZC, 4), np.float32)], axis=0), NZC),
        "theta1": _chunk_cols(np.asarray(A["L1_attn_theta"]).reshape(D), ND),
        "slopes1": np.concatenate([np.asarray(A["L1_attn_decay_slopes"]),
                                   np.asarray(A["L1_attn_anchor_slopes"])]
                                  ).reshape(32, 1).astype(np.float32),
        "score1": np.asarray(A["L1_attn_score_scale"]).reshape(32, 1),
        "amask": np.array([1.0] * 28 + [-1.0] * 4, np.float32).reshape(32, 1),
        "dmask": np.array([-float(T - 1)] * 28 + [0.0] * 4,
                          np.float32).reshape(32, 1),
        "w_re": np.ascontiguousarray(A["L1_attn_W_re"]),
        "w_im": np.ascontiguousarray(A["L1_attn_W_im"]),
        "ns_re": _chunk_cols(np.tile(np.asarray(A["L1_attn_norm_scale"])[:32], 32), ND),
        "ns_im": _chunk_cols(np.tile(np.asarray(A["L1_attn_norm_scale"])[32:], 32), ND),
        "w1_out": np.ascontiguousarray(A["L1_attn_W_out"]),
        "wg1": np.ascontiguousarray(A["L1_ffn_w_gate"]),
        "wu1": np.ascontiguousarray(A["L1_ffn_w_up"]),
        "wd1": np.ascontiguousarray(A["L1_ffn_w_down"]),
        "ident": np.eye(128, dtype=np.float32),
        "selrow_b": _selrow_host().astype(ml_dtypes.bfloat16),
        "selinv": _selinv_host(),
        "selsq": _selsq_host(),
        "ones1f": np.ones((1, 128), dtype=np.float32),
        "consts": np.tile(np.array(CONST_VALS + [0.0] * (16 - len(CONST_VALS)),
                                   np.float32), (128, 1)),
    }
    for k in d:
        if d[k].dtype == np.float64:
            d[k] = d[k].astype(np.float32)
    return d


def kernel(**inputs):
    nc = build_program()
    in_maps = [prep_inputs(c, inputs) for c in range(NCORES)]
    res = run_bass_kernel_spmd(nc, in_maps, core_ids=list(range(NCORES)))
    out = np.empty((16, T, D), dtype=np.float32)
    for c in range(NCORES):
        o = np.asarray(res.results[c]["out_fm"])
        out[2 * c:2 * c + 2] = o.reshape(D, SEQ, T).transpose(1, 2, 0)
    return out


if __name__ == "__main__":
    nc = build_program()
    print("built; instructions:", sum(1 for _ in nc.all_instructions()))
